# version 16
# baseline (speedup 1.0000x reference)
import numpy as np
import ml_dtypes
from contextlib import ExitStack

import concourse.bass as bass
import concourse.mybir as mybir
from concourse.bass_utils import run_bass_kernel_spmd
from concourse.bass_core_rust import IndirectOffsetOnAxis

F32 = mybir.dt.float32
BF16 = mybir.dt.bfloat16
U32 = mybir.dt.uint32
AF = mybir.ActivationFunctionType
ALU = mybir.AluOpType
AXX = mybir.AxisListType.X

D = 1024
DC = 8
S = 4096
NBLK = 32
NCT = 8
CAP = 512
DFF = 2730
NH = 16
HD = 64
SCALE = 1024 ** -0.5
EPS_LN = 1e-5
EPS_RMS = 1e-6
NEG = -1.0e30
CANDK = 16
NCAND = 128 * CANDK
NCH = NCAND // 128
MYCH = NCH // 2
NPB = CAP // 128
MYPB = NPB // 2
FT_SIZES = [512, 512, 512, 512, 512, 170]
FT_OFF = [0, 512, 1024, 1536, 2048, 2560]
NFT = len(FT_SIZES)
INV_D = 1.0 / D
W16 = 16
DBG = False
NO_AR1 = False


def build_nc(stage=99):
    nc = bass.Bass("TRN2", target_bir_lowering=False, debug=False)
    ST1 = stage >= 1
    ST2 = stage >= 2
    ST3 = stage >= 3
    ST4 = stage >= 4
    ST5 = stage >= 5
    ST6 = stage >= 6

    x_tm = nc.dram_tensor("x_tm", [S, D], F32, kind="ExternalInput").ap()
    x_fm = nc.dram_tensor("x_fm", [D, S], F32, kind="ExternalInput").ap()
    Wq_d = nc.dram_tensor("Wq", [D, D], F32, kind="ExternalInput").ap()
    Wk_d = nc.dram_tensor("Wk", [D, D], F32, kind="ExternalInput").ap()
    WvT_d = nc.dram_tensor("WvT", [D, D], F32, kind="ExternalInput").ap()
    Wr_d = nc.dram_tensor("Wr", [D, D], F32, kind="ExternalInput").ap()
    WqkvT_d = nc.dram_tensor("WqkvT", [D, 3 * D], F32, kind="ExternalInput").ap()
    WoutT_d = nc.dram_tensor("WoutT", [D, D], F32, kind="ExternalInput").ap()
    W1T_d = nc.dram_tensor("W1T", [D, DFF], F32, kind="ExternalInput").ap()
    W2T_d = nc.dram_tensor("W2T", [D, DFF], F32, kind="ExternalInput").ap()
    W3T_d = nc.dram_tensor("W3T", [DFF, D], F32, kind="ExternalInput").ap()
    wrout_d = nc.dram_tensor("w_router", [1, D], F32, kind="ExternalInput").ap()
    g1_d = nc.dram_tensor("g1", [1, D], F32, kind="ExternalInput").ap()
    g2_d = nc.dram_tensor("g2", [1, D], F32, kind="ExternalInput").ap()
    ident_d = nc.dram_tensor("ident_bf", [128, 128], BF16, kind="ExternalInput").ap()
    idf32_d = nc.dram_tensor("ident_f32", [128, 128], F32, kind="ExternalInput").ap()
    ones_d = nc.dram_tensor("ones_bf", [128, 1], BF16, kind="ExternalInput").ap()
    iota64_d = nc.dram_tensor("iota64", [128, 1], F32, kind="ExternalInput").ap()
    maskf_d = nc.dram_tensor("maskffn", [128, MYPB * NPB * 128], BF16, kind="ExternalInput").ap()
    cinfu_d = nc.dram_tensor("cinfo_u", [1, 8], U32, kind="ExternalInput").ap()
    cinff_d = nc.dram_tensor("cinfo_f", [1, 8], F32, kind="ExternalInput").ap()

    out = nc.dram_tensor("out", [S, D], F32, kind="ExternalOutput").ap()
    if DBG:
        d_gq = nc.dram_tensor("d_gq", [1, D], F32, kind="ExternalOutput").ap()
        d_gk = nc.dram_tensor("d_gk", [1, D], F32, kind="ExternalOutput").ap()
        d_sc = nc.dram_tensor("d_sc", [1, 2 * S], F32, kind="ExternalOutput").ap()
        d_oi = nc.dram_tensor("d_oi", [CAP, 1], U32, kind="ExternalOutput").ap()
        d_q = nc.dram_tensor("d_q", [S, D], BF16, kind="ExternalOutput").ap()

    q_dram = nc.dram_tensor("q_dram", [S, D], BF16, kind="Internal").ap()
    sx_dram = nc.dram_tensor("sx_dram", [NBLK, 128], F32, kind="Internal").ap()
    sq_dram = nc.dram_tensor("sq_dram", [NBLK, 128], F32, kind="Internal").ap()
    csq_dram = nc.dram_tensor("csq_dram", [1, D], F32, kind="Internal").ap()
    csk_dram = nc.dram_tensor("csk_dram", [1, D], F32, kind="Internal").ap()
    csv_dram = nc.dram_tensor("csv_dram", [1, D], F32, kind="Internal").ap()
    sumw_dram = nc.dram_tensor("sumw_dram", [1, 1], F32, kind="Internal").ap()
    candv_dram = nc.dram_tensor("candv_dram", [1, NCAND], F32, kind="Internal").ap()
    candt_dram = nc.dram_tensor("candt_dram", [1, NCAND], U32, kind="Internal").ap()
    sums_dram = nc.dram_tensor("sums_dram", [2, NH * MYPB * 128], F32, kind="Internal").ap()
    ln1_loc = nc.dram_tensor("ln1_loc", [S, D], BF16, kind="Internal").ap()
    wqkvT_bf = nc.dram_tensor("wqkvT_bf", [D, 3 * D], BF16, kind="Internal").ap()
    woutT_bf = nc.dram_tensor("woutT_bf", [D, D], BF16, kind="Internal").ap()

    cc_w_i = nc.dram_tensor("cc_w_i", [1, 8], F32, kind="Internal").ap()
    cc_w_o = nc.dram_tensor("cc_w_o", [1, 8], F32, kind="Internal").ap()
    cc_gq_i = nc.dram_tensor("cc_gq_i", [1, D], F32, kind="Internal").ap()
    cc_gq_o = nc.dram_tensor("cc_gq_o", [1, D], F32, kind="Internal").ap()
    cc_gk_i = nc.dram_tensor("cc_gk_i", [1, D], F32, kind="Internal").ap()
    cc_gk_o = nc.dram_tensor("cc_gk_o", [1, D], F32, kind="Internal").ap()
    cc_sc_i = nc.dram_tensor("cc_sc_i", [1, 2 * S], F32, kind="Internal").ap()
    cc_sc_o = nc.dram_tensor("cc_sc_o", [1, 2 * S], F32, kind="Internal").ap()
    cc_b_i = nc.dram_tensor("cc_b_i", [1, 8], F32, kind="Internal").ap()
    cc_b_o = nc.dram_tensor("cc_b_o", [1, 8], F32, kind="Internal").ap()

    ln1_sh = nc.dram_tensor("ln1_sh", [2 * S, D], BF16, kind="Internal",
                            addr_space="Shared").ap()
    ordidx_sh = nc.dram_tensor("ordidx_sh", [CAP, 1], U32, kind="Internal",
                               addr_space="Shared").ap()
    kv_sh = nc.dram_tensor("kv_sh", [CAP, 2 * D], BF16, kind="Internal",
                           addr_space="Shared").ap()
    delta_sh = nc.dram_tensor("delta_sh", [CAP, D], F32, kind="Internal",
                              addr_space="Shared").ap()

    rg = [[0, 1], [2, 3], [4, 5], [6, 7]]

    es0 = ExitStack()
    esA = ExitStack()
    esA2 = ExitStack()
    esP3 = ExitStack()
    esT = ExitStack()
    esF = ExitStack()
    esF2 = ExitStack()
    esW = ExitStack()

    def sbt(stk, name, shape, dt):
        return stk.enter_context(nc.sbuf_tensor(name, shape, dt))

    sem = lambda name: es0.enter_context(nc.semaphore(name))

    id_sb = sbt(es0, "id_sb", [128, 128], BF16)
    id32_sb = sbt(es0, "id32_sb", [128, 128], F32)
    ones_sb = sbt(es0, "ones_sb", [128, 1], BF16)
    iota64_sb = sbt(es0, "iota64_sb", [128, 1], F32)
    maskf_sb = sbt(es0, "maskf_sb", [128, MYPB * NPB * 128], BF16)
    cinfu_sb = sbt(es0, "cinfu_sb", [1, 8], U32)
    cinfob = sbt(es0, "cinfob", [128, 8], F32)
    wrb = sbt(es0, "wrb", [128, D], F32)
    sumwb = sbt(es0, "sumwb", [128, 1], F32)
    g1b = sbt(es0, "g1b", [128, D], BF16)
    g2b = sbt(es0, "g2b", [128, D], BF16)
    sumw_row = sbt(es0, "sumw_row", [1, 1], F32)
    zrow = sbt(es0, "zrow", [128, 64], F32)
    zidx = sbt(es0, "zidx", [4, 128], U32)
    epsln_c = sbt(es0, "epsln_c", [128, 1], F32)
    epsrms_c = sbt(es0, "epsrms_c", [128, 1], F32)
    csrow = sbt(es0, "csrow", [1, D], F32)
    cskrow = sbt(es0, "cskrow", [1, D], F32)
    gqrow = sbt(es0, "gqrow", [1, D], F32)
    grow = gqrow
    gqb = sbt(es0, "gqb", [128, D], BF16)
    gksc = sbt(es0, "gksc", [128, DC], F32)
    s_all = sbt(es0, "s_all", [128, NBLK], F32)
    c_all = sbt(es0, "c_all", [128, NBLK], F32)
    sc_all = sbt(es0, "sc_all", [128, NBLK], F32)
    scT = sbt(es0, "scT", [32, 128], F32)
    sxrow = sbt(es0, "sxrow", [1, 512], F32)
    sqrow = sbt(es0, "sqrow", [1, 512], F32)
    mean_t = [sbt(es0, f"mean_t{j}", [128, 1], F32) for j in range(3)]
    e2_t = [sbt(es0, f"e2_t{j}", [128, 1], F32) for j in range(3)]
    var_t = [sbt(es0, f"var_t{j}", [128, 1], F32) for j in range(3)]
    std_t = [sbt(es0, f"std_t{j}", [128, 1], F32) for j in range(3)]
    sume = [sbt(es0, f"sume{j}", [128, 1], F32) for j in range(2)]
    recip_t = [sbt(es0, f"recip{j}", [128, 1], F32) for j in range(2)]
    recqt = [sbt(es0, f"recqt{j}", [128, 1], BF16) for j in range(3)]
    sx_bl = [sbt(es0, f"sx_bl{j}", [128, 1], F32) for j in range(2)]
    sq_bl = [sbt(es0, f"sq_bl{j}", [128, 1], F32) for j in range(2)]
    sxw_t = [sbt(es0, f"sxw_t{j}", [128, 1], F32) for j in range(3)]
    sx1_t = [sbt(es0, f"sx1_t{j}", [128, 1], F32) for j in range(3)]
    sx1q_t = [sbt(es0, f"sx1q_t{j}", [128, 1], F32) for j in range(3)]
    s1_t = [sbt(es0, f"s1_t{j}", [128, 1], F32) for j in range(3)]
    c1_t = [sbt(es0, f"c1_t{j}", [128, 1], F32) for j in range(3)]
    sp_t = [sbt(es0, f"sp_t{j}", [128, 1], F32) for j in range(3)]
    c1s_t = [sbt(es0, f"c1s_t{j}", [128, 1], F32) for j in range(3)]
    sbf_scr = sbt(es0, "sbf_scr", [128, D], BF16)
    ln1_t = [sbt(es0, f"ln1_t{j}", [128, D], BF16) for j in range(2)]

    xfm = sbt(esA, "xfm", [128, DC * S], BF16)
    wA = sbt(esA, "wA", [128, DC * D], BF16)
    wB = sbt(esA, "wB", [128, DC * D], BF16)
    wVT = sbt(esA, "wVT", [128, DC * D], BF16)
    wR = sbt(esA, "wR", [128, DC * D], BF16)
    csb = sbt(esA, "csb", [128, D], F32)
    qs_sb = [sbt(esA, f"qs_sb{j}", [128, D], F32) for j in range(3)]
    qfix = [sbt(esA, f"qfix{j}", [128, D], F32) for j in range(2)]
    sqs = sbt(esA2, "sqs", [128, DC * 512], BF16)
    qa = [sbt(esA2, f"qa{j}", [128, D], BF16) for j in range(3)]
    esA2.close()
    qf32 = [sbt(esP3, f"qf32{j}", [128, D], BF16) for j in range(2)]
    xtm_t = [sbt(esP3, f"xtm_t{j}", [128, D], F32) for j in range(3)]
    x1_t = xtm_t
    u1_sb = qs_sb
    u2_sb = qfix
    scw = qfix
    wexp = qs_sb
    esP3.close()
    esA.close()

    scfull = sbt(esT, "scfull", [128, 64], F32)
    ws_sb = sbt(esT, "ws_sb", [128, 64], F32)
    candv = sbt(esT, "candv", [128, CANDK], F32)
    candi = sbt(esT, "candi", [128, CANDK], U32)
    candif = sbt(esT, "candif", [128, CANDK], F32)
    candtu = sbt(esT, "candtu", [128, CANDK], U32)
    candrep = sbt(esT, "candrep", [128, NCAND], F32)
    cmp_scr = sbt(esT, "cmp_scr", [128, NCAND], BF16)
    tchall = sbt(esT, "tchall", [128, MYCH], F32)
    tokall = sbt(esT, "tokall", [128, MYCH], U32)
    rank_f = [sbt(esT, f"rank_f{j}", [128, 1], F32) for j in range(2)]
    rank_u = [sbt(esT, f"rank_u{j}", [128, 1], U32) for j in range(2)]
    esT.close()

    wout_sb = sbt(esF, "wout_sb", [128, DC * D], BF16)
    qkv_sb = [sbt(esF, f"qkv_sb{b}", [128, 3 * D], BF16) for b in range(MYPB)]
    xsel = [sbt(esF, f"xsel{b}", [128, D], BF16) for b in range(MYPB)]
    xself = [sbt(esF, f"xself{b}", [128, D], F32) for b in range(MYPB)]
    h_sb = [sbt(esF, f"h_sb{b}", [128, D], BF16) for b in range(MYPB)]
    sums_sb = sbt(esF, "sums_sb", [1, NH * MYPB * 128], F32)
    recqall = sbt(esF, "recqall", [128, 2 * NH], F32)
    o_sb = [sbt(esF, f"o_sb{b}", [128, D], BF16) for b in range(MYPB)]
    idxp = [sbt(esF, f"idxp{b}", [128, 1], U32) for b in range(NPB)]
    idxf = [sbt(esF, f"idxf{b}", [128, 1], F32) for b in range(NPB)]
    idxm = [sbt(esF, f"idxm{b}", [128, 1], F32) for b in range(NPB)]
    idxm2 = [sbt(esF, f"idxm2{b}", [128, 1], F32) for b in range(NPB)]
    idxu = [sbt(esF, f"idxu{b}", [128, 1], U32) for b in range(NPB)]
    ssq_t = [sbt(esF, f"ssq_t{b}", [128, 1], F32) for b in range(MYPB)]
    rms_t = [sbt(esF, f"rms_t{b}", [128, 1], F32) for b in range(MYPB)]
    xattn = [sbt(esF, f"xattn{b}", [128, D], F32) for b in range(MYPB)]
    wqkvq = sbt(esF2, "wqkvq", [128, DC * D], BF16)
    wqkvk = sbt(esF2, "wqkvk", [128, DC * D], BF16)
    wqkvv = sbt(esF2, "wqkvv", [128, DC * D], BF16)
    kvall = [sbt(esF2, f"kvall{b}", [128, 2 * D], BF16) for b in range(NPB)]
    kfm = sbt(esF2, "kfm", [128, 8 * NPB * 128], BF16)
    qfm = sbt(esF2, "qfm", [128, 8 * MYPB * 128], BF16)
    hfm = sbt(esF2, "hfm", [128, MYPB * D], BF16)
    attm = [sbt(esF2, f"attm{j}", [128, NPB * 128], F32) for j in range(3)]
    probsT = [sbt(esF2, f"probsT{j}", [128, NPB * 128], BF16) for j in range(3)]
    ofm = sbt(esF2, "ofm", [128, MYPB * D], BF16)
    esF2.close()

    wf1 = [sbt(esW, f"wf1_{j}", [128, DC * 512], BF16) for j in range(2)]
    wf2 = [sbt(esW, f"wf2_{j}", [128, DC * 512], BF16) for j in range(2)]
    wf3 = [sbt(esW, f"wf3_{j}", [128, 4 * D], BF16) for j in range(2)]
    s2_t = [sbt(esW, f"s2_t{b}", [128, D], F32) for b in range(MYPB)]
    s2q_t = [sbt(esW, f"s2q_t{b}", [128, 1], F32) for b in range(MYPB)]
    rms2_t = [sbt(esW, f"rms2_t{b}", [128, 1], F32) for b in range(MYPB)]
    h2_sb = [sbt(esW, f"h2_sb{b}", [128, D], BF16) for b in range(MYPB)]
    h2fm = sbt(esW, "h2fm", [128, MYPB * D], BF16)
    asil = sbt(esW, "asil", [128, 512], F32)
    g_sb = sbt(esW, "g_sb", [128, 512], BF16)
    gfm = sbt(esW, "gfm", [128, 512], BF16)
    delta_my = [sbt(esW, f"delta_my{b}", [128, D], F32) for b in range(MYPB)]
    delta_ld = [sbt(esW, f"delta_ld{b}", [128, D], F32) for b in range(NPB)]
    esW.close()

    pb = [es0.enter_context(nc.psum_tensor(f"pb{j}", [128, 512], F32)) for j in range(6)]
    pb6 = es0.enter_context(nc.psum_tensor("pb6", [128, 512], F32))
    pT = es0.enter_context(nc.psum_tensor("pT", [128, 1024], BF16))

    m_hw = sem("m_hw"); m_gx = sem("m_gx")
    m_wq = sem("m_wq"); m_wk = sem("m_wk"); m_wv = sem("m_wv")
    m_wr = sem("m_wr"); m_w12 = sem("m_w12"); m_wqkv = sem("m_wqkv")
    m_wout = sem("m_wout")
    m_wf = [sem("m_wf0"), sem("m_wf1")]
    m_st2 = [sem("m_st0"), sem("m_st1")]
    m_xt = [sem("m_xt0"), sem("m_xt1")]
    m_stc = sem("m_stc")
    m_sd = sem("m_sd")
    m_sp = [sem("m_sp0"), sem("m_sp1")]
    m_ln = [sem("m_ln0"), sem("m_ln1")]
    m_hb2 = [sem("m_hb0"), sem("m_hb1")]
    m_gst2 = [sem("m_gst0"), sem("m_gst1")]
    m_gq3 = sem("m_gq3"); m_gi = sem("m_gi"); m_gm = sem("m_gm")
    m_lncp = sem("m_lncp"); m_pc = sem("m_pc")
    m_t = sem("m_t"); m_v = sem("m_v"); m_s = sem("m_s"); m_cc = sem("m_cc")
    m_g = sem("m_g")
    m_vx = sem("m_vx"); m_sx = sem("m_sx"); m_gx2 = sem("m_gx2")

    class ChainProxy:
        _PASS = {"wait_ge", "value_load", "dma_start", "indirect_dma_start",
                 "collective_compute", "then_inc", "nop"}

        def __init__(self, eng, semh):
            self._e = eng
            self._sem = semh
            self._n = 0

        def __getattr__(self, name):
            fn = getattr(self._e, name)
            if name in ChainProxy._PASS or name.startswith("_"):
                return fn

            def call(*a, **kw):
                ins = fn(*a, **kw)
                try:
                    ins.then_inc(self._sem, 1)
                except Exception:
                    return ins
                self._n += 1
                self._e.wait_ge(self._sem, self._n)
                return _MarkShim(self._e)
            return call

    class _MarkShim:
        def __init__(self, eng):
            self._e = eng

        def then_inc(self, semh, n):
            self._e.sem_inc(semh, n)
            return self

    T_P0 = lambda ct: 2 * (ct + 1)
    T_P0_END = 2 * NCT
    T_CSQ = T_P0_END + 1
    T_CSK = T_P0_END + 2
    T_P1Q = lambda i: T_CSK + (i + 1 if i < 3 else 2 * i - 2)
    T_P1G = lambda i: T_CSK + (2 * i + 5 if i < NBLK - 3 else NBLK + 1 + i)
    T_P1_END = T_CSK + 2 * NBLK
    T_P2Q = lambda i: T_P1_END + (i + 1 if i < 3 else 2 * i - 2)
    T_P2G = lambda i: T_P1_END + (2 * i + 5 if i < NBLK - 3 else NBLK + 1 + i)
    T_P2_END = T_P1_END + 2 * NBLK
    T_WVR = lambda u: T_P2_END + u + 1
    T_CSV = T_P2_END + 17
    T_P3Q = lambda i: T_CSV + i + 1
    T_SCT = T_CSV + NBLK + 1
    T_P3_END = T_SCT
    T_FH = lambda b: T_P3_END + b + 1
    T_QKV = lambda b, j: T_P3_END + 2 + 6 * b + j + 1
    T_QKV_END = T_P3_END + 14
    T_KT = lambda kb: T_QKV_END + kb + 1
    T_QT = lambda b: T_QKV_END + 4 + b + 1
    T_KQ_END = T_QKV_END + 6
    T_ATT = lambda h, b: T_KQ_END + (
        (2 * h + b + 1) if 2 * h + b < 2 else 3 * (2 * h + b) - 3)
    T_SUM = lambda h, b: T_KQ_END + (
        (3 * (2 * h + b) + 4) if 2 * h + b < 30 else 93 + 2 * (2 * h + b - 30))
    T_O = lambda h, b: T_KQ_END + (
        (3 * (2 * h + b) + 5) if 2 * h + b < 30 else 94 + 2 * (2 * h + b - 30))
    T_AT_END = T_KQ_END + 6 * NH
    T_OT = lambda b: T_AT_END + b + 1
    T_XA = lambda b: T_AT_END + 2 + b + 1
    T_H2T = lambda b: T_AT_END + 4 + b + 1
    T_XA_END = T_AT_END + 6
    T_SW = lambda ft, b, k: T_XA_END + 4 * (ft * MYPB + b) + k
    T_SW_END = T_XA_END + 4 * MYPB * NFT
    T_END = T_SW_END

    S_P0 = lambda ct: 8 * (ct + 1)
    S_P0_END = 8 * NCT
    S_P1 = lambda i, k: S_P0_END + (
        (1 if i == 0 else 3 * i) if k == 1 else
        (2 if i == 0 else 3 * i + 1) if k == 2 else
        (3 * i + 5 if i < NBLK - 1 else 3 * NBLK))
    S_P1_END = S_P0_END + 3 * NBLK
    S_P2 = lambda i, k: S_P1_END + (
        (1 if i == 0 else 2 * i) if k == 1 else
        (2 * i + 3 if i < NBLK - 1 else 2 * NBLK))
    S_P2_END = S_P1_END + 2 * NBLK
    S_P3 = lambda i, k: S_P2_END + (
        (i + 1 if i < 2 else 5 * i - 6) if k == 1 else
        (5 * i + 3 if i < NBLK - 1 else 5 * NBLK - 3) if k == 2
        else (5 * i + 2 + k if i <= NBLK - 3 else
              (5 * NBLK - 9 + k if i == NBLK - 2 else 5 * NBLK - 5 + k)))
    S_P3_END = S_P2_END + 5 * NBLK
    S_XS = lambda b: S_P3_END + 2 * b + 1
    S_RSQ = lambda b: S_P3_END + 2 * b + 2
    S_XS_END = S_P3_END + 2 * MYPB
    S_EXP = lambda h, b: S_XS_END + 2 * h + b + 1
    S_EXP_END = S_XS_END + 2 * NH
    S_H2 = lambda b, k: S_EXP_END + 2 * b + k
    S_H2_END = S_EXP_END + 2 * MYPB
    S_SIL = lambda ft, b: S_H2_END + ft * MYPB + b + 1
    S_SIL_END = S_H2_END + MYPB * NFT
    S_END = S_SIL_END

    V_MZ = 1
    V_P0 = lambda ct: V_MZ + 2 * (ct + 1)
    V_P0_END = V_MZ + 2 * NCT
    V_CSQ = V_P0_END + 1
    V_P1 = lambda i, k: V_CSQ + (
        (k if i == 0 else 4 * i + k - 1) if k < 4 else
        (4 * i + 7 if i < NBLK - 1 else 4 * NBLK))
    V_P1_END = V_CSQ + 4 * NBLK
    V_GQ = V_P1_END + 1
    V_WKP = V_P1_END + 2
    V_CSK = V_P1_END + 3
    V_P2 = lambda i, k: V_CSK + (
        (1 if i == 0 else 2 * i) if k == 1 else
        (2 * i + 3 if i < NBLK - 1 else 2 * NBLK))
    V_P2_END = V_CSK + 2 * NBLK
    V_GK = V_P2_END + 1
    V_WRP = V_P2_END + 2
    V_WVR = lambda u: V_WRP + u + 1
    V_CSV = V_WRP + 17
    V_P3 = lambda i, k: V_CSV + 5 * i + k
    V_P3_END = V_CSV + 5 * NBLK
    V_SCT = V_P3_END + 1
    V_TK = lambda k: V_SCT + k
    V_TK_END = V_SCT + 6
    V_RK = lambda j: V_TK_END + j + 1
    V_RK_END = V_TK_END + MYCH
    V_XS = lambda b: V_RK_END + b + 1
    V_XS_END = V_RK_END + MYPB
    V_FH = lambda b: V_XS_END + b + 1
    V_QKV = lambda b, j: V_XS_END + 2 + 6 * b + j + 1
    V_QKV_END = V_XS_END + 14
    V_KT = lambda kb: V_QKV_END + kb + 1
    V_QT = lambda b: V_QKV_END + 4 + b + 1
    V_KQ_END = V_QKV_END + 6
    V_MSK = lambda h, b: V_KQ_END + (
        (2 * h + b + 1) if 2 * h + b < 2 else 3 * (2 * h + b) - 3)
    V_SUMC = lambda h, b: V_KQ_END + (
        (3 * (2 * h + b) + 4) if 2 * h + b < 30 else 93 + 2 * (2 * h + b - 30))
    V_OE = lambda h, b: V_KQ_END + (
        (3 * (2 * h + b) + 5) if 2 * h + b < 30 else 94 + 2 * (2 * h + b - 30))
    V_AT_END = V_KQ_END + 6 * NH
    V_ON = lambda b: V_AT_END + b + 1
    V_OT = lambda b: V_AT_END + 2 + b + 1
    V_XAE = lambda b: V_AT_END + 4 + b + 1
    V_S2 = lambda b: V_AT_END + 6 + b + 1
    V_H2 = lambda b: V_AT_END + 8 + b + 1
    V_H2T = lambda b: V_AT_END + 10 + b + 1
    V_H2_END = V_AT_END + 12
    V_SW = lambda ft, b, k: V_H2_END + 2 * (ft * MYPB + b) + k
    V_SW_END = V_H2_END + 2 * MYPB * NFT
    V_XFF = lambda b: V_SW_END + b + 1
    V_XFF_END = V_SW_END + MYPB
    V_FIL = lambda b: V_XFF_END + b + 1
    V_FIL_END = V_XFF_END + NPB
    V_END = V_FIL_END

    G_P3 = lambda i: 2 * i + 1
    G_SC = lambda i: 2 * i + 2
    G_END = NBLK

    CC_WARM, CC_GQ, CC_GK, CC_SC, CC_ORD, CC_KV, CC_DEL = 1, 2, 3, 4, 5, 6, 7

    HW_CONST = 12 * W16

    STW = lambda i: 32 * (i // 2 + 1)
    XTW = lambda i: 16 * (i // 2 + 1)
    SPW = lambda ct: 32 * (ct // 2 + 1)
    LNW = lambda i: 16 * (i // 2 + 1)
    HBW = lambda i: 16 * (i // 2 + 1)
    GSTW = lambda i: 16 * (i // 2 + 1)
    LN_FULL = 16 * (NBLK // 2)
    HB_FULL = 16 * (NBLK // 2)

    STC_GKC = 16
    STC_SCF = 32
    STC_CREP = 48
    STC_TCH = 64
    STC_TOK = 80
    STC_IDXP = lambda b: 80 + 16 * (b + 1)
    STC_IDXP_END = 80 + 16 * MYPB
    STC_KV = lambda kb: STC_IDXP_END + 16 * (kb + 1)
    STC_KV_END = STC_IDXP_END + 16 * NPB
    STC_RQA = STC_KV_END + 16
    STC_DEL = lambda b: STC_RQA + 16 * (b + 1)
    STC_DEL_END = STC_RQA + 16 * NPB
    STC_IDXA = lambda b: STC_DEL_END + 16 * (b + 1)
    STC_END = STC_DEL_END + 16 * NPB

    SD_CSQ = 16
    SD_SUMW = 32
    SD_CSB1 = 48
    SD_SWB = 64
    SD_GQ = 80
    SD_CSK = 96
    SD_CSB2 = 112
    SD_GK = 128
    SD_CSV = 144
    SD_CSB3 = 160
    SD_SC = 176
    SD_CANDV = 192
    SD_CANDT = 208
    SD_KV = lambda b: 208 + 16 * (b + 1)
    SD_KV_END = 208 + 16 * MYPB
    SD_SUMSA = SD_KV_END + 16
    SD_DEL = lambda b: SD_SUMSA + 16 * (b + 1)
    SD_DEL_END = SD_SUMSA + 16 * MYPB
    SD_END = SD_DEL_END

    GX_END = 16 * NCT
    W_FULL = 16 * 8
    W12_FULL = 32
    WQKV_FULL = 16 * 24
    _ft_dmas = [8 + 8 + ((fs + 127) // 128) for fs in FT_SIZES]
    _wf_cum = [[], []]
    for _j, _d in enumerate(_ft_dmas):
        _wf_cum[_j % 2].append((sum(_wf_cum[_j % 2][-1:]) if False else 0))
    _wf_tot = [0, 0]
    _wf_at = []
    for _j, _d in enumerate(_ft_dmas):
        _wf_tot[_j % 2] += 16 * _d
        _wf_at.append(_wf_tot[_j % 2])
    WF_AT = lambda j: _wf_at[j]
    GQ3 = lambda i: 16 * (i + 1)
    GQ3_END = 16 * NBLK
    GM_GQB = 16
    GM_GQR = 32
    GI_OZ = 16
    GI_RS = lambda j: GI_OZ + 16 * (j + 1)
    GI_RS_END = GI_OZ + 16 * MYCH
    GI_GA = lambda b: GI_RS_END + 16 * (b + 1)
    GI_GA_END = GI_RS_END + 16 * MYPB
    GI_FS = lambda b: GI_GA_END + 16 * (b + 1)
    GI_FS_END = GI_GA_END + 16 * NPB

    cnt = {}
    def mk(inst, semh, n=1, expect=None):
        key = semh.name
        cnt[key] = cnt.get(key, 0) + n
        if expect is not None:
            assert cnt[key] == expect, (key, cnt[key], expect)
        return inst.then_inc(semh, n)

    blk = es0.enter_context(nc.Block())

    def wchunk(t, c, n=D):
        return t[:, c * n:(c + 1) * n]

    @blk.sync
    def _(e):
        e.dma_start(out=id_sb[:, :], in_=ident_d[:, :]).then_inc(m_hw, W16)
        e.dma_start(out=id32_sb[:, :], in_=idf32_d[:, :]).then_inc(m_hw, W16)
        e.dma_start(out=ones_sb[:, :], in_=ones_d[:, :]).then_inc(m_hw, W16)
        e.dma_start(out=iota64_sb[:, :], in_=iota64_d[:, :]).then_inc(m_hw, W16)
        e.wait_ge(m_hw, 4 * W16)
        e.dma_start(out=maskf_sb[:, :], in_=maskf_d[:, :]).then_inc(m_hw, W16)
        e.dma_start(out=cinfu_sb[:, :], in_=cinfu_d[:, :]).then_inc(m_hw, W16)
        e.dma_start(out=cinfob[:, :], in_=cinff_d[0:1, :].to_broadcast([128, 8])).then_inc(m_hw, W16)
        e.dma_start(out=wrb[:, :], in_=wrout_d[0:1, :].to_broadcast([128, D])).then_inc(m_hw, W16)
        e.dma_start(out=scT[0:1, 0:1], in_=cinff_d[0:1, 1:2]).then_inc(m_hw, W16)
        e.wait_ge(m_v, V_MZ)
        e.dma_start(out=cc_sc_i.rearrange("a (p c) -> (a p) c", p=128),
                    in_=zrow[:, :]).then_inc(m_hw, W16)
        e.dma_start(out=cc_w_i[:, :], in_=zrow[0:1, 0:8]).then_inc(m_hw, W16)
        e.dma_start(out=cc_b_i[:, :], in_=zrow[0:1, 0:8]).then_inc(m_hw, W16)
        cnt["m_hw"] = 12 * W16

        def stc(inst, expect):
            mk(inst, m_stc, W16, expect=expect)
            e.wait_ge(m_stc, expect)

        if not ST1:
            e.wait_ge(m_sp[0], SPW(NCT - 2))
            e.wait_ge(m_sp[1], SPW(NCT - 1))
            if DBG:
                e.wait_ge(m_v, V_CSQ)
                e.dma_start(out=d_gq[:, :], in_=csq_dram[:, :]).then_inc(m_hw, W16)
                e.wait_ge(m_hw, HW_CONST + 16)
            return
        for i in range(NBLK):
            ct = i // 4
            e.wait_ge(m_sp[ct % 2], SPW(ct))
            if i >= 2:
                e.wait_ge(m_v, V_P1(i - 2, 2))
            mk(e.dma_start(out=sx_bl[i % 2][:, :],
                           in_=sx_dram[i:i + 1, :].rearrange("a (p c) -> (a p) c", c=1)),
               m_st2[i % 2], W16)
            mk(e.dma_start(out=sq_bl[i % 2][:, :],
                           in_=sq_dram[i:i + 1, :].rearrange("a (p c) -> (a p) c", c=1)),
               m_st2[i % 2], W16, expect=STW(i))
        if not ST2:
            if NO_AR1:
                e.wait_ge(m_gst2[0], GSTW(NBLK - 2))
                e.wait_ge(m_gst2[1], GSTW(NBLK - 1))
                e.wait_ge(m_t, T_P1_END)
            else:
                e.wait_ge(m_cc, CC_GQ)
            if DBG:
                e.dma_start(out=d_gq[:, :], in_=cc_gq_o[:, :]).then_inc(m_hw, W16)
                e.wait_ge(m_gst2[0], GSTW(NBLK - 2))
                e.wait_ge(m_gst2[1], GSTW(NBLK - 1))
                e.dma_start(out=d_q[:, :], in_=q_dram[:, :]).then_inc(m_hw, W16)
                e.wait_ge(m_hw, HW_CONST + 32)
            return
        e.wait_ge(m_cc, CC_GK)
        stc(e.dma_start(out=gksc[:, :],
                        in_=cc_gk_o.rearrange("a (c p) -> (a p) c", p=128),
                        allow_slow_non_contiguous=True), STC_GKC)
        par_c = e.value_load(cinfu_sb[0:1, 3:4])
        mylncp = ln1_sh.rearrange("(o r) d -> o (r d)", o=2)[bass.ts(par_c, 1), :]
        ln1f = ln1_loc.rearrange("r d -> (r d)")
        BLKE = 128 * D
        LNCH = [(0, 14), (14, 8), (22, 8), (30, 2)]

        def lncp(k):
            b0, nb = LNCH[k]
            e.wait_ge(m_ln[0], 16 * ((b0 + nb + 1) // 2))
            e.wait_ge(m_ln[1], 16 * ((b0 + nb + 1) // 2))
            mk(e.dma_start(out=mylncp[:, b0 * BLKE:(b0 + nb) * BLKE],
                           in_=ln1f[b0 * BLKE:(b0 + nb) * BLKE]), m_lncp, W16)

        for i in range(NBLK):
            if i >= 3:
                e.wait_ge(m_g, G_SC(i - 3))
                e.wait_ge(m_hb2[(i - 1) % 2], HBW(i - 3))
            mk(e.dma_start(out=xtm_t[i % 3][:, :],
                           in_=x_tm[128 * i:128 * (i + 1), :]),
               m_xt[i % 2], W16, expect=XTW(i))
            if i == 17:
                lncp(0)
            elif i == 25:
                lncp(1)
            if i >= 1:
                e.wait_ge(m_s, S_P3(i - 1, 5))
                mk(e.dma_start(out=out[128 * (i - 1):128 * i, :],
                               in_=xtm_t[(i - 1) % 3][:, :]),
                   m_hb2[(i - 1) % 2], W16, expect=HBW(i - 1))
        lncp(2)
        lncp(3)
        e.wait_ge(m_s, S_P3(NBLK - 1, 5))
        mk(e.dma_start(out=out[128 * (NBLK - 1):128 * NBLK, :],
                       in_=xtm_t[(NBLK - 1) % 3][:, :]),
           m_hb2[(NBLK - 1) % 2], W16, expect=HBW(NBLK - 1))
        if not ST3:
            e.wait_ge(m_sd, SD_SC)
            e.wait_ge(m_hb2[0], HB_FULL)
            e.wait_ge(m_hb2[1], HB_FULL)
            if DBG:
                e.dma_start(out=d_gq[:, :], in_=cc_gq_o[:, :]).then_inc(m_hw, W16)
                e.dma_start(out=d_gk[:, :], in_=cc_gk_o[:, :]).then_inc(m_hw, W16)
                e.dma_start(out=d_sc[:, :], in_=cc_sc_i[:, :]).then_inc(m_hw, W16)
                e.wait_ge(m_hw, HW_CONST + 48)
            return
        e.wait_ge(m_cc, CC_SC)
        stc(e.dma_start(out=scfull[:, :],
                        in_=cc_sc_o.rearrange("a (p c) -> (a p) c", p=128)),
            STC_SCF)
        e.wait_ge(m_sd, SD_CANDT)
        stc(e.dma_start(out=candrep[:, :],
                        in_=candv_dram[0:1, :].to_broadcast([128, NCAND])),
            STC_CREP)
        par_r = e.value_load(cinfu_sb[0:1, 3:4])
        cvv = candv_dram.rearrange("a (o c p) -> (a o) p c", o=2, p=128)[bass.ts(par_r, 1), :, :]
        ctv = candt_dram.rearrange("a (o c p) -> (a o) p c", o=2, p=128)[bass.ts(par_r, 1), :, :]
        stc(e.dma_start(out=tchall[:, :],
                        in_=cvv, allow_slow_non_contiguous=True), STC_TCH)
        stc(e.dma_start(out=tokall[:, :],
                        in_=ctv, allow_slow_non_contiguous=True), STC_TOK)
        if not ST4:
            e.wait_ge(m_cc, CC_ORD)
            if DBG:
                e.dma_start(out=d_oi[:, :], in_=ordidx_sh[:, :]).then_inc(m_hw, W16)
                e.dma_start(out=d_sc[:, :], in_=cc_sc_o[:, :]).then_inc(m_hw, W16)
                e.wait_ge(m_hw, HW_CONST + 32)
            return
        ordv = ordidx_sh.rearrange("(o r) b -> o r b", o=2)[bass.ts(par_r, 1), :, :]
        e.wait_ge(m_cc, CC_ORD)
        for b in range(MYPB):
            stc(e.dma_start(out=idxp[b][:, :],
                            in_=ordv[:, 128 * b:128 * (b + 1), :]), STC_IDXP(b))
        e.wait_ge(m_cc, CC_KV)
        for kb in range(NPB):
            stc(e.dma_start(out=kvall[kb][:, :],
                            in_=kv_sh[128 * kb:128 * (kb + 1), :]), STC_KV(kb))
        if not ST5:
            e.wait_ge(m_v, V_KT(NPB - 1))
            e.wait_ge(m_sd, SD_KV_END)
            return
        smv = sums_dram.rearrange("o (w p) -> o p w", p=128)[bass.ts(par_r, 1), :, :]
        e.wait_ge(m_sd, SD_SUMSA)
        stc(e.dma_start(out=recqall[:, :], in_=smv,
                        allow_slow_non_contiguous=True), STC_RQA)
        if not ST6:
            e.wait_ge(m_v, V_S2(MYPB - 1))
            e.wait_ge(m_s, S_H2(MYPB - 1, 2))
            return
        e.wait_ge(m_cc, CC_DEL)
        for b in range(NPB):
            stc(e.dma_start(out=delta_ld[b][:, :],
                            in_=delta_sh[128 * b:128 * (b + 1), :]), STC_DEL(b))
        for b in range(NPB):
            stc(e.dma_start(out=idxp[b][:, :],
                            in_=ordidx_sh[128 * b:128 * (b + 1), :]), STC_IDXA(b))
        e.wait_ge(m_gi, GI_FS_END)

    @blk.scalar
    def _(e):
        e = ChainProxy(e, m_sx)
        e.wait_ge(m_hw, HW_CONST)
        par_s = e.value_load(cinfu_sb[0:1, 3:4])
        myln1 = ln1_sh.rearrange("(o r) d -> o (r d)", o=2)[bass.ts(par_s, 1), :]
        mykv = kv_sh.rearrange("(o r) d -> o r d", o=2)[bass.ts(par_s, 1), :, :]
        mydel = delta_sh.rearrange("(o r) d -> o r d", o=2)[bass.ts(par_s, 1), :, :]
        mysums = sums_dram[bass.ts(par_s, 1), :]

        def sdw(inst, expect):
            mk(inst, m_sd, W16, expect=expect)
            e.wait_ge(m_sd, expect)


        def statw(ct):
            e.wait_ge(m_v, V_P0(ct))
            if ct >= 2:
                e.wait_ge(m_sp[ct % 2], SPW(ct - 2))
            mk(e.dma_start(out=sx_dram[4 * ct:4 * (ct + 1), :],
                           in_=sxrow[0:1, :]), m_sp[ct % 2], W16)
            mk(e.dma_start(out=sq_dram[4 * ct:4 * (ct + 1), :],
                           in_=sqrow[0:1, :]), m_sp[ct % 2], W16,
               expect=SPW(ct))

        for ct in range(NCT):
            e.wait_ge(m_gx, 16 * (ct + 1))
            if ct >= 1:
                e.wait_ge(m_t, T_P0(ct - 1))
            for c in range(DC):
                mk(e.activation(sqs[:, c * 512:(c + 1) * 512],
                                xfm[:, c * S + 512 * ct:c * S + 512 * (ct + 1)],
                                AF.Square), m_s, 1)
            assert cnt["m_s"] == S_P0(ct)
            if ct >= 1:
                statw(ct - 1)
        statw(NCT - 1)
        e.wait_ge(m_v, V_CSQ)
        sdw(e.dma_start(out=csq_dram[:, :], in_=csrow[:, :]), SD_CSQ)
        sdw(e.dma_start(out=sumw_dram[:, :], in_=sumw_row[:, :]), SD_SUMW)
        sdw(e.dma_start(out=csb[:, :],
                        in_=csq_dram[0:1, :].to_broadcast([128, D])), SD_CSB1)
        sdw(e.dma_start(out=sumwb[:, :],
                        in_=sumw_dram[0:1, :].to_broadcast([128, 1])), SD_SWB)
        if not ST1:
            return
        for i in range(NBLK):
            e.wait_ge(m_v, V_P1(i, 1))
            e.activation(std_t[i % 2][:, :], var_t[i % 2][:, :], AF.Ln,
                         bias=epsln_c[:, :])
            mk(e.activation(s_all[:, i:i + 1], std_t[i % 2][:, :], AF.Exp,
                            scale=-0.5), m_s, 1, expect=S_P1(i, 1))
            e.wait_ge(m_t, T_P1Q(i))
            if i >= 3:
                e.wait_ge(m_v, V_P1(i - 3, 4))
            for half in range(2):
                ins = e.activation(qs_sb[i % 3][:, 512 * half:512 * (half + 1)],
                                   pb[2 * (i % 2) + half][:, :], AF.Identity,
                                   scale=s_all[:, i:i + 1])
                if half:
                    mk(ins, m_s, 1, expect=S_P1(i, 2))
            if i >= 1:
                e.wait_ge(m_v, V_P1(i - 1, 3))
                mk(e.activation(wexp[(i - 1) % 3][:, :], qfix[(i - 1) % 2][:, :],
                                AF.Exp, scale=SCALE,
                                accum_out=sume[(i - 1) % 2][:, :]), m_s, 1,
                   expect=S_P1(i - 1, 3))
        e.wait_ge(m_v, V_P1(NBLK - 1, 3))
        mk(e.activation(wexp[(NBLK - 1) % 3][:, :], qfix[(NBLK - 1) % 2][:, :],
                        AF.Exp, scale=SCALE,
                        accum_out=sume[(NBLK - 1) % 2][:, :]), m_s, 1,
           expect=S_P1(NBLK - 1, 3))
        e.wait_ge(m_v, V_GQ)
        sdw(e.dma_start(out=cc_gq_i[:, :], in_=gqrow[:, :]), SD_GQ)
        if not ST2:
            return
        e.wait_ge(m_v, V_CSK)
        sdw(e.dma_start(out=csk_dram[:, :], in_=cskrow[:, :]), SD_CSK)
        sdw(e.dma_start(out=csb[:, :],
                        in_=csk_dram[0:1, :].to_broadcast([128, D])), SD_CSB2)
        for i in range(NBLK):
            e.wait_ge(m_t, T_P2Q(i))
            if i >= 3:
                e.wait_ge(m_v, V_P2(i - 3, 2))
            for half in range(2):
                ins = e.activation(qs_sb[i % 3][:, 512 * half:512 * (half + 1)],
                                   pb[2 * (i % 2) + half][:, :], AF.Identity,
                                   scale=s_all[:, i:i + 1])
                if half:
                    mk(ins, m_s, 1, expect=S_P2(i, 1))
            if i >= 1:
                e.wait_ge(m_v, V_P2(i - 1, 1))
                mk(e.activation(wexp[(i - 1) % 3][:, :], qfix[(i - 1) % 2][:, :],
                                AF.Exp, scale=SCALE,
                                accum_out=sume[(i - 1) % 2][:, :]), m_s, 1,
                   expect=S_P2(i - 1, 2))
        e.wait_ge(m_v, V_P2(NBLK - 1, 1))
        mk(e.activation(wexp[(NBLK - 1) % 3][:, :], qfix[(NBLK - 1) % 2][:, :],
                        AF.Exp, scale=SCALE,
                        accum_out=sume[(NBLK - 1) % 2][:, :]), m_s, 1,
           expect=S_P2(NBLK - 1, 2))
        e.wait_ge(m_v, V_GK)
        sdw(e.dma_start(out=cc_gk_i[:, :], in_=grow[:, :]), SD_GK)
        e.wait_ge(m_v, V_CSV)
        sdw(e.dma_start(out=csv_dram[:, :], in_=csrow[:, :]), SD_CSV)
        sdw(e.dma_start(out=csb[:, :],
                        in_=csv_dram[0:1, :].to_broadcast([128, D])), SD_CSB3)
        def p3sq(j):
            e.wait_ge(m_v, V_P3(j, 2))
            mk(e.activation(sbf_scr[:, :], x1_t[j % 3][:, :], AF.Square,
                            accum_out=sx1q_t[j % 3][:, :]), m_s, 1,
               expect=S_P3(j, 2))

        def p3tail(j):
            e.wait_ge(m_v, V_P3(j, 3))
            e.activation(std_t[j % 3][:, :], var_t[j % 3][:, :], AF.Ln,
                         bias=epsln_c[:, :])
            mk(e.activation(s1_t[j % 3][:, :], std_t[j % 3][:, :], AF.Exp,
                            scale=-0.5), m_s, 1, expect=S_P3(j, 3))
            e.wait_ge(m_v, V_P3(j, 4))
            if j >= 2:
                e.wait_ge(m_ln[j % 2], LNW(j - 2))
            mk(e.activation(ln1_t[j % 2][:, :], x1_t[j % 3][:, :], AF.Identity,
                            scale=s1_t[j % 3][:, :], bias=c1_t[j % 3][:, :]),
               m_s, 1, expect=S_P3(j, 4))
            mk(e.dma_start(out=ln1_loc[128 * j:128 * (j + 1), :],
                           in_=ln1_t[j % 2][:, :]), m_ln[j % 2], W16,
               expect=LNW(j))
            e.wait_ge(m_g, G_SC(j))
            mk(e.activation(xtm_t[j % 3][:, :], x1_t[j % 3][:, :], AF.Identity,
                            scale=sp_t[j % 3][:, :], bias=c1_t[j % 3][:, :]),
               m_s, 1, expect=S_P3(j, 5))

        for i in range(NBLK):
            e.wait_ge(m_t, T_P3Q(i))
            if i >= 2:
                e.wait_ge(m_v, V_P3(i - 2, 1))
            for half in range(2):
                ins = e.activation(u1_sb[i % 2][:, 512 * half:512 * (half + 1)],
                                   pb[2 * (i % 3) + half][:, :], AF.Identity,
                                   scale=s_all[:, i:i + 1])
                if half:
                    mk(ins, m_s, 1, expect=S_P3(i, 1))
            if i >= 2:
                p3tail(i - 2)
            if i >= 1:
                p3sq(i - 1)
        p3tail(NBLK - 2)
        p3sq(NBLK - 1)
        p3tail(NBLK - 1)
        e.wait_ge(m_v, V_SCT)
        scv = cc_sc_i.rearrange("a (r p) -> (a r) p", p=128)
        sdw(e.dma_start(out=scv[bass.ts(par_s, 32), :], in_=scT[:, :]), SD_SC)
        if not ST3:
            return
        e.wait_ge(m_v, V_TK_END)
        mk(e.dma_start(out=candv_dram.rearrange("a (p c) -> (a p) c", p=128),
                       in_=candv[:, :]), m_sd, W16, expect=SD_CANDV)
        sdw(e.dma_start(out=candt_dram.rearrange("a (p c) -> (a p) c", p=128),
                        in_=candtu[:, :]), SD_CANDT)
        e.wait_ge(m_pc, 16 * 16)
        qv = wqkvT_bf.rearrange("(c p) n -> p c n", p=128)
        for part in range(3):
            for c in range(DC):
                mk(e.dma_start(
                    out=wchunk([wqkvq, wqkvk, wqkvv][part], c),
                    in_=qv[:, c, part * D:(part + 1) * D]), m_wqkv, W16)
            e.wait_ge(m_wqkv, 128 * (part + 1))
        assert cnt["m_wqkv"] == WQKV_FULL
        e.wait_ge(m_pc, 24 * 16)
        e.wait_ge(m_v, V_RK(MYCH - 1))
        e.wait_ge(m_gi, GI_RS(MYCH - 1))
        ov = woutT_bf.rearrange("(c p) n -> p c n", p=128)
        for c in range(DC):
            mk(e.dma_start(out=wchunk(wout_sb, c), in_=ov[:, c, :]),
               m_wout, W16)
        assert cnt["m_wout"] == W_FULL
        if not ST4:
            return
        for b in range(MYPB):
            e.wait_ge(m_gi, GI_GA(b))
            e.activation(sbf_scr[:, :], xsel[b][:, :], AF.Square,
                         accum_out=ssq_t[b][:, :])
            mk(e.copy(out=xself[b][:, :], in_=xsel[b][:, :]), m_s, 1,
               expect=S_XS(b))
            e.activation(rms_t[b][:, :], ssq_t[b][:, :], AF.Ln,
                         scale=INV_D, bias=epsrms_c[:, :])
            mk(e.activation(ssq_t[b][:, :], rms_t[b][:, :], AF.Exp,
                            scale=-0.5), m_s, 1, expect=S_RSQ(b))
        for b in range(MYPB):
            e.wait_ge(m_v, V_QKV(b, 5))
            sdw(e.dma_start(out=mykv[:, 128 * b:128 * (b + 1), :],
                            in_=qkv_sb[b][:, D:3 * D]), SD_KV(b))
        if not ST5:
            return
        for h in range(NH):
            for b in range(MYPB):
                w = 2 * h + b
                e.wait_ge(m_v, V_MSK(h, b))
                if w >= 3:
                    e.wait_ge(m_t, T_O((w - 3) // 2, (w - 3) % 2))
                mk(e.activation(probsT[w % 3][:, :], attm[w % 3][:, :], AF.Exp,
                                scale=0.125), m_s, 1, expect=S_EXP(h, b))
        e.wait_ge(m_v, V_SUMC(NH - 1, MYPB - 1))
        sdw(e.dma_start(out=mysums, in_=sums_sb[0:1, :]), SD_SUMSA)
        for b in range(MYPB):
            e.wait_ge(m_v, V_S2(b))
            mk(e.activation(sbf_scr[:, :], s2_t[b][:, :], AF.Square,
                            accum_out=s2q_t[b][:, :]), m_s, 1, expect=S_H2(b, 1))
            e.activation(rms2_t[b][:, :], s2q_t[b][:, :], AF.Ln,
                         scale=INV_D, bias=epsrms_c[:, :])
            mk(e.activation(s2q_t[b][:, :], rms2_t[b][:, :], AF.Exp,
                            scale=-0.5), m_s, 1, expect=S_H2(b, 2))
        if not ST6:
            return
        for ft in range(NFT):
            for b in range(MYPB):
                fs = FT_SIZES[ft]
                e.wait_ge(m_t, T_SW(ft, b, 2))
                if (ft * MYPB + b) >= 1:
                    pi = ft * MYPB + b - 1
                    e.wait_ge(m_v, V_SW(pi // MYPB, pi % MYPB, 1))
                mk(e.activation(asil[:, 0:fs], pb[0][:, 0:fs], AF.Silu), m_s,
                   1, expect=S_SIL(ft, b))
        for b in range(MYPB):
            e.wait_ge(m_v, V_XFF(b))
            sdw(e.dma_start(out=mydel[:, 128 * b:128 * (b + 1), :],
                            in_=delta_my[b][:, :]), SD_DEL(b))

    @blk.vector
    def _(e):
        e = ChainProxy(e, m_vx)
        e.memset(zrow[:, :], 0.0)
        e.memset(epsln_c[:, :], EPS_LN)
        e.memset(epsrms_c[:, :], EPS_RMS)
        mk(e.memset(zidx[:, :], 0), m_v, 1, expect=V_MZ)
        for ct in range(NCT):
            e.wait_ge(m_t, T_P0(ct))
            if ct >= 1:
                e.wait_ge(m_sp[(ct - 1) % 2], SPW(ct - 1))
            mk(e.tensor_copy(sxrow[0:1, :], pb[4][0:1, :]), m_v, 1)
            mk(e.tensor_copy(sqrow[0:1, :], pb[5][0:1, :]), m_v, 1,
               expect=V_P0(ct))
        e.wait_ge(m_t, T_CSK)
        e.tensor_copy(csrow[0:1, 0:512], pb[4][0:1, :])
        e.tensor_copy(csrow[0:1, 512:1024], pb[5][0:1, :])
        e.tensor_copy(cskrow[0:1, 0:512], pb[0][0:1, :])
        e.tensor_copy(cskrow[0:1, 512:1024], pb[1][0:1, :])
        e.wait_ge(m_hw, HW_CONST)
        mk(e.reduce_sum(sumw_row[:, :], wrb[0:1, :], axis=AXX), m_v, 1,
           expect=V_CSQ)
        if not ST1:
            return

        def p1norm(j, vm=None):
            e.wait_ge(m_s, S_P1(j, 3) if vm is None else vm[0])
            if j >= 3:
                e.wait_ge(m_t, (T_P1G if vm is None else vm[1])(j - 3))
            with nc.allow_low_precision(reason="1/sum feeds a bf16 matmul"):
                e.reciprocal(recqt[j % 3][:, :], sume[j % 2][:, :])
            mk(e.tensor_mul(qa[j % 3][:, :], qfix[j % 2][:, :],
                            wexp[j % 3][:, :]), m_v, 1,
               expect=V_P1(j, 4) if vm is None else vm[2](j))

        for i in range(NBLK):
            e.wait_ge(m_st2[i % 2], STW(i))
            e.tensor_scalar_mul(mean_t[i % 2][:, :], sx_bl[i % 2][:, :], INV_D)
            e.tensor_scalar_mul(e2_t[i % 2][:, :], sq_bl[i % 2][:, :], INV_D)
            e.tensor_mul(var_t[i % 2][:, :], mean_t[i % 2][:, :],
                         mean_t[i % 2][:, :])
            mk(e.tensor_sub(var_t[i % 2][:, :], e2_t[i % 2][:, :],
                            var_t[i % 2][:, :]), m_v, 1, expect=V_P1(i, 1))
            e.wait_ge(m_s, S_P1(i, 1))
            mk(e.tensor_scalar(c_all[:, i:i + 1], mean_t[i % 2][:, :],
                               s_all[:, i:i + 1], -1.0, op0=ALU.mult,
                               op1=ALU.mult), m_v, 1, expect=V_P1(i, 2))
            e.wait_ge(m_s, S_P1(i, 2))
            e.wait_ge(m_sd, SD_CSB1)
            if i >= 2:
                e.wait_ge(m_s, S_P1(i - 2, 3))
                e.wait_ge(m_gst2[i % 2], GSTW(i - 2))
            mk(e.scalar_tensor_tensor(qfix[i % 2][:, :], csb[:, :],
                                      c_all[:, i:i + 1], qs_sb[i % 3][:, :],
                                      op0=ALU.mult, op1=ALU.add), m_v, 1,
               expect=V_P1(i, 3))
            if i >= 1:
                p1norm(i - 1)
        p1norm(NBLK - 1)
        e.wait_ge(m_t, T_P1G(NBLK - 1))
        e.tensor_copy(gqrow[0:1, 0:512], pb[4][0:1, :])
        mk(e.tensor_copy(gqrow[0:1, 512:1024], pb[5][0:1, :]), m_v, 1,
           expect=V_GQ)
        if not ST2:
            return
        e.wait_ge(m_gm, GM_GQB)
        e.wait_ge(m_wk, W_FULL)
        for c in range(DC):
            ins = e.tensor_mul(wchunk(wB, c), wchunk(wB, c), gqb[:, :])
            if c == DC - 1:
                mk(ins, m_v, 1, expect=V_WKP)
        e.wait_ge(m_gm, GM_GQR)
        mk(e.tensor_mul(cskrow[:, :], cskrow[:, :], gqrow[:, :]), m_v, 1,
           expect=V_CSK)
        for i in range(NBLK):
            e.wait_ge(m_s, S_P2(i, 1))
            e.wait_ge(m_sd, SD_CSB2)
            if i >= 2:
                e.wait_ge(m_s, S_P2(i - 2, 2))
            else:
                e.wait_ge(m_gst2[i % 2], GSTW(NBLK - 2 + i))
            mk(e.scalar_tensor_tensor(qfix[i % 2][:, :], csb[:, :],
                                      c_all[:, i:i + 1], qs_sb[i % 3][:, :],
                                      op0=ALU.mult, op1=ALU.add), m_v, 1,
               expect=V_P2(i, 1))
            if i >= 1:
                p1norm(i - 1, vm=(S_P2(i - 1, 2), T_P2G,
                                  lambda j: V_P2(j, 2)))
        p1norm(NBLK - 1, vm=(S_P2(NBLK - 1, 2), T_P2G,
                             lambda j: V_P2(j, 2)))
        e.wait_ge(m_t, T_P2G(NBLK - 1))
        e.tensor_copy(grow[0:1, 0:512], pb[4][0:1, :])
        mk(e.tensor_copy(grow[0:1, 512:1024], pb[5][0:1, :]), m_v, 1,
           expect=V_GK)
        e.wait_ge(m_stc, STC_GKC)
        e.wait_ge(m_wr, W_FULL)
        for c in range(DC):
            ins = e.tensor_scalar(wchunk(wR, c), wchunk(wR, c),
                                  gksc[:, c:c + 1], None, op0=ALU.mult)
            if c == DC - 1:
                mk(ins, m_v, 1, expect=V_WRP)
        for u in range(16):
            m, jh = divmod(u, 2)
            e.wait_ge(m_t, T_WVR(u))
            mk(e.tensor_copy(wA[:, m * D + jh * 512:m * D + (jh + 1) * 512],
                             pb[u % 4][:, :]), m_v, 1, expect=V_WVR(u))
        e.wait_ge(m_t, T_CSV)
        e.tensor_copy(csrow[0:1, 0:512], pb[4][0:1, :])
        mk(e.tensor_copy(csrow[0:1, 512:1024], pb[5][0:1, :]), m_v, 1,
           expect=V_CSV)
        for i in range(NBLK):
            e.wait_ge(m_s, S_P3(i, 1))
            e.wait_ge(m_sd, SD_CSB3)
            if i >= 2:
                e.wait_ge(m_g, G_SC(i - 2))
            mk(e.scalar_tensor_tensor(u2_sb[i % 2][:, :], csb[:, :],
                                      c_all[:, i:i + 1], u1_sb[i % 2][:, :],
                                      op0=ALU.mult, op1=ALU.add), m_v, 1,
               expect=V_P3(i, 1))
            e.wait_ge(m_g, G_P3(i))
            mk(e.scalar_tensor_tensor(x1_t[i % 3][:, :], xtm_t[i % 3][:, :],
                                      1.0, u2_sb[i % 2][:, :], op0=ALU.mult,
                                      op1=ALU.add,
                                      accum_out=sx1_t[i % 3][:, :]), m_v, 1,
               expect=V_P3(i, 2))
            e.wait_ge(m_s, S_P3(i, 2))
            e.tensor_scalar_mul(mean_t[i % 3][:, :], sx1_t[i % 3][:, :], INV_D)
            e.tensor_scalar_mul(e2_t[i % 3][:, :], sx1q_t[i % 3][:, :], INV_D)
            e.tensor_mul(var_t[i % 3][:, :], mean_t[i % 3][:, :],
                         mean_t[i % 3][:, :])
            mk(e.tensor_sub(var_t[i % 3][:, :], e2_t[i % 3][:, :],
                            var_t[i % 3][:, :]), m_v, 1, expect=V_P3(i, 3))
            e.wait_ge(m_s, S_P3(i, 3))
            e.tensor_scalar(c1_t[i % 3][:, :], mean_t[i % 3][:, :],
                            s1_t[i % 3][:, :], -1.0, op0=ALU.mult, op1=ALU.mult)
            e.tensor_scalar(sp_t[i % 3][:, :], s1_t[i % 3][:, :], 1.0, None,
                            op0=ALU.add)
            mk(e.tensor_mul(c1s_t[i % 3][:, :], c1_t[i % 3][:, :],
                            sumwb[:, :]), m_v, 1, expect=V_P3(i, 4))
            e.wait_ge(m_g, G_SC(i))
            e.tensor_scalar(scw[i % 2][:, :], scw[i % 2][:, :], 1.0, None,
                            op0=ALU.mult, op1=ALU.add,
                            accum_out=sxw_t[i % 3][:, :])
            mk(e.tensor_scalar(sc_all[:, i:i + 1], sxw_t[i % 3][:, :],
                               s1_t[i % 3][:, :], c1s_t[i % 3][:, :],
                               op0=ALU.mult, op1=ALU.add), m_v, 1,
               expect=V_P3(i, 5))
        e.wait_ge(m_t, T_SCT)
        mk(e.tensor_copy(scT[:, :], pb6[0:32, 0:128]), m_v, 1, expect=V_SCT)
        if not ST3:
            return
        e.wait_ge(m_stc, STC_SCF)
        mk(e.max(out=candv[:, 0:8], in_=scfull[:, :]), m_v, 1, expect=V_TK(1))
        e.wait_ge(m_v, V_TK(1))
        mk(e.max_index(out=candi[:, 0:8], in_max=candv[:, 0:8],
                       in_values=scfull[:, :]), m_v, 1, expect=V_TK(2))
        e.wait_ge(m_v, V_TK(2))
        mk(e.match_replace(out=ws_sb[:, :], in_to_replace=candv[:, 0:8],
                           in_values=scfull[:, :], imm_value=NEG), m_v, 1,
           expect=V_TK(3))
        e.wait_ge(m_v, V_TK(3))
        mk(e.max(out=candv[:, 8:16], in_=ws_sb[:, :]), m_v, 1, expect=V_TK(4))
        e.wait_ge(m_v, V_TK(4))
        mk(e.max_index(out=candi[:, 8:16], in_max=candv[:, 8:16],
                       in_values=ws_sb[:, :]), m_v, 1, expect=V_TK(5))
        e.wait_ge(m_v, V_TK(5))
        e.tensor_copy(candif[:, :], candi[:, :])
        e.tensor_scalar(candif[:, :], candif[:, :], iota64_sb[:, :], None,
                        op0=ALU.add)
        mk(e.tensor_copy(candtu[:, :], candif[:, :]), m_v, 1, expect=V_TK(6))
        e.wait_ge(m_stc, STC_TOK)
        for j in range(MYCH):
            if j >= 2:
                e.wait_ge(m_gi, GI_RS(j - 2))
            e.tensor_scalar(cmp_scr[:, :], candrep[:, :], tchall[:, j:j + 1],
                            None, op0=ALU.is_gt, op1=ALU.add,
                            accum_out=rank_f[j % 2][:, :])
            mk(e.tensor_copy(rank_u[j % 2][:, :], rank_f[j % 2][:, :]), m_v,
               1, expect=V_RK(j))
        if not ST4:
            return
        for b in range(MYPB):
            e.wait_ge(m_s, S_RSQ(b))
            e.wait_ge(m_w12, W12_FULL)
            mk(e.scalar_tensor_tensor(h_sb[b][:, :], xsel[b][:, :],
                                      ssq_t[b][:, :], g1b[:, :],
                                      op0=ALU.mult, op1=ALU.mult), m_v, 1,
               expect=V_XS(b))
        for b in range(MYPB):
            e.wait_ge(m_t, T_FH(b))
            mk(e.tensor_copy(hfm[:, b * D:(b + 1) * D], pT[:, :]), m_v, 1,
               expect=V_FH(b))
        for b in range(MYPB):
            for j in range(6):
                e.wait_ge(m_t, T_QKV(b, j))
                mk(e.tensor_copy(qkv_sb[b][:, j * 512:(j + 1) * 512],
                                 pb[j % 2][:, :]), m_v, 1, expect=V_QKV(b, j))
        kfm4 = kfm.rearrange("p (hp kb c) -> p hp kb c", hp=8, kb=NPB)
        pT3 = pT.rearrange("p (hp c) -> p hp c", hp=8)
        for kb in range(NPB):
            e.wait_ge(m_t, T_KT(kb))
            mk(e.tensor_copy(kfm4[:, :, kb, :], pT3[:, :, :]), m_v, 1,
               expect=V_KT(kb))
        if not ST5:
            return
        qfm4 = qfm.rearrange("p (hp bx c) -> p hp bx c", hp=8, bx=MYPB)
        for b in range(MYPB):
            e.wait_ge(m_t, T_QT(b))
            mk(e.tensor_copy(qfm4[:, :, b, :], pT3[:, :, :]), m_v, 1,
               expect=V_QT(b))
        def att_evac(w):
            h2, b2 = divmod(w, 2)
            e.wait_ge(m_t, T_SUM(h2, b2))
            mk(e.tensor_copy(sums_sb[0:1, w * 128:(w + 1) * 128],
                             pb[w % 2][0:1, 0:128]),
               m_v, 1, expect=V_SUMC(h2, b2))
            e.wait_ge(m_t, T_O(h2, b2))
            mk(e.tensor_copy(o_sb[b2][:, h2 * 64:(h2 + 1) * 64],
                             pb[2 + w % 2][:, 0:64]),
               m_v, 1, expect=V_OE(h2, b2))

        for h in range(NH):
            for b in range(MYPB):
                w = 2 * h + b
                e.wait_ge(m_t, T_ATT(h, b))
                if w >= 3:
                    e.wait_ge(m_s, S_EXP((w - 3) // 2, (w - 3) % 2))
                mk(e.tensor_add(attm[w % 3][:, :],
                                [pb[4], pb[5], pb6][w % 3][:, :],
                                maskf_sb[:, b * 512:(b + 1) * 512]), m_v, 1,
                   expect=V_MSK(h, b))
                if w >= 2:
                    att_evac(w - 2)
        for w in range(2 * NH - 2, 2 * NH):
            att_evac(w)
        e.wait_ge(m_stc, STC_RQA)
        e.reciprocal(recqall[:, :], recqall[:, :])
        for b in range(MYPB):
            for h in range(NH):
                w = 2 * h + b
                ins = e.tensor_scalar(o_sb[b][:, h * 64:(h + 1) * 64],
                                      o_sb[b][:, h * 64:(h + 1) * 64],
                                      recqall[:, w:w + 1], None, op0=ALU.mult)
                if h == NH - 1:
                    mk(ins, m_v, 1, expect=V_ON(b))
        for b in range(MYPB):
            e.wait_ge(m_t, T_OT(b))
            mk(e.tensor_copy(ofm[:, b * D:(b + 1) * D], pT[:, :]), m_v, 1,
               expect=V_OT(b))
        for b in range(MYPB):
            e.wait_ge(m_t, T_XA(b))
            e.tensor_copy(xattn[b][:, 0:512], pb[0][:, :])
            mk(e.tensor_copy(xattn[b][:, 512:1024], pb[1][:, :]), m_v, 1,
               expect=V_XAE(b))
        for b in range(MYPB):
            e.wait_ge(m_s, S_XS(b))
            mk(e.tensor_add(s2_t[b][:, :], xattn[b][:, :], xself[b][:, :]),
               m_v, 1, expect=V_S2(b))
        if not ST6:
            return
        for b in range(MYPB):
            e.wait_ge(m_s, S_H2(b, 2))
            e.wait_ge(m_w12, W12_FULL)
            mk(e.scalar_tensor_tensor(h2_sb[b][:, :], s2_t[b][:, :],
                                      s2q_t[b][:, :], g2b[:, :],
                                      op0=ALU.mult, op1=ALU.mult), m_v, 1,
               expect=V_H2(b))
        for b in range(MYPB):
            e.wait_ge(m_t, T_H2T(b))
            mk(e.tensor_copy(h2fm[:, b * D:(b + 1) * D], pT[:, :]), m_v, 1,
               expect=V_H2T(b))
        for ft in range(NFT):
            for b in range(MYPB):
                fs = FT_SIZES[ft]
                e.wait_ge(m_s, S_SIL(ft, b))
                mk(e.tensor_mul(g_sb[:, 0:fs], asil[:, 0:fs],
                                pb[1][:, 0:fs]), m_v, 1, expect=V_SW(ft, b, 1))
                e.wait_ge(m_t, T_SW(ft, b, 3))
                ngc = (fs + 127) // 128
                mk(e.tensor_copy(gfm[:, 0:ngc * 128], pT[:, 0:ngc * 128]),
                   m_v, 1, expect=V_SW(ft, b, 2))
        for b in range(MYPB):
            e.wait_ge(m_t, T_SW(NFT - 1, b, 4))
            e.tensor_add(delta_my[b][:, 0:512], xattn[b][:, 0:512],
                         pb[2 + 2 * b][:, :])
            mk(e.tensor_add(delta_my[b][:, 512:1024], xattn[b][:, 512:1024],
                            pb[3 + 2 * b][:, :]), m_v, 1, expect=V_XFF(b))
        for b in range(NPB):
            e.wait_ge(m_stc, STC_IDXA(b))
            e.tensor_copy(idxf[b][:, :], idxp[b][:, :])
            e.tensor_scalar(idxf[b][:, :], idxf[b][:, :], cinfob[:, 0:1],
                            None, op0=ALU.subtract)
            e.tensor_scalar(idxm[b][:, :], idxf[b][:, :], 0.0, None,
                            op0=ALU.is_ge)
            e.tensor_scalar(idxm2[b][:, :], idxf[b][:, :], float(S), None,
                            op0=ALU.is_lt)
            e.tensor_mul(idxm[b][:, :], idxm[b][:, :], idxm2[b][:, :])
            e.tensor_mul(idxf[b][:, :], idxf[b][:, :], idxm[b][:, :])
            e.tensor_scalar(idxm[b][:, :], idxm[b][:, :], -100000.0, 100000.0,
                            op0=ALU.mult, op1=ALU.add)
            e.tensor_add(idxf[b][:, :], idxf[b][:, :], idxm[b][:, :])
            mk(e.tensor_copy(idxu[b][:, :], idxf[b][:, :]), m_v, 1,
               expect=V_FIL(b))

    @blk.tensor
    def _(e):
        e.wait_ge(m_hw, 4 * W16)
        tl = [None]

        def M(*a, **kw):
            tl[0] = e.matmul(*a, **kw)
            return tl[0]

        def T(*a, **kw):
            tl[0] = e.transpose(*a, **kw)
            return tl[0]

        for ct in range(NCT):
            e.wait_ge(m_gx, 16 * (ct + 1))
            if ct >= 1:
                e.wait_ge(m_v, V_P0(ct - 1))
            for c in range(DC):
                M(pb[4][0:1, :], ones_sb[:, :],
                         xfm[:, c * S + 512 * ct:c * S + 512 * (ct + 1)],
                         start=(c == 0), stop=(c == DC - 1))
            mk(tl[0], m_t, 1)
            e.wait_ge(m_s, S_P0(ct))
            for c in range(DC):
                M(pb[5][0:1, :], ones_sb[:, :],
                         sqs[:, c * 512:(c + 1) * 512],
                         start=(c == 0), stop=(c == DC - 1))
            mk(tl[0], m_t, 1, expect=T_P0(ct))
        e.wait_ge(m_wq, W_FULL)
        e.wait_ge(m_v, V_P0(NCT - 1))
        for half in range(2):
            for c in range(DC):
                M([pb[4], pb[5]][half][0:1, :], ones_sb[:, :],
                         wA[:, c * D + half * 512:c * D + (half + 1) * 512],
                         start=(c == 0), stop=(c == DC - 1))
        mk(tl[0], m_t, 1, expect=T_CSQ)
        e.wait_ge(m_wk, W_FULL)
        for half in range(2):
            for c in range(DC):
                M([pb[0], pb[1]][half][0:1, :], ones_sb[:, :],
                         wB[:, c * D + half * 512:c * D + (half + 1) * 512],
                         start=(c == 0), stop=(c == DC - 1))
        mk(tl[0], m_t, 1, expect=T_CSK)
        if not ST1:
            return
        e.wait_ge(m_v, V_CSQ)

        def gq_mm(i, pss, end):
            e.wait_ge(m_v, V_P1(i, 4) if not end else V_P2(i, 2))
            M(pss[0][0:1, :], recqt[i % 3][:, :], qa[i % 3][:, 0:512],
                     start=(i == 0), stop=(i == NBLK - 1))
            M(pss[1][0:1, :], recqt[i % 3][:, :], qa[i % 3][:, 512:1024],
                     start=(i == 0), stop=(i == NBLK - 1))

        for i in range(NBLK):
            if i >= 2:
                e.wait_ge(m_s, S_P1(i - 2, 2))
            for half in range(2):
                for c in range(DC):
                    M(pb[2 * (i % 2) + half][:, :],
                             xfm[:, c * S + 128 * i:c * S + 128 * (i + 1)],
                             wchunk(wA, c)[:, 512 * half:512 * (half + 1)],
                             start=(c == 0), stop=(c == DC - 1))
            mk(tl[0], m_t, 1, expect=T_P1Q(i))
            if i >= 3:
                gq_mm(i - 3, (pb[4], pb[5]), False)
                mk(tl[0], m_t, 1, expect=T_P1G(i - 3))
        for j in range(NBLK - 3, NBLK):
            gq_mm(j, (pb[4], pb[5]), False)
            mk(tl[0], m_t, 1, expect=T_P1G(j))
        if not ST2:
            return
        e.wait_ge(m_v, V_WKP)
        e.wait_ge(m_v, V_GQ)
        for i in range(NBLK):
            if i >= 2:
                e.wait_ge(m_s, S_P2(i - 2, 1))
            else:
                e.wait_ge(m_s, S_P1(NBLK - 2 + i, 2))
            for half in range(2):
                for c in range(DC):
                    M(pb[2 * (i % 2) + half][:, :],
                             xfm[:, c * S + 128 * i:c * S + 128 * (i + 1)],
                             wchunk(wB, c)[:, 512 * half:512 * (half + 1)],
                             start=(c == 0), stop=(c == DC - 1))
            mk(tl[0], m_t, 1, expect=T_P2Q(i))
            if i >= 3:
                gq_mm(i - 3, (pb[4], pb[5]), True)
                mk(tl[0], m_t, 1, expect=T_P2G(i - 3))
        for j in range(NBLK - 3, NBLK):
            gq_mm(j, (pb[4], pb[5]), True)
            mk(tl[0], m_t, 1, expect=T_P2G(j))
        e.wait_ge(m_wv, W_FULL)
        e.wait_ge(m_v, V_WRP)
        for u in range(16):
            m, jh = divmod(u, 2)
            if u >= 4:
                e.wait_ge(m_v, V_WVR(u - 4))
            else:
                e.wait_ge(m_s, S_P2(NBLK - 4 + u, 1))
            for c in range(DC):
                M(pb[u % 4][:, :],
                         wchunk(wVT, c)[:, m * 128:(m + 1) * 128],
                         wchunk(wR, c)[:, jh * 512:(jh + 1) * 512],
                         start=(c == 0), stop=(c == DC - 1))
            mk(tl[0], m_t, 1, expect=T_WVR(u))
        e.wait_ge(m_v, V_GK)
        e.wait_ge(m_v, V_WVR(15))
        for half in range(2):
            for c in range(DC):
                M([pb[4], pb[5]][half][0:1, :], ones_sb[:, :],
                         wA[:, c * D + half * 512:c * D + (half + 1) * 512],
                         start=(c == 0), stop=(c == DC - 1))
        mk(tl[0], m_t, 1, expect=T_CSV)
        e.wait_ge(m_v, V_CSV)
        for i in range(NBLK):
            if i >= 3:
                e.wait_ge(m_s, S_P3(i - 3, 1))
            elif i == 0:
                e.wait_ge(m_v, V_WVR(13))
            elif i == 1:
                e.wait_ge(m_v, V_WVR(15))
            for half in range(2):
                for c in range(DC):
                    M(pb[2 * (i % 3) + half][:, :],
                             xfm[:, c * S + 128 * i:c * S + 128 * (i + 1)],
                             wchunk(wA, c)[:, 512 * half:512 * (half + 1)],
                             start=(c == 0), stop=(c == DC - 1))
            mk(tl[0], m_t, 1, expect=T_P3Q(i))
        e.wait_ge(m_v, V_P3(NBLK - 1, 5))
        mk(T(pb6[0:32, 0:128], sc_all[:, :], id32_sb[:, :]), m_t, 1,
           expect=T_SCT)
        if not ST4:
            return
        for b in range(MYPB):
            e.wait_ge(m_v, V_XS(b))
            if b >= 1:
                e.wait_ge(m_v, V_FH(b - 1))
            for c in range(DC):
                T(pT[:, c * 128:(c + 1) * 128],
                            h_sb[b][:, c * 128:(c + 1) * 128], id_sb[:, :])
            mk(tl[0], m_t, 1, expect=T_FH(b))
        for b in range(MYPB):
            for j in range(6):
                e.wait_ge(m_wqkv, 128 * (j // 2 + 1))
                wsrc = [wqkvq, wqkvk, wqkvv][j // 2]
                jj = j % 2
                if 6 * b + j >= 2:
                    pi = 6 * b + j - 2
                    e.wait_ge(m_v, V_QKV(pi // 6, pi % 6))
                for c in range(DC):
                    M(pb[j % 2][:, :],
                             hfm[:, b * D + c * 128:b * D + (c + 1) * 128],
                             wchunk(wsrc, c)[:, jj * 512:(jj + 1) * 512],
                             start=(c == 0), stop=(c == DC - 1))
                mk(tl[0], m_t, 1, expect=T_QKV(b, j))
        for kb in range(NPB):
            e.wait_ge(m_stc, STC_KV(kb))
            if kb >= 1:
                e.wait_ge(m_v, V_KT(kb - 1))
            else:
                e.wait_ge(m_v, V_FH(MYPB - 1))
            for hp in range(8):
                T(pT[:, hp * 128:(hp + 1) * 128],
                            kvall[kb][:, hp * 128:(hp + 1) * 128], id_sb[:, :])
            mk(tl[0], m_t, 1, expect=T_KT(kb))
        if not ST5:
            return
        for b in range(MYPB):
            e.wait_ge(m_v, V_QKV(b, 1))
            if b >= 1:
                e.wait_ge(m_v, V_QT(b - 1))
            else:
                e.wait_ge(m_v, V_KT(NPB - 1))
            for hp in range(8):
                T(pT[:, hp * 128:(hp + 1) * 128],
                            qkv_sb[b][:, hp * 128:(hp + 1) * 128], id_sb[:, :])
            mk(tl[0], m_t, 1, expect=T_QT(b))
        abank = [pb[4], pb[5], pb6]

        def sum_o(w):
            h2, b2 = divmod(w, 2)
            e.wait_ge(m_s, S_EXP(h2, b2))
            if w >= 2:
                e.wait_ge(m_v, V_SUMC((w - 2) // 2, (w - 2) % 2))
            for kb in range(NPB):
                M(pb[w % 2][0:1, 0:128],
                         ones_sb[:, :],
                         probsT[w % 3][:, kb * 128:(kb + 1) * 128],
                         start=(kb == 0), stop=(kb == NPB - 1))
            mk(tl[0], m_t, 1, expect=T_SUM(h2, b2))
            if w >= 2:
                e.wait_ge(m_v, V_OE((w - 2) // 2, (w - 2) % 2))
            for kb in range(NPB):
                M(pb[2 + w % 2][:, 0:64],
                         probsT[w % 3][:, kb * 128:(kb + 1) * 128],
                         kvall[kb][:, D + h2 * 64:D + (h2 + 1) * 64],
                         start=(kb == 0), stop=(kb == NPB - 1))
            mk(tl[0], m_t, 1, expect=T_O(h2, b2))

        for h in range(NH):
            hp, hh = divmod(h, 2)
            for b in range(MYPB):
                w = 2 * h + b
                if w >= 3:
                    e.wait_ge(m_v, V_MSK((w - 3) // 2, (w - 3) % 2))
                if w == 0:
                    e.wait_ge(m_v, V_QT(MYPB - 1))
                for kb in range(NPB):
                    M(abank[w % 3][:, kb * 128:(kb + 1) * 128],
                             kfm4_slice(kfm, hp, kb, hh),
                             qfm4_slice(qfm, hp, b, hh),
                             start=True, stop=True)
                mk(tl[0], m_t, 1, expect=T_ATT(h, b))
                if w >= 2:
                    sum_o(w - 2)
        for w in range(2 * NH - 2, 2 * NH):
            sum_o(w)
        for b in range(MYPB):
            e.wait_ge(m_v, V_ON(b))
            if b >= 1:
                e.wait_ge(m_v, V_OT(b - 1))
            else:
                e.wait_ge(m_v, V_QT(MYPB - 1))
            for hp in range(8):
                T(pT[:, hp * 128:(hp + 1) * 128],
                            o_sb[b][:, hp * 128:(hp + 1) * 128], id_sb[:, :])
            mk(tl[0], m_t, 1, expect=T_OT(b))
        e.wait_ge(m_wout, W_FULL)
        for b in range(MYPB):
            e.wait_ge(m_v, V_OT(b))
            if b >= 1:
                e.wait_ge(m_v, V_XAE(b - 1))
            for half in range(2):
                for c in range(DC):
                    M(pb[half][:, :],
                             ofm[:, b * D + c * 128:b * D + (c + 1) * 128],
                             wchunk(wout_sb, c)[:, half * 512:(half + 1) * 512],
                             start=(c == 0), stop=(c == DC - 1))
            mk(tl[0], m_t, 1, expect=T_XA(b))
        if not ST6:
            return
        for b in range(MYPB):
            e.wait_ge(m_v, V_H2(b))
            if b >= 1:
                e.wait_ge(m_v, V_H2T(b - 1))
            else:
                e.wait_ge(m_v, V_OT(MYPB - 1))
            for c in range(DC):
                T(pT[:, c * 128:(c + 1) * 128],
                            h2_sb[b][:, c * 128:(c + 1) * 128], id_sb[:, :])
            mk(tl[0], m_t, 1, expect=T_H2T(b))
        for ft in range(NFT):
            for b in range(MYPB):
                fs = FT_SIZES[ft]
                u = ft * MYPB + b
                jp = ft % 2
                e.wait_ge(m_wf[ft % 2], WF_AT(ft))
                if u >= 1:
                    pi = u - 1
                    e.wait_ge(m_s, S_SIL(pi // MYPB, pi % MYPB))
                else:
                    e.wait_ge(m_v, V_XAE(MYPB - 1))
                    e.wait_ge(m_v, V_MSK(NH - 1, MYPB - 1))
                for c in range(DC):
                    M(pb[0][:, 0:fs],
                             h2fm[:, b * D + c * 128:b * D + (c + 1) * 128],
                             wchunk(wf1[jp], c, 512)[:, 0:fs],
                             start=(c == 0), stop=(c == DC - 1))
                mk(tl[0], m_t, 1, expect=T_SW(ft, b, 1))
                if u >= 1:
                    pi = u - 1
                    e.wait_ge(m_v, V_SW(pi // MYPB, pi % MYPB, 1))
                for c in range(DC):
                    M(pb[1][:, 0:fs],
                             h2fm[:, b * D + c * 128:b * D + (c + 1) * 128],
                             wchunk(wf2[jp], c, 512)[:, 0:fs],
                             start=(c == 0), stop=(c == DC - 1))
                mk(tl[0], m_t, 1, expect=T_SW(ft, b, 2))
                ngc = (fs + 127) // 128
                e.wait_ge(m_v, V_SW(ft, b, 1))
                if u >= 1:
                    pi = u - 1
                    e.wait_ge(m_v, V_SW(pi // MYPB, pi % MYPB, 2))
                else:
                    e.wait_ge(m_v, V_H2T(MYPB - 1))
                for gc in range(ngc):
                    gw = min(128, fs - gc * 128)
                    T(pT[0:gw, gc * 128:gc * 128 + 128],
                                g_sb[:, gc * 128:gc * 128 + gw], id_sb[:, :])
                mk(tl[0], m_t, 1, expect=T_SW(ft, b, 3))
                e.wait_ge(m_v, V_SW(ft, b, 2))
                for gc in range(ngc):
                    gw = min(128, fs - gc * 128)
                    for half in range(2):
                        M(pb[2 + 2 * b + half][:, :],
                                 gfm[0:gw, gc * 128:gc * 128 + 128],
                                 wf3[jp][0:gw, gc * D + half * 512:gc * D + (half + 1) * 512],
                                 start=(ft == 0 and gc == 0),
                                 stop=(ft == NFT - 1 and gc == ngc - 1))
                mk(tl[0], m_t, 1, expect=T_SW(ft, b, 4))

    @blk.gpsimd
    def _(e):
        e = ChainProxy(e, m_gx2)
        xv = x_fm.rearrange("(c p) s -> p c s", p=128)
        for ct in range(NCT):
            mk(e.dma_start(out=xfm.rearrange("p (c s) -> p c s", c=DC)[:, :, 512 * ct:512 * (ct + 1)],
                           in_=xv[:, :, 512 * ct:512 * (ct + 1)]),
               m_gx, W16)
            e.wait_ge(m_gx, 16 * (ct + 1))

        def wload(dst, src, semh):
            sv = src.rearrange("(c p) n -> p c n", p=128)
            for c in range(DC):
                mk(e.dma_start(out=wchunk(dst, c, sv.shape[2]),
                               in_=sv[:, c, :]), semh, W16)

        wload(wA, Wq_d, m_wq)
        e.wait_ge(m_hw, HW_CONST)
        mk(e.collective_compute("AllReduce", ALU.add, replica_groups=rg,
                                ins=[cc_w_i[:, :]], outs=[cc_w_o[:, :]]),
           m_cc, 1, expect=CC_WARM)
        wload(wB, Wk_d, m_wk)
        wload(wVT, WvT_d, m_wv)
        wload(wR, Wr_d, m_wr)
        mk(e.dma_start(out=g1b[:, :], in_=g1_d[0:1, :].to_broadcast([128, D])),
           m_w12, W16)
        mk(e.dma_start(out=g2b[:, :], in_=g2_d[0:1, :].to_broadcast([128, D])),
           m_w12, W16, expect=W12_FULL)
        if not ST1:
            return
        for i in range(NBLK):
            e.wait_ge(m_v, V_P1(i, 3))
            mk(e.dma_start(out=q_dram[128 * i:128 * (i + 1), :],
                           in_=qfix[i % 2][:, :]), m_gst2[i % 2], W16,
               expect=GSTW(i))
            if i < 16:
                mk(e.dma_start(out=wqkvT_bf[64 * i:64 * (i + 1), :],
                               in_=WqkvT_d[64 * i:64 * (i + 1), :]),
                   m_pc, W16)
            elif i < 24:
                j = i - 16
                if j == 0:
                    e.wait_ge(m_pc, 16 * 16)
                mk(e.dma_start(out=woutT_bf[128 * j:128 * (j + 1), :],
                               in_=WoutT_d[128 * j:128 * (j + 1), :]),
                   m_pc, W16)
        if not ST2 and NO_AR1:
            return
        e.wait_ge(m_sd, SD_GQ)
        mk(e.collective_compute("AllReduce", ALU.add, replica_groups=rg,
                                ins=[cc_gq_i[:, :]], outs=[cc_gq_o[:, :]]),
           m_cc, 1, expect=CC_GQ)
        if not ST2:
            return
        e.wait_ge(m_cc, CC_GQ)
        mk(e.dma_start(out=gqb[:, :],
                       in_=cc_gq_o[0:1, :].to_broadcast([128, D])), m_gm, W16,
           expect=GM_GQB)
        e.wait_ge(m_gm, GM_GQB)
        mk(e.dma_start(out=gqrow[:, :], in_=cc_gq_o[:, :]), m_gm, W16,
           expect=GM_GQR)
        e.wait_ge(m_sd, SD_GK)
        mk(e.collective_compute("AllReduce", ALU.add, replica_groups=rg,
                                ins=[cc_gk_i[:, :]], outs=[cc_gk_o[:, :]]),
           m_cc, 1, expect=CC_GK)
        for i in range(NBLK):
            mk(e.dma_start(out=qf32[i % 2][:, :],
                           in_=q_dram[128 * i:128 * (i + 1), :]), m_gq3, W16,
               expect=GQ3(i))
            e.wait_ge(m_xt[i % 2], XTW(i))
            e.wait_ge(m_gq3, GQ3(i))
            mk(e.tensor_tensor(xtm_t[i % 3][:, :], xtm_t[i % 3][:, :],
                               qf32[i % 2][:, :], ALU.add), m_g, 1,
               expect=G_P3(i))
            e.wait_ge(m_v, V_P3(i, 2))
            mk(e.tensor_tensor(scw[i % 2][:, :], x1_t[i % 3][:, :],
                               wrb[:, :], ALU.mult), m_g, 1, expect=G_SC(i))
        if not ST3:
            return
        e.wait_ge(m_v, V_MZ)
        par_g = e.value_load(cinfu_sb[0:1, 3:4])
        ordz = ordidx_sh.rearrange("(a r) b -> a (r b)", a=2)[bass.ts(par_g, 1), :]
        mk(e.dma_start(out=ordz, in_=zidx[0:2, :]), m_gi, W16, expect=GI_OZ)
        e.wait_ge(m_gi, GI_OZ)
        e.wait_ge(m_sd, SD_SC)
        e.wait_ge(m_lncp, 64)
        e.wait_ge(m_hb2[0], HB_FULL)
        e.wait_ge(m_hb2[1], HB_FULL)
        mk(e.collective_compute("AllReduce", ALU.add, replica_groups=rg,
                                ins=[cc_sc_i[:, :]], outs=[cc_sc_o[:, :]]),
           m_cc, 1, expect=CC_SC)
        e.wait_ge(m_cc, CC_SC)
        for j in range(MYCH):
            e.wait_ge(m_v, V_RK(j))
            mk(e.indirect_dma_start(
                out=ordidx_sh[:, :],
                out_offset=IndirectOffsetOnAxis(rank_u[j % 2][:, :], 0),
                in_=tokall[:, j:j + 1], in_offset=None,
                bounds_check=CAP - 1, oob_is_err=False), m_gi, W16,
               expect=GI_RS(j))
            e.wait_ge(m_gi, GI_RS(j))
        mk(e.collective_compute("AllReduce", ALU.add, replica_groups=rg,
                                ins=[cc_b_i[:, :]], outs=[cc_b_o[:, :]]),
           m_cc, 1, expect=CC_ORD)
        if not ST4:
            return
        e.wait_ge(m_cc, CC_ORD)
        for b in range(MYPB):
            e.wait_ge(m_stc, STC_IDXP(b))
            mk(e.indirect_dma_start(
                out=xsel[b][:, :], out_offset=None,
                in_=ln1_sh[:, :],
                in_offset=IndirectOffsetOnAxis(idxp[b][:, :], 0),
                bounds_check=2 * S - 1, oob_is_err=False), m_gi, W16,
               expect=GI_GA(b))
            e.wait_ge(m_gi, GI_GA(b))
        e.wait_ge(m_sd, SD_KV_END)
        mk(e.collective_compute("AllReduce", ALU.add, replica_groups=rg,
                                ins=[cc_b_i[:, :]], outs=[cc_b_o[:, :]]),
           m_cc, 1, expect=CC_KV)
        if not ST6:
            return
        w1v = W1T_d.rearrange("(c p) n -> p c n", p=128)
        w2v = W2T_d.rearrange("(c p) n -> p c n", p=128)
        for ft in range(NFT):
            fs = FT_SIZES[ft]
            fo = FT_OFF[ft]
            jp = ft % 2
            if ft >= 2:
                e.wait_ge(m_t, T_SW(ft - 2, MYPB - 1, 4))
            else:
                e.wait_ge(m_t, T_XA(MYPB - 1))
            for c in range(DC):
                mk(e.dma_start(out=wchunk(wf1[jp], c, 512)[:, 0:fs],
                               in_=w1v[:, c, fo:fo + fs]), m_wf[jp], W16)
            for c in range(DC):
                mk(e.dma_start(out=wchunk(wf2[jp], c, 512)[:, 0:fs],
                               in_=w2v[:, c, fo:fo + fs]), m_wf[jp], W16)
            ngc = (fs + 127) // 128
            for gc in range(ngc):
                gw_ = min(128, fs - gc * 128)
                mk(e.dma_start(out=wf3[jp][0:gw_, gc * D:(gc + 1) * D],
                               in_=W3T_d[fo + gc * 128:fo + gc * 128 + gw_, :]),
                   m_wf[jp], W16)
            assert cnt[m_wf[jp].name] == WF_AT(ft), (cnt[m_wf[jp].name], WF_AT(ft))
        e.wait_ge(m_sd, SD_DEL_END)
        mk(e.collective_compute("AllReduce", ALU.add, replica_groups=rg,
                                ins=[cc_b_i[:, :]], outs=[cc_b_o[:, :]]),
           m_cc, 1, expect=CC_DEL)
        for b in range(NPB):
            e.wait_ge(m_v, V_FIL(b))
            e.wait_ge(m_stc, STC_DEL(b))
            e.wait_ge(m_hb2[0], HB_FULL)
            e.wait_ge(m_hb2[1], HB_FULL)
            mk(e.indirect_dma_start(
                out=out[:, :],
                out_offset=IndirectOffsetOnAxis(idxu[b][:, :], 0),
                in_=delta_ld[b][:, :], in_offset=None,
                bounds_check=S - 1, oob_is_err=False,
                compute_op=ALU.add), m_gi, W16, expect=GI_FS(b))

    esF.close()
    es0.close()
    return nc


def kfm4_slice(kfm, hp, kb, hh):
    c0 = (hp * NPB + kb) * 128
    return kfm[hh * 64:(hh + 1) * 64, c0:c0 + 128]


def qfm4_slice(qfm, hp, b, hh):
    c0 = (hp * MYPB + b) * 128
    return qfm[hh * 64:(hh + 1) * 64, c0:c0 + 128]


_NC_CACHE = {}
_LAST_RES = None


def kernel(L0_attn_x, L0_attn_Wq, L0_attn_Wk, L0_attn_Wv, L0_attn_Wr,
           L0_ffn_w_router, L0_ffn_W_qkv, L0_ffn_W_out, L0_ffn_g1, L0_ffn_g2,
           L0_ffn_W1, L0_ffn_W2, L0_ffn_W3, stage=99):
    x = np.asarray(L0_attn_x)
    B = x.shape[0]
    n = 8

    f32 = np.float32
    cA = np.ascontiguousarray
    Wq = cA(L0_attn_Wq, dtype=f32)
    Wk = cA(L0_attn_Wk, dtype=f32)
    WvT = cA(np.asarray(L0_attn_Wv).T, dtype=f32)
    Wr = cA(L0_attn_Wr, dtype=f32)
    WqkvT = cA(np.asarray(L0_ffn_W_qkv).T, dtype=f32)
    WoutT = cA(np.asarray(L0_ffn_W_out).T, dtype=f32)
    W1T = cA(np.asarray(L0_ffn_W1).T, dtype=f32)
    W2T = cA(np.asarray(L0_ffn_W2).T, dtype=f32)
    W3T = cA(np.asarray(L0_ffn_W3).T, dtype=f32)
    wr_row = cA(np.asarray(L0_ffn_w_router).reshape(1, D), dtype=f32)
    g1 = cA(np.asarray(L0_ffn_g1).reshape(1, D), dtype=f32)
    g2 = cA(np.asarray(L0_ffn_g2).reshape(1, D), dtype=f32)

    ident_bf = np.eye(128, dtype=ml_dtypes.bfloat16)
    ident_f32 = np.eye(128, dtype=f32)
    ones_bf = np.ones((128, 1), dtype=ml_dtypes.bfloat16)
    iota64 = (np.arange(128, dtype=f32) * 64.0).reshape(128, 1)

    key = (stage, DBG)
    if key not in _NC_CACHE:
        _NC_CACHE[key] = build_nc(stage)
    nc = _NC_CACHE[key]

    in_maps = []
    for core in range(n):
        batch = core // 2
        parity = core % 2
        lo = parity * S
        xs = cA(x[batch, lo:lo + S, :], dtype=f32)
        xfm_s = cA(xs.T, dtype=f32)
        mask = np.empty((128, MYPB * NPB * 128), dtype=ml_dtypes.bfloat16)
        for b in range(MYPB):
            qb_g = parity * MYPB + b
            for kb in range(NPB):
                kk = kb * 128 + np.arange(128)[:, None]
                qq = qb_g * 128 + np.arange(128)[None, :]
                blkm = np.where(kk <= qq, 0.0, NEG).astype(f32)
                mask[:, b * NPB * 128 + kb * 128:(b) * NPB * 128 + (kb + 1) * 128] = blkm
        cinfo_u = np.zeros((1, 8), np.uint32)
        cinfo_u[0, 0] = parity * NBLK
        cinfo_u[0, 1] = parity * MYPB
        cinfo_u[0, 2] = parity * MYCH
        cinfo_u[0, 3] = parity
        cinfo_f = np.zeros((1, 8), f32)
        cinfo_f[0, 0] = float(lo)
        in_maps.append({
            "x_tm": xs, "x_fm": xfm_s,
            "Wq": Wq, "Wk": Wk, "WvT": WvT, "Wr": Wr,
            "WqkvT": WqkvT, "WoutT": WoutT, "W1T": W1T, "W2T": W2T,
            "W3T": W3T, "w_router": wr_row, "g1": g1, "g2": g2,
            "ident_bf": ident_bf, "ident_f32": ident_f32, "ones_bf": ones_bf,
            "iota64": iota64, "maskffn": mask,
            "cinfo_u": cinfo_u, "cinfo_f": cinfo_f,
        })

    res = run_bass_kernel_spmd(nc, in_maps, core_ids=list(range(n)))
    global _LAST_RES
    _LAST_RES = res
    outp = np.empty((B, 2 * S, D), np.float32)
    for core in range(n):
        batch = core // 2
        parity = core % 2
        lo = parity * S
        outp[batch, lo:lo + S, :] = res.results[core]["out"]
    return outp


# version 17
# speedup vs baseline: 1.1142x; 1.1142x over previous
import numpy as np
import ml_dtypes
from contextlib import ExitStack

import concourse.bass as bass
import concourse.mybir as mybir
from concourse.bass_utils import run_bass_kernel_spmd
from concourse.bass_core_rust import IndirectOffsetOnAxis

F32 = mybir.dt.float32
BF16 = mybir.dt.bfloat16
U32 = mybir.dt.uint32
AF = mybir.ActivationFunctionType
ALU = mybir.AluOpType
AXX = mybir.AxisListType.X

D = 1024
DC = 8
S = 4096
NBLK = 32
NCT = 8
CAP = 512
DFF = 2730
NH = 16
HD = 64
SCALE = 1024 ** -0.5
EPS_LN = 1e-5
EPS_RMS = 1e-6
NEG = -1.0e30
CANDK = 16
NCAND = 128 * CANDK
NCH = NCAND // 128
MYCH = NCH // 2
NPB = CAP // 128
MYPB = NPB // 2
FT_SIZES = [512, 512, 512, 512, 512, 170]
FT_OFF = [0, 512, 1024, 1536, 2048, 2560]
NFT = len(FT_SIZES)
INV_D = 1.0 / D
W16 = 16
DBG = False
NO_AR1 = False


def build_nc(stage=99):
    nc = bass.Bass("TRN2", target_bir_lowering=False, debug=False)
    ST1 = stage >= 1
    ST2 = stage >= 2
    ST3 = stage >= 3
    ST4 = stage >= 4
    ST5 = stage >= 5
    ST6 = stage >= 6

    x_tm = nc.dram_tensor("x_tm", [S, D], F32, kind="ExternalInput").ap()
    x_fm = nc.dram_tensor("x_fm", [D, S], F32, kind="ExternalInput").ap()
    Wq_d = nc.dram_tensor("Wq", [D, D], F32, kind="ExternalInput").ap()
    Wk_d = nc.dram_tensor("Wk", [D, D], F32, kind="ExternalInput").ap()
    WvT_d = nc.dram_tensor("WvT", [D, D], F32, kind="ExternalInput").ap()
    Wr_d = nc.dram_tensor("Wr", [D, D], F32, kind="ExternalInput").ap()
    WqkvT_d = nc.dram_tensor("WqkvT", [D, 3 * D], F32, kind="ExternalInput").ap()
    WoutT_d = nc.dram_tensor("WoutT", [D, D], F32, kind="ExternalInput").ap()
    W1T_d = nc.dram_tensor("W1T", [D, DFF], F32, kind="ExternalInput").ap()
    W2T_d = nc.dram_tensor("W2T", [D, DFF], F32, kind="ExternalInput").ap()
    W3T_d = nc.dram_tensor("W3T", [DFF, D], F32, kind="ExternalInput").ap()
    wrout_d = nc.dram_tensor("w_router", [1, D], F32, kind="ExternalInput").ap()
    g1_d = nc.dram_tensor("g1", [1, D], F32, kind="ExternalInput").ap()
    g2_d = nc.dram_tensor("g2", [1, D], F32, kind="ExternalInput").ap()
    ident_d = nc.dram_tensor("ident_bf", [128, 128], BF16, kind="ExternalInput").ap()
    idf32_d = nc.dram_tensor("ident_f32", [128, 128], F32, kind="ExternalInput").ap()
    ones_d = nc.dram_tensor("ones_bf", [128, 1], BF16, kind="ExternalInput").ap()
    iota64_d = nc.dram_tensor("iota64", [128, 1], F32, kind="ExternalInput").ap()
    maskf_d = nc.dram_tensor("maskffn", [128, MYPB * NPB * 128], BF16, kind="ExternalInput").ap()
    cinfu_d = nc.dram_tensor("cinfo_u", [1, 8], U32, kind="ExternalInput").ap()
    cinff_d = nc.dram_tensor("cinfo_f", [1, 8], F32, kind="ExternalInput").ap()

    out = nc.dram_tensor("out", [S, D], F32, kind="ExternalOutput").ap()
    if DBG:
        d_gq = nc.dram_tensor("d_gq", [1, D], F32, kind="ExternalOutput").ap()
        d_gk = nc.dram_tensor("d_gk", [1, D], F32, kind="ExternalOutput").ap()
        d_sc = nc.dram_tensor("d_sc", [1, 2 * S], F32, kind="ExternalOutput").ap()
        d_oi = nc.dram_tensor("d_oi", [CAP, 1], U32, kind="ExternalOutput").ap()
        d_q = nc.dram_tensor("d_q", [S, D], BF16, kind="ExternalOutput").ap()

    q_dram = nc.dram_tensor("q_dram", [S, D], BF16, kind="Internal").ap()
    sx_dram = nc.dram_tensor("sx_dram", [NBLK, 128], F32, kind="Internal").ap()
    sq_dram = nc.dram_tensor("sq_dram", [NBLK, 128], F32, kind="Internal").ap()
    csq_dram = nc.dram_tensor("csq_dram", [1, D], F32, kind="Internal").ap()
    csk_dram = nc.dram_tensor("csk_dram", [1, D], F32, kind="Internal").ap()
    csv_dram = nc.dram_tensor("csv_dram", [1, D], F32, kind="Internal").ap()
    sumw_dram = nc.dram_tensor("sumw_dram", [1, 1], F32, kind="Internal").ap()
    candv_dram = nc.dram_tensor("candv_dram", [1, NCAND], F32, kind="Internal").ap()
    candt_dram = nc.dram_tensor("candt_dram", [1, NCAND], U32, kind="Internal").ap()
    sums_dram = nc.dram_tensor("sums_dram", [2, NH * MYPB * 128], F32, kind="Internal").ap()
    ln1_loc = nc.dram_tensor("ln1_loc", [S, D], BF16, kind="Internal").ap()
    wqkvT_bf = nc.dram_tensor("wqkvT_bf", [D, 3 * D], BF16, kind="Internal").ap()
    woutT_bf = nc.dram_tensor("woutT_bf", [D, D], BF16, kind="Internal").ap()

    cc_w_i = nc.dram_tensor("cc_w_i", [1, 8], F32, kind="Internal").ap()
    cc_w_o = nc.dram_tensor("cc_w_o", [1, 8], F32, kind="Internal").ap()
    cc_gq_i = nc.dram_tensor("cc_gq_i", [1, D], F32, kind="Internal").ap()
    cc_gq_o = nc.dram_tensor("cc_gq_o", [1, D], F32, kind="Internal").ap()
    cc_gk_i = nc.dram_tensor("cc_gk_i", [1, D], F32, kind="Internal").ap()
    cc_gk_o = nc.dram_tensor("cc_gk_o", [1, D], F32, kind="Internal").ap()
    cc_sc_i = nc.dram_tensor("cc_sc_i", [1, 2 * S], F32, kind="Internal").ap()
    cc_sc_o = nc.dram_tensor("cc_sc_o", [1, 2 * S], F32, kind="Internal").ap()
    cc_b_i = nc.dram_tensor("cc_b_i", [1, 8], F32, kind="Internal").ap()
    cc_b_o = nc.dram_tensor("cc_b_o", [1, 8], F32, kind="Internal").ap()

    ln1_sh = nc.dram_tensor("ln1_sh", [2 * S, D], BF16, kind="Internal",
                            addr_space="Shared").ap()
    ordidx_sh = nc.dram_tensor("ordidx_sh", [CAP, 1], U32, kind="Internal",
                               addr_space="Shared").ap()
    kv_sh = nc.dram_tensor("kv_sh", [CAP, 2 * D], BF16, kind="Internal",
                           addr_space="Shared").ap()
    delta_sh = nc.dram_tensor("delta_sh", [CAP, D], F32, kind="Internal",
                              addr_space="Shared").ap()

    rg = [[0, 1], [2, 3], [4, 5], [6, 7]]

    es0 = ExitStack()
    esA = ExitStack()
    esA2 = ExitStack()
    esP3 = ExitStack()
    esT = ExitStack()
    esF = ExitStack()
    esF2 = ExitStack()
    esW = ExitStack()

    def sbt(stk, name, shape, dt):
        return stk.enter_context(nc.sbuf_tensor(name, shape, dt))

    sem = lambda name: es0.enter_context(nc.semaphore(name))

    id_sb = sbt(es0, "id_sb", [128, 128], BF16)
    id32_sb = sbt(es0, "id32_sb", [128, 128], F32)
    ones_sb = sbt(es0, "ones_sb", [128, 1], BF16)
    iota64_sb = sbt(es0, "iota64_sb", [128, 1], F32)
    maskf_sb = sbt(es0, "maskf_sb", [128, MYPB * NPB * 128], BF16)
    cinfu_sb = sbt(es0, "cinfu_sb", [1, 8], U32)
    cinfob = sbt(es0, "cinfob", [128, 8], F32)
    wrb = sbt(es0, "wrb", [128, D], F32)
    sumwb = sbt(es0, "sumwb", [128, 1], F32)
    g1b = sbt(es0, "g1b", [128, D], BF16)
    g2b = sbt(es0, "g2b", [128, D], BF16)
    sumw_row = sbt(es0, "sumw_row", [1, 1], F32)
    zrow = sbt(es0, "zrow", [128, 64], F32)
    zidx = sbt(es0, "zidx", [4, 128], U32)
    epsln_c = sbt(es0, "epsln_c", [128, 1], F32)
    epsrms_c = sbt(es0, "epsrms_c", [128, 1], F32)
    csrow = sbt(es0, "csrow", [1, D], F32)
    cskrow = sbt(es0, "cskrow", [1, D], F32)
    gqrow = sbt(es0, "gqrow", [1, D], F32)
    grow = gqrow
    gqb = sbt(es0, "gqb", [128, D], BF16)
    gksc = sbt(es0, "gksc", [128, DC], F32)
    s_all = sbt(es0, "s_all", [128, NBLK], F32)
    c_all = sbt(es0, "c_all", [128, NBLK], F32)
    sc_all = sbt(es0, "sc_all", [128, NBLK], F32)
    scT = sbt(es0, "scT", [32, 128], F32)
    sxrow = sbt(es0, "sxrow", [1, 512], F32)
    sqrow = sbt(es0, "sqrow", [1, 512], F32)
    mean_t = [sbt(es0, f"mean_t{j}", [128, 1], F32) for j in range(3)]
    e2_t = [sbt(es0, f"e2_t{j}", [128, 1], F32) for j in range(3)]
    var_t = [sbt(es0, f"var_t{j}", [128, 1], F32) for j in range(3)]
    std_t = [sbt(es0, f"std_t{j}", [128, 1], F32) for j in range(3)]
    sume = [sbt(es0, f"sume{j}", [128, 1], F32) for j in range(2)]
    recip_t = [sbt(es0, f"recip{j}", [128, 1], F32) for j in range(2)]
    recqt = [sbt(es0, f"recqt{j}", [128, 1], BF16) for j in range(3)]
    sx_bl = [sbt(es0, f"sx_bl{j}", [128, 1], F32) for j in range(2)]
    sq_bl = [sbt(es0, f"sq_bl{j}", [128, 1], F32) for j in range(2)]
    sxw_t = [sbt(es0, f"sxw_t{j}", [128, 1], F32) for j in range(3)]
    sx1_t = [sbt(es0, f"sx1_t{j}", [128, 1], F32) for j in range(3)]
    sx1q_t = [sbt(es0, f"sx1q_t{j}", [128, 1], F32) for j in range(3)]
    s1_t = [sbt(es0, f"s1_t{j}", [128, 1], F32) for j in range(3)]
    c1_t = [sbt(es0, f"c1_t{j}", [128, 1], F32) for j in range(3)]
    sp_t = [sbt(es0, f"sp_t{j}", [128, 1], F32) for j in range(3)]
    c1s_t = [sbt(es0, f"c1s_t{j}", [128, 1], F32) for j in range(3)]
    sbf_scr = sbt(es0, "sbf_scr", [128, D], BF16)
    ln1_t = [sbt(es0, f"ln1_t{j}", [128, D], BF16) for j in range(2)]

    xfm = sbt(esA, "xfm", [128, DC * S], BF16)
    wA = sbt(esA, "wA", [128, DC * D], BF16)
    wB = sbt(esA, "wB", [128, DC * D], BF16)
    wVT = sbt(esA, "wVT", [128, DC * D], BF16)
    wR = sbt(esA, "wR", [128, DC * D], BF16)
    csb = sbt(esA, "csb", [128, D], F32)
    qs_sb = [sbt(esA, f"qs_sb{j}", [128, D], F32) for j in range(3)]
    qfix = [sbt(esA, f"qfix{j}", [128, D], F32) for j in range(2)]
    sqs = sbt(esA2, "sqs", [128, DC * 512], BF16)
    qa = [sbt(esA2, f"qa{j}", [128, D], BF16) for j in range(3)]
    esA2.close()
    qf32 = [sbt(esP3, f"qf32{j}", [128, D], BF16) for j in range(2)]
    xtm_t = [sbt(esP3, f"xtm_t{j}", [128, D], F32) for j in range(3)]
    x1_t = xtm_t
    u1_sb = qs_sb
    u2_sb = qfix
    scw = qfix
    wexp = qs_sb
    esP3.close()
    esA.close()

    scfull = sbt(esT, "scfull", [128, 64], F32)
    ws_sb = sbt(esT, "ws_sb", [128, 64], F32)
    candv = sbt(esT, "candv", [128, CANDK], F32)
    candi = sbt(esT, "candi", [128, CANDK], U32)
    candif = sbt(esT, "candif", [128, CANDK], F32)
    candtu = sbt(esT, "candtu", [128, CANDK], U32)
    candrep = sbt(esT, "candrep", [128, NCAND], F32)
    cmp_scr = sbt(esT, "cmp_scr", [128, NCAND], BF16)
    tchall = sbt(esT, "tchall", [128, MYCH], F32)
    tokall = sbt(esT, "tokall", [128, MYCH], U32)
    rank_f = [sbt(esT, f"rank_f{j}", [128, 1], F32) for j in range(2)]
    rank_u = [sbt(esT, f"rank_u{j}", [128, 1], U32) for j in range(2)]
    esT.close()

    wout_sb = sbt(esF, "wout_sb", [128, DC * D], BF16)
    qkv_sb = [sbt(esF, f"qkv_sb{b}", [128, 3 * D], BF16) for b in range(MYPB)]
    xsel = [sbt(esF, f"xsel{b}", [128, D], BF16) for b in range(MYPB)]
    xself = [sbt(esF, f"xself{b}", [128, D], F32) for b in range(MYPB)]
    h_sb = [sbt(esF, f"h_sb{b}", [128, D], BF16) for b in range(MYPB)]
    sums_sb = sbt(esF, "sums_sb", [1, NH * MYPB * 128], F32)
    recqall = sbt(esF, "recqall", [128, 2 * NH], F32)
    o_sb = [sbt(esF, f"o_sb{b}", [128, D], BF16) for b in range(MYPB)]
    idxp = [sbt(esF, f"idxp{b}", [128, 1], U32) for b in range(NPB)]
    idxf = [sbt(esF, f"idxf{b}", [128, 1], F32) for b in range(NPB)]
    idxm = [sbt(esF, f"idxm{b}", [128, 1], F32) for b in range(NPB)]
    idxm2 = [sbt(esF, f"idxm2{b}", [128, 1], F32) for b in range(NPB)]
    idxu = [sbt(esF, f"idxu{b}", [128, 1], U32) for b in range(NPB)]
    ssq_t = [sbt(esF, f"ssq_t{b}", [128, 1], F32) for b in range(MYPB)]
    rms_t = [sbt(esF, f"rms_t{b}", [128, 1], F32) for b in range(MYPB)]
    xattn = [sbt(esF, f"xattn{b}", [128, D], F32) for b in range(MYPB)]
    wqkvq = sbt(esF2, "wqkvq", [128, DC * D], BF16)
    wqkvk = sbt(esF2, "wqkvk", [128, DC * D], BF16)
    wqkvv = sbt(esF2, "wqkvv", [128, DC * D], BF16)
    kvall = [sbt(esF2, f"kvall{b}", [128, 2 * D], BF16) for b in range(NPB)]
    kfm = sbt(esF2, "kfm", [128, 8 * NPB * 128], BF16)
    qfm = sbt(esF2, "qfm", [128, 8 * MYPB * 128], BF16)
    hfm = sbt(esF2, "hfm", [128, MYPB * D], BF16)
    attm = [sbt(esF2, f"attm{j}", [128, NPB * 128], F32) for j in range(3)]
    probsT = [sbt(esF2, f"probsT{j}", [128, NPB * 128], BF16) for j in range(3)]
    ofm = sbt(esF2, "ofm", [128, MYPB * D], BF16)
    esF2.close()

    wf1 = [sbt(esW, f"wf1_{j}", [128, DC * 512], BF16) for j in range(2)]
    wf2 = [sbt(esW, f"wf2_{j}", [128, DC * 512], BF16) for j in range(2)]
    wf3 = [sbt(esW, f"wf3_{j}", [128, 4 * D], BF16) for j in range(2)]
    s2_t = [sbt(esW, f"s2_t{b}", [128, D], F32) for b in range(MYPB)]
    s2q_t = [sbt(esW, f"s2q_t{b}", [128, 1], F32) for b in range(MYPB)]
    rms2_t = [sbt(esW, f"rms2_t{b}", [128, 1], F32) for b in range(MYPB)]
    h2_sb = [sbt(esW, f"h2_sb{b}", [128, D], BF16) for b in range(MYPB)]
    h2fm = sbt(esW, "h2fm", [128, MYPB * D], BF16)
    asil = sbt(esW, "asil", [128, 512], F32)
    g_sb = sbt(esW, "g_sb", [128, 512], BF16)
    gfm = sbt(esW, "gfm", [128, 512], BF16)
    delta_my = [sbt(esW, f"delta_my{b}", [128, D], F32) for b in range(MYPB)]
    delta_ld = [sbt(esW, f"delta_ld{b}", [128, D], F32) for b in range(NPB)]
    esW.close()

    pb = [es0.enter_context(nc.psum_tensor(f"pb{j}", [128, 512], F32)) for j in range(6)]
    pb6 = es0.enter_context(nc.psum_tensor("pb6", [128, 512], F32))
    pT = es0.enter_context(nc.psum_tensor("pT", [128, 1024], BF16))

    m_hw = sem("m_hw"); m_gx = sem("m_gx")
    m_wq = sem("m_wq"); m_wk = sem("m_wk"); m_wv = sem("m_wv")
    m_wr = sem("m_wr"); m_w12 = sem("m_w12"); m_wqkv = sem("m_wqkv")
    m_wout = sem("m_wout")
    m_wf = [sem("m_wf0"), sem("m_wf1")]
    m_st2 = [sem("m_st0"), sem("m_st1")]
    m_xt = [sem("m_xt0"), sem("m_xt1")]
    m_stc = sem("m_stc")
    m_sd = sem("m_sd")
    m_sp = [sem("m_sp0"), sem("m_sp1")]
    m_ln = [sem("m_ln0"), sem("m_ln1")]
    m_hb2 = [sem("m_hb0"), sem("m_hb1")]
    m_gst2 = [sem("m_gst0"), sem("m_gst1")]
    m_gq3 = sem("m_gq3"); m_gi = sem("m_gi"); m_gm = sem("m_gm")
    m_lncp = sem("m_lncp"); m_pc = sem("m_pc")
    m_t = sem("m_t"); m_v = sem("m_v"); m_s = sem("m_s"); m_cc = sem("m_cc")
    m_g = sem("m_g")
    m_vx = sem("m_vx"); m_sx = sem("m_sx"); m_gx2 = sem("m_gx2")

    class ChainProxy:
        _PASS = {"wait_ge", "value_load", "dma_start", "indirect_dma_start",
                 "collective_compute", "then_inc", "nop"}

        def __init__(self, eng, semh):
            self._e = eng
            self._sem = semh
            self._n = 0

        def __getattr__(self, name):
            fn = getattr(self._e, name)
            if name in ChainProxy._PASS or name.startswith("_"):
                return fn

            def call(*a, **kw):
                ins = fn(*a, **kw)
                try:
                    ins.then_inc(self._sem, 1)
                except Exception:
                    return ins
                self._n += 1
                self._e.wait_ge(self._sem, self._n)
                return _MarkShim(self._e)
            return call

    class _MarkShim:
        def __init__(self, eng):
            self._e = eng

        def then_inc(self, semh, n):
            self._e.sem_inc(semh, n)
            return self

    T_P0 = lambda ct: 2 * (ct + 1)
    T_P0_END = 2 * NCT
    T_CSQ = T_P0_END + 1
    T_CSK = T_P0_END + 2
    T_P1Q = lambda i: T_CSK + (i + 1 if i < 3 else 2 * i - 2)
    T_P1G = lambda i: T_CSK + (2 * i + 5 if i < NBLK - 3 else NBLK + 1 + i)
    T_P1_END = T_CSK + 2 * NBLK
    T_P2Q = lambda i: T_P1_END + (i + 1 if i < 3 else 2 * i - 2)
    T_P2G = lambda i: T_P1_END + (2 * i + 5 if i < NBLK - 3 else NBLK + 1 + i)
    T_P2_END = T_P1_END + 2 * NBLK
    T_WVR = lambda u: T_P2_END + u + 1
    T_CSV = T_P2_END + 17
    T_P3Q = lambda i: T_CSV + i + 1
    T_SCT = T_CSV + NBLK + 1
    T_P3_END = T_SCT
    T_FH = lambda b: T_P3_END + b + 1
    T_QKV = lambda b, j: T_P3_END + 2 + 6 * b + j + 1
    T_QKV_END = T_P3_END + 14
    T_KT = lambda kb: T_QKV_END + kb + 1
    T_QT = lambda b: T_QKV_END + 4 + b + 1
    T_KQ_END = T_QKV_END + 6
    T_ATT = lambda h, b: T_KQ_END + (
        (2 * h + b + 1) if 2 * h + b < 2 else 3 * (2 * h + b) - 3)
    T_SUM = lambda h, b: T_KQ_END + (
        (3 * (2 * h + b) + 4) if 2 * h + b < 30 else 93 + 2 * (2 * h + b - 30))
    T_O = lambda h, b: T_KQ_END + (
        (3 * (2 * h + b) + 5) if 2 * h + b < 30 else 94 + 2 * (2 * h + b - 30))
    T_AT_END = T_KQ_END + 6 * NH
    T_OT = lambda b: T_AT_END + b + 1
    T_XA = lambda b: T_AT_END + 2 + b + 1
    T_H2T = lambda b: T_AT_END + 4 + b + 1
    T_XA_END = T_AT_END + 6
    T_SW = lambda ft, b, k: T_XA_END + 4 * (ft * MYPB + b) + k
    T_SW_END = T_XA_END + 4 * MYPB * NFT
    T_END = T_SW_END

    S_P0 = lambda ct: 8 * (ct + 1)
    S_P0_END = 8 * NCT
    S_P1 = lambda i, k: S_P0_END + (
        (1 if i == 0 else 3 * i) if k == 1 else
        (2 if i == 0 else 3 * i + 1) if k == 2 else
        (3 * i + 5 if i < NBLK - 1 else 3 * NBLK))
    S_P1_END = S_P0_END + 3 * NBLK
    S_P2 = lambda i, k: S_P1_END + (
        (1 if i == 0 else 2 * i) if k == 1 else
        (2 * i + 3 if i < NBLK - 1 else 2 * NBLK))
    S_P2_END = S_P1_END + 2 * NBLK
    S_P3 = lambda i, k: S_P2_END + (
        (i + 1 if i < 2 else 5 * i - 6) if k == 1 else
        (5 * i + 3 if i < NBLK - 1 else 5 * NBLK - 3) if k == 2
        else (5 * i + 2 + k if i <= NBLK - 3 else
              (5 * NBLK - 9 + k if i == NBLK - 2 else 5 * NBLK - 5 + k)))
    S_P3_END = S_P2_END + 5 * NBLK
    S_XS = lambda b: S_P3_END + 2 * b + 1
    S_RSQ = lambda b: S_P3_END + 2 * b + 2
    S_XS_END = S_P3_END + 2 * MYPB
    S_EXP = lambda h, b: S_XS_END + 2 * h + b + 1
    S_EXP_END = S_XS_END + 2 * NH
    S_H2 = lambda b, k: S_EXP_END + 2 * b + k
    S_H2_END = S_EXP_END + 2 * MYPB
    S_SIL = lambda ft, b: S_H2_END + ft * MYPB + b + 1
    S_SIL_END = S_H2_END + MYPB * NFT
    S_END = S_SIL_END

    V_MZ = 1
    V_P0 = lambda ct: V_MZ + 2 * (ct + 1)
    V_P0_END = V_MZ + 2 * NCT
    V_CSQ = V_P0_END + 1
    V_P1 = lambda i, k: V_CSQ + (
        (k if i == 0 else 4 * i + k - 1) if k < 4 else
        (4 * i + 7 if i < NBLK - 1 else 4 * NBLK))
    V_P1_END = V_CSQ + 4 * NBLK
    V_GQ = V_P1_END + 1
    V_WKP = V_P1_END + 2
    V_CSK = V_P1_END + 3
    V_P2 = lambda i, k: V_CSK + (
        (1 if i == 0 else 2 * i) if k == 1 else
        (2 * i + 3 if i < NBLK - 1 else 2 * NBLK))
    V_P2_END = V_CSK + 2 * NBLK
    V_GK = V_P2_END + 1
    V_WRP = V_P2_END + 2
    V_WVR = lambda u: V_WRP + u + 1
    V_CSV = V_WRP + 17
    V_P3 = lambda i, k: V_CSV + 5 * i + k
    V_P3_END = V_CSV + 5 * NBLK
    V_SCT = V_P3_END + 1
    V_TK = lambda k: V_SCT + k
    V_TK_END = V_SCT + 6
    V_RK = lambda j: V_TK_END + j + 1
    V_RK_END = V_TK_END + MYCH
    V_XS = lambda b: V_RK_END + b + 1
    V_XS_END = V_RK_END + MYPB
    V_FH = lambda b: V_XS_END + b + 1
    V_QKV = lambda b, j: V_XS_END + 2 + 6 * b + j + 1
    V_QKV_END = V_XS_END + 14
    V_KT = lambda kb: V_QKV_END + kb + 1
    V_QT = lambda b: V_QKV_END + 4 + b + 1
    V_KQ_END = V_QKV_END + 6
    V_MSK = lambda h, b: V_KQ_END + (
        (2 * h + b + 1) if 2 * h + b < 2 else 3 * (2 * h + b) - 3)
    V_SUMC = lambda h, b: V_KQ_END + (
        (3 * (2 * h + b) + 4) if 2 * h + b < 30 else 93 + 2 * (2 * h + b - 30))
    V_OE = lambda h, b: V_KQ_END + (
        (3 * (2 * h + b) + 5) if 2 * h + b < 30 else 94 + 2 * (2 * h + b - 30))
    V_AT_END = V_KQ_END + 6 * NH
    V_ON = lambda b: V_AT_END + b + 1
    V_OT = lambda b: V_AT_END + 2 + b + 1
    V_XAE = lambda b: V_AT_END + 4 + b + 1
    V_S2 = lambda b: V_AT_END + 6 + b + 1
    V_H2 = lambda b: V_AT_END + 8 + b + 1
    V_H2T = lambda b: V_AT_END + 10 + b + 1
    V_H2_END = V_AT_END + 12
    V_SW = lambda ft, b, k: V_H2_END + 2 * (ft * MYPB + b) + k
    V_SW_END = V_H2_END + 2 * MYPB * NFT
    V_XFF = lambda b: V_SW_END + b + 1
    V_XFF_END = V_SW_END + MYPB
    V_FIL = lambda b: V_XFF_END + b + 1
    V_FIL_END = V_XFF_END + NPB
    V_END = V_FIL_END

    G_P3 = lambda i: 2 * i + 1
    G_SC = lambda i: 2 * i + 2
    G_END = NBLK

    CC_WARM, CC_GQ, CC_GK, CC_SC, CC_ORD, CC_KV, CC_DEL = 1, 2, 3, 4, 5, 6, 7

    HW_CONST = 12 * W16

    STW = lambda i: 32 * (i // 2 + 1)
    XTW = lambda i: 16 * (i // 2 + 1)
    SPW = lambda ct: 32 * (ct // 2 + 1)
    LNW = lambda i: 16 * (i // 2 + 1)
    HBW = lambda i: 16 * (i // 2 + 1)
    GSTW = lambda i: 16 * (i // 2 + 1)
    LN_FULL = 16 * (NBLK // 2)
    HB_FULL = 16 * (NBLK // 2)

    STC_GKC = 16
    STC_SCF = 32
    STC_CREP = 48
    STC_TCH = 64
    STC_TOK = 80
    STC_IDXP = lambda b: 80 + 16 * (b + 1)
    STC_IDXP_END = 80 + 16 * MYPB
    STC_KV = lambda kb: STC_IDXP_END + 16 * (kb + 1)
    STC_KV_END = STC_IDXP_END + 16 * NPB
    STC_RQA = STC_KV_END + 16
    STC_DEL = lambda b: STC_RQA + 16 * (b + 1)
    STC_DEL_END = STC_RQA + 16 * NPB
    STC_IDXA = lambda b: STC_DEL_END + 16 * (b + 1)
    STC_END = STC_DEL_END + 16 * NPB

    SD_CSQ = 16
    SD_SUMW = 32
    SD_CSB1 = 48
    SD_SWB = 64
    SD_GQ = 80
    SD_CSK = 96
    SD_CSB2 = 112
    SD_GK = 128
    SD_CSV = 144
    SD_CSB3 = 160
    SD_SC = 176
    SD_CANDV = 192
    SD_CANDT = 208
    SD_KV = lambda b: 208 + 16 * (b + 1)
    SD_KV_END = 208 + 16 * MYPB
    SD_SUMSA = SD_KV_END + 16
    SD_DEL = lambda b: SD_SUMSA + 16 * (b + 1)
    SD_DEL_END = SD_SUMSA + 16 * MYPB
    SD_END = SD_DEL_END

    GX_END = 16 * NCT
    W_FULL = 16 * 8
    W12_FULL = 32
    WQKV_FULL = 16 * 24
    _ft_dmas = [8 + 8 + ((fs + 127) // 128) for fs in FT_SIZES]
    _wf_cum = [[], []]
    for _j, _d in enumerate(_ft_dmas):
        _wf_cum[_j % 2].append((sum(_wf_cum[_j % 2][-1:]) if False else 0))
    _wf_tot = [0, 0]
    _wf_at = []
    for _j, _d in enumerate(_ft_dmas):
        _wf_tot[_j % 2] += 16 * _d
        _wf_at.append(_wf_tot[_j % 2])
    WF_AT = lambda j: _wf_at[j]
    GQ3 = lambda i: 16 * (i + 1)
    GQ3_END = 16 * NBLK
    GM_GQB = 16
    GM_GQR = 32
    GI_OZ = 16
    GI_RS = lambda j: GI_OZ + 16 * (j + 1)
    GI_RS_END = GI_OZ + 16 * MYCH
    GI_GA = lambda b: GI_RS_END + 16 * (b + 1)
    GI_GA_END = GI_RS_END + 16 * MYPB
    GI_FS = lambda b: GI_GA_END + 16 * (b + 1)
    GI_FS_END = GI_GA_END + 16 * NPB

    cnt = {}
    def mk(inst, semh, n=1, expect=None):
        key = semh.name
        cnt[key] = cnt.get(key, 0) + n
        if expect is not None:
            assert cnt[key] == expect, (key, cnt[key], expect)
        return inst.then_inc(semh, n)

    blk = es0.enter_context(nc.Block())

    def wchunk(t, c, n=D):
        return t[:, c * n:(c + 1) * n]

    @blk.sync
    def _(e):
        e.dma_start(out=id_sb[:, :], in_=ident_d[:, :]).then_inc(m_hw, W16)
        e.dma_start(out=id32_sb[:, :], in_=idf32_d[:, :]).then_inc(m_hw, W16)
        e.dma_start(out=ones_sb[:, :], in_=ones_d[:, :]).then_inc(m_hw, W16)
        e.dma_start(out=iota64_sb[:, :], in_=iota64_d[:, :]).then_inc(m_hw, W16)
        e.wait_ge(m_hw, 4 * W16)
        e.dma_start(out=maskf_sb[:, :], in_=maskf_d[:, :]).then_inc(m_hw, W16)
        e.dma_start(out=cinfu_sb[:, :], in_=cinfu_d[:, :]).then_inc(m_hw, W16)
        e.dma_start(out=cinfob[:, :], in_=cinff_d[0:1, :].to_broadcast([128, 8])).then_inc(m_hw, W16)
        e.dma_start(out=wrb[:, :], in_=wrout_d[0:1, :].to_broadcast([128, D])).then_inc(m_hw, W16)
        e.dma_start(out=scT[0:1, 0:1], in_=cinff_d[0:1, 1:2]).then_inc(m_hw, W16)
        e.wait_ge(m_v, V_MZ)
        e.dma_start(out=cc_sc_i.rearrange("a (p c) -> (a p) c", p=128),
                    in_=zrow[:, :]).then_inc(m_hw, W16)
        e.dma_start(out=cc_w_i[:, :], in_=zrow[0:1, 0:8]).then_inc(m_hw, W16)
        e.dma_start(out=cc_b_i[:, :], in_=zrow[0:1, 0:8]).then_inc(m_hw, W16)
        cnt["m_hw"] = 12 * W16

        def stc(inst, expect):
            mk(inst, m_stc, W16, expect=expect)
            e.wait_ge(m_stc, expect)

        if not ST1:
            e.wait_ge(m_sp[0], SPW(NCT - 2))
            e.wait_ge(m_sp[1], SPW(NCT - 1))
            if DBG:
                e.wait_ge(m_v, V_CSQ)
                e.dma_start(out=d_gq[:, :], in_=csq_dram[:, :]).then_inc(m_hw, W16)
                e.wait_ge(m_hw, HW_CONST + 16)
            return
        for i in range(NBLK):
            ct = i // 4
            e.wait_ge(m_sp[ct % 2], SPW(ct))
            if i >= 2:
                e.wait_ge(m_v, V_P1(i - 2, 2))
            mk(e.dma_start(out=sx_bl[i % 2][:, :],
                           in_=sx_dram[i:i + 1, :].rearrange("a (p c) -> (a p) c", c=1)),
               m_st2[i % 2], W16)
            mk(e.dma_start(out=sq_bl[i % 2][:, :],
                           in_=sq_dram[i:i + 1, :].rearrange("a (p c) -> (a p) c", c=1)),
               m_st2[i % 2], W16, expect=STW(i))
        if not ST2:
            if NO_AR1:
                e.wait_ge(m_gst2[0], GSTW(NBLK - 2))
                e.wait_ge(m_gst2[1], GSTW(NBLK - 1))
                e.wait_ge(m_t, T_P1_END)
            else:
                e.wait_ge(m_cc, CC_GQ)
            if DBG:
                e.dma_start(out=d_gq[:, :], in_=cc_gq_o[:, :]).then_inc(m_hw, W16)
                e.wait_ge(m_gst2[0], GSTW(NBLK - 2))
                e.wait_ge(m_gst2[1], GSTW(NBLK - 1))
                e.dma_start(out=d_q[:, :], in_=q_dram[:, :]).then_inc(m_hw, W16)
                e.wait_ge(m_hw, HW_CONST + 32)
            return
        e.wait_ge(m_cc, CC_GK)
        stc(e.dma_start(out=gksc[:, :],
                        in_=cc_gk_o.rearrange("a (c p) -> (a p) c", p=128),
                        allow_slow_non_contiguous=True), STC_GKC)
        par_c = e.value_load(cinfu_sb[0:1, 3:4])
        mylncp = ln1_sh.rearrange("(o r) d -> o (r d)", o=2)[bass.ts(par_c, 1), :]
        ln1f = ln1_loc.rearrange("r d -> (r d)")
        BLKE = 128 * D
        LNCH = [(0, 14), (14, 8), (22, 8), (30, 2)]

        def lncp(k):
            b0, nb = LNCH[k]
            e.wait_ge(m_ln[0], 16 * ((b0 + nb + 1) // 2))
            e.wait_ge(m_ln[1], 16 * ((b0 + nb + 1) // 2))
            mk(e.dma_start(out=mylncp[:, b0 * BLKE:(b0 + nb) * BLKE],
                           in_=ln1f[b0 * BLKE:(b0 + nb) * BLKE]), m_lncp, W16)

        for i in range(NBLK):
            if i >= 3:
                e.wait_ge(m_g, G_SC(i - 3))
                e.wait_ge(m_hb2[(i - 1) % 2], HBW(i - 3))
            mk(e.dma_start(out=xtm_t[i % 3][:, :],
                           in_=x_tm[128 * i:128 * (i + 1), :]),
               m_xt[i % 2], W16, expect=XTW(i))
            if i == 17:
                lncp(0)
            elif i == 25:
                lncp(1)
            if i >= 1:
                e.wait_ge(m_s, S_P3(i - 1, 5))
                mk(e.dma_start(out=out[128 * (i - 1):128 * i, :],
                               in_=xtm_t[(i - 1) % 3][:, :]),
                   m_hb2[(i - 1) % 2], W16, expect=HBW(i - 1))
        lncp(2)
        lncp(3)
        e.wait_ge(m_s, S_P3(NBLK - 1, 5))
        mk(e.dma_start(out=out[128 * (NBLK - 1):128 * NBLK, :],
                       in_=xtm_t[(NBLK - 1) % 3][:, :]),
           m_hb2[(NBLK - 1) % 2], W16, expect=HBW(NBLK - 1))
        if not ST3:
            e.wait_ge(m_sd, SD_SC)
            e.wait_ge(m_hb2[0], HB_FULL)
            e.wait_ge(m_hb2[1], HB_FULL)
            if DBG:
                e.dma_start(out=d_gq[:, :], in_=cc_gq_o[:, :]).then_inc(m_hw, W16)
                e.dma_start(out=d_gk[:, :], in_=cc_gk_o[:, :]).then_inc(m_hw, W16)
                e.dma_start(out=d_sc[:, :], in_=cc_sc_i[:, :]).then_inc(m_hw, W16)
                e.wait_ge(m_hw, HW_CONST + 48)
            return
        e.wait_ge(m_cc, CC_SC)
        stc(e.dma_start(out=scfull[:, :],
                        in_=cc_sc_o.rearrange("a (p c) -> (a p) c", p=128)),
            STC_SCF)
        e.wait_ge(m_sd, SD_CANDT)
        stc(e.dma_start(out=candrep[:, :],
                        in_=candv_dram[0:1, :].to_broadcast([128, NCAND])),
            STC_CREP)
        par_r = e.value_load(cinfu_sb[0:1, 3:4])
        cvv = candv_dram.rearrange("a (o c p) -> (a o) p c", o=2, p=128)[bass.ts(par_r, 1), :, :]
        ctv = candt_dram.rearrange("a (o c p) -> (a o) p c", o=2, p=128)[bass.ts(par_r, 1), :, :]
        stc(e.dma_start(out=tchall[:, :],
                        in_=cvv, allow_slow_non_contiguous=True), STC_TCH)
        stc(e.dma_start(out=tokall[:, :],
                        in_=ctv, allow_slow_non_contiguous=True), STC_TOK)
        if not ST4:
            e.wait_ge(m_cc, CC_ORD)
            if DBG:
                e.dma_start(out=d_oi[:, :], in_=ordidx_sh[:, :]).then_inc(m_hw, W16)
                e.dma_start(out=d_sc[:, :], in_=cc_sc_o[:, :]).then_inc(m_hw, W16)
                e.wait_ge(m_hw, HW_CONST + 32)
            return
        ordv = ordidx_sh.rearrange("(o r) b -> o r b", o=2)[bass.ts(par_r, 1), :, :]
        e.wait_ge(m_cc, CC_ORD)
        for b in range(MYPB):
            stc(e.dma_start(out=idxp[b][:, :],
                            in_=ordv[:, 128 * b:128 * (b + 1), :]), STC_IDXP(b))
        e.wait_ge(m_cc, CC_KV)
        for kb in range(NPB):
            stc(e.dma_start(out=kvall[kb][:, :],
                            in_=kv_sh[128 * kb:128 * (kb + 1), :]), STC_KV(kb))
        if not ST5:
            e.wait_ge(m_v, V_KT(NPB - 1))
            e.wait_ge(m_sd, SD_KV_END)
            return
        smv = sums_dram.rearrange("o (w p) -> o p w", p=128)[bass.ts(par_r, 1), :, :]
        e.wait_ge(m_sd, SD_SUMSA)
        stc(e.dma_start(out=recqall[:, :], in_=smv,
                        allow_slow_non_contiguous=True), STC_RQA)
        if not ST6:
            e.wait_ge(m_v, V_S2(MYPB - 1))
            e.wait_ge(m_s, S_H2(MYPB - 1, 2))
            return
        e.wait_ge(m_cc, CC_DEL)
        for b in range(NPB):
            stc(e.dma_start(out=delta_ld[b][:, :],
                            in_=delta_sh[128 * b:128 * (b + 1), :]), STC_DEL(b))
        for b in range(NPB):
            stc(e.dma_start(out=idxp[b][:, :],
                            in_=ordidx_sh[128 * b:128 * (b + 1), :]), STC_IDXA(b))
        e.wait_ge(m_gi, GI_FS_END)

    @blk.scalar
    def _(e):
        e = ChainProxy(e, m_sx)
        e.wait_ge(m_hw, HW_CONST)
        par_s = e.value_load(cinfu_sb[0:1, 3:4])
        myln1 = ln1_sh.rearrange("(o r) d -> o (r d)", o=2)[bass.ts(par_s, 1), :]
        mykv = kv_sh.rearrange("(o r) d -> o r d", o=2)[bass.ts(par_s, 1), :, :]
        mydel = delta_sh.rearrange("(o r) d -> o r d", o=2)[bass.ts(par_s, 1), :, :]
        mysums = sums_dram[bass.ts(par_s, 1), :]

        def sdw(inst, expect):
            mk(inst, m_sd, W16, expect=expect)
            e.wait_ge(m_sd, expect)


        def statw(ct):
            e.wait_ge(m_v, V_P0(ct))
            if ct >= 2:
                e.wait_ge(m_sp[ct % 2], SPW(ct - 2))
            mk(e.dma_start(out=sx_dram[4 * ct:4 * (ct + 1), :],
                           in_=sxrow[0:1, :]), m_sp[ct % 2], W16)
            mk(e.dma_start(out=sq_dram[4 * ct:4 * (ct + 1), :],
                           in_=sqrow[0:1, :]), m_sp[ct % 2], W16,
               expect=SPW(ct))

        for ct in range(NCT):
            e.wait_ge(m_gx, 16 * (ct + 1))
            if ct >= 1:
                e.wait_ge(m_t, T_P0(ct - 1))
            for c in range(DC):
                mk(e.activation(sqs[:, c * 512:(c + 1) * 512],
                                xfm[:, c * S + 512 * ct:c * S + 512 * (ct + 1)],
                                AF.Square), m_s, 1)
            assert cnt["m_s"] == S_P0(ct)
            if ct >= 1:
                statw(ct - 1)
        statw(NCT - 1)
        e.wait_ge(m_v, V_CSQ)
        sdw(e.dma_start(out=csq_dram[:, :], in_=csrow[:, :]), SD_CSQ)
        sdw(e.dma_start(out=sumw_dram[:, :], in_=sumw_row[:, :]), SD_SUMW)
        sdw(e.dma_start(out=csb[:, :],
                        in_=csq_dram[0:1, :].to_broadcast([128, D])), SD_CSB1)
        sdw(e.dma_start(out=sumwb[:, :],
                        in_=sumw_dram[0:1, :].to_broadcast([128, 1])), SD_SWB)
        if not ST1:
            return
        for i in range(NBLK):
            e.wait_ge(m_v, V_P1(i, 1))
            e.activation(std_t[i % 2][:, :], var_t[i % 2][:, :], AF.Ln,
                         bias=epsln_c[:, :])
            mk(e.activation(s_all[:, i:i + 1], std_t[i % 2][:, :], AF.Exp,
                            scale=-0.5), m_s, 1, expect=S_P1(i, 1))
            e.wait_ge(m_t, T_P1Q(i))
            if i >= 3:
                e.wait_ge(m_v, V_P1(i - 3, 4))
            for half in range(2):
                ins = e.activation(qs_sb[i % 3][:, 512 * half:512 * (half + 1)],
                                   pb[2 * (i % 2) + half][:, :], AF.Identity,
                                   scale=s_all[:, i:i + 1])
                if half:
                    mk(ins, m_s, 1, expect=S_P1(i, 2))
            if i >= 1:
                e.wait_ge(m_v, V_P1(i - 1, 3))
                mk(e.activation(wexp[(i - 1) % 3][:, :], qfix[(i - 1) % 2][:, :],
                                AF.Exp, scale=SCALE,
                                accum_out=sume[(i - 1) % 2][:, :]), m_s, 1,
                   expect=S_P1(i - 1, 3))
        e.wait_ge(m_v, V_P1(NBLK - 1, 3))
        mk(e.activation(wexp[(NBLK - 1) % 3][:, :], qfix[(NBLK - 1) % 2][:, :],
                        AF.Exp, scale=SCALE,
                        accum_out=sume[(NBLK - 1) % 2][:, :]), m_s, 1,
           expect=S_P1(NBLK - 1, 3))
        e.wait_ge(m_v, V_GQ)
        sdw(e.dma_start(out=cc_gq_i[:, :], in_=gqrow[:, :]), SD_GQ)
        if not ST2:
            return
        e.wait_ge(m_v, V_CSK)
        sdw(e.dma_start(out=csk_dram[:, :], in_=cskrow[:, :]), SD_CSK)
        sdw(e.dma_start(out=csb[:, :],
                        in_=csk_dram[0:1, :].to_broadcast([128, D])), SD_CSB2)
        for i in range(NBLK):
            e.wait_ge(m_t, T_P2Q(i))
            if i >= 3:
                e.wait_ge(m_v, V_P2(i - 3, 2))
            for half in range(2):
                ins = e.activation(qs_sb[i % 3][:, 512 * half:512 * (half + 1)],
                                   pb[2 * (i % 2) + half][:, :], AF.Identity,
                                   scale=s_all[:, i:i + 1])
                if half:
                    mk(ins, m_s, 1, expect=S_P2(i, 1))
            if i >= 1:
                e.wait_ge(m_v, V_P2(i - 1, 1))
                mk(e.activation(wexp[(i - 1) % 3][:, :], qfix[(i - 1) % 2][:, :],
                                AF.Exp, scale=SCALE,
                                accum_out=sume[(i - 1) % 2][:, :]), m_s, 1,
                   expect=S_P2(i - 1, 2))
        e.wait_ge(m_v, V_P2(NBLK - 1, 1))
        mk(e.activation(wexp[(NBLK - 1) % 3][:, :], qfix[(NBLK - 1) % 2][:, :],
                        AF.Exp, scale=SCALE,
                        accum_out=sume[(NBLK - 1) % 2][:, :]), m_s, 1,
           expect=S_P2(NBLK - 1, 2))
        e.wait_ge(m_v, V_GK)
        sdw(e.dma_start(out=cc_gk_i[:, :], in_=grow[:, :]), SD_GK)
        e.wait_ge(m_v, V_CSV)
        sdw(e.dma_start(out=csv_dram[:, :], in_=csrow[:, :]), SD_CSV)
        sdw(e.dma_start(out=csb[:, :],
                        in_=csv_dram[0:1, :].to_broadcast([128, D])), SD_CSB3)
        def p3sq(j):
            e.wait_ge(m_v, V_P3(j, 2))
            mk(e.activation(sbf_scr[:, :], x1_t[j % 3][:, :], AF.Square,
                            accum_out=sx1q_t[j % 3][:, :]), m_s, 1,
               expect=S_P3(j, 2))

        def p3tail(j):
            e.wait_ge(m_v, V_P3(j, 3))
            e.activation(std_t[j % 3][:, :], var_t[j % 3][:, :], AF.Ln,
                         bias=epsln_c[:, :])
            mk(e.activation(s1_t[j % 3][:, :], std_t[j % 3][:, :], AF.Exp,
                            scale=-0.5), m_s, 1, expect=S_P3(j, 3))
            e.wait_ge(m_v, V_P3(j, 4))
            if j >= 2:
                e.wait_ge(m_ln[j % 2], LNW(j - 2))
            mk(e.activation(ln1_t[j % 2][:, :], x1_t[j % 3][:, :], AF.Identity,
                            scale=s1_t[j % 3][:, :], bias=c1_t[j % 3][:, :]),
               m_s, 1, expect=S_P3(j, 4))
            mk(e.dma_start(out=ln1_loc[128 * j:128 * (j + 1), :],
                           in_=ln1_t[j % 2][:, :]), m_ln[j % 2], W16,
               expect=LNW(j))
            e.wait_ge(m_g, G_SC(j))
            mk(e.activation(xtm_t[j % 3][:, :], x1_t[j % 3][:, :], AF.Identity,
                            scale=sp_t[j % 3][:, :], bias=c1_t[j % 3][:, :]),
               m_s, 1, expect=S_P3(j, 5))

        for i in range(NBLK):
            e.wait_ge(m_t, T_P3Q(i))
            if i >= 2:
                e.wait_ge(m_v, V_P3(i - 2, 1))
            for half in range(2):
                ins = e.activation(u1_sb[i % 2][:, 512 * half:512 * (half + 1)],
                                   pb[2 * (i % 3) + half][:, :], AF.Identity,
                                   scale=s_all[:, i:i + 1])
                if half:
                    mk(ins, m_s, 1, expect=S_P3(i, 1))
            if i >= 2:
                p3tail(i - 2)
            if i >= 1:
                p3sq(i - 1)
        p3tail(NBLK - 2)
        p3sq(NBLK - 1)
        p3tail(NBLK - 1)
        e.wait_ge(m_v, V_SCT)
        scv = cc_sc_i.rearrange("a (r p) -> (a r) p", p=128)
        sdw(e.dma_start(out=scv[bass.ts(par_s, 32), :], in_=scT[:, :]), SD_SC)
        if not ST3:
            return
        e.wait_ge(m_v, V_TK_END)
        mk(e.dma_start(out=candv_dram.rearrange("a (p c) -> (a p) c", p=128),
                       in_=candv[:, :]), m_sd, W16, expect=SD_CANDV)
        sdw(e.dma_start(out=candt_dram.rearrange("a (p c) -> (a p) c", p=128),
                        in_=candtu[:, :]), SD_CANDT)
        e.wait_ge(m_pc, 16 * 16)
        qv = wqkvT_bf.rearrange("(c p) n -> p c n", p=128)
        for part in range(3):
            for c in range(DC):
                mk(e.dma_start(
                    out=wchunk([wqkvq, wqkvk, wqkvv][part], c),
                    in_=qv[:, c, part * D:(part + 1) * D]), m_wqkv, W16)
            e.wait_ge(m_wqkv, 128 * (part + 1))
        assert cnt["m_wqkv"] == WQKV_FULL
        e.wait_ge(m_pc, 24 * 16)
        e.wait_ge(m_v, V_RK(MYCH - 1))
        e.wait_ge(m_gi, GI_RS(MYCH - 1))
        ov = woutT_bf.rearrange("(c p) n -> p c n", p=128)
        for c in range(DC):
            mk(e.dma_start(out=wchunk(wout_sb, c), in_=ov[:, c, :]),
               m_wout, W16)
        assert cnt["m_wout"] == W_FULL
        if not ST4:
            return
        for b in range(MYPB):
            e.wait_ge(m_gi, GI_GA(b))
            e.activation(sbf_scr[:, :], xsel[b][:, :], AF.Square,
                         accum_out=ssq_t[b][:, :])
            mk(e.copy(out=xself[b][:, :], in_=xsel[b][:, :]), m_s, 1,
               expect=S_XS(b))
            e.activation(rms_t[b][:, :], ssq_t[b][:, :], AF.Ln,
                         scale=INV_D, bias=epsrms_c[:, :])
            mk(e.activation(ssq_t[b][:, :], rms_t[b][:, :], AF.Exp,
                            scale=-0.5), m_s, 1, expect=S_RSQ(b))
        for b in range(MYPB):
            e.wait_ge(m_v, V_QKV(b, 5))
            sdw(e.dma_start(out=mykv[:, 128 * b:128 * (b + 1), :],
                            in_=qkv_sb[b][:, D:3 * D]), SD_KV(b))
        if not ST5:
            return
        for h in range(NH):
            for b in range(MYPB):
                w = 2 * h + b
                e.wait_ge(m_v, V_MSK(h, b))
                if w >= 3:
                    e.wait_ge(m_t, T_O((w - 3) // 2, (w - 3) % 2))
                wd = 384 if b == 0 else 512
                mk(e.activation(probsT[w % 3][:, 0:wd], attm[w % 3][:, 0:wd],
                                AF.Exp, scale=0.125), m_s, 1,
                   expect=S_EXP(h, b))
        e.wait_ge(m_v, V_SUMC(NH - 1, MYPB - 1))
        sdw(e.dma_start(out=mysums, in_=sums_sb[0:1, :]), SD_SUMSA)
        for b in range(MYPB):
            e.wait_ge(m_v, V_S2(b))
            mk(e.activation(sbf_scr[:, :], s2_t[b][:, :], AF.Square,
                            accum_out=s2q_t[b][:, :]), m_s, 1, expect=S_H2(b, 1))
            e.activation(rms2_t[b][:, :], s2q_t[b][:, :], AF.Ln,
                         scale=INV_D, bias=epsrms_c[:, :])
            mk(e.activation(s2q_t[b][:, :], rms2_t[b][:, :], AF.Exp,
                            scale=-0.5), m_s, 1, expect=S_H2(b, 2))
        if not ST6:
            return
        for ft in range(NFT):
            for b in range(MYPB):
                fs = FT_SIZES[ft]
                e.wait_ge(m_t, T_SW(ft, b, 2))
                if (ft * MYPB + b) >= 1:
                    pi = ft * MYPB + b - 1
                    e.wait_ge(m_v, V_SW(pi // MYPB, pi % MYPB, 1))
                mk(e.activation(asil[:, 0:fs], pb[0][:, 0:fs], AF.Silu), m_s,
                   1, expect=S_SIL(ft, b))
        for b in range(MYPB):
            e.wait_ge(m_v, V_XFF(b))
            sdw(e.dma_start(out=mydel[:, 128 * b:128 * (b + 1), :],
                            in_=delta_my[b][:, :]), SD_DEL(b))

    @blk.vector
    def _(e):
        e = ChainProxy(e, m_vx)
        e.memset(zrow[:, :], 0.0)
        e.memset(epsln_c[:, :], EPS_LN)
        e.memset(epsrms_c[:, :], EPS_RMS)
        mk(e.memset(zidx[:, :], 0), m_v, 1, expect=V_MZ)
        for ct in range(NCT):
            e.wait_ge(m_t, T_P0(ct))
            if ct >= 1:
                e.wait_ge(m_sp[(ct - 1) % 2], SPW(ct - 1))
            mk(e.tensor_copy(sxrow[0:1, :], pb[4][0:1, :]), m_v, 1)
            mk(e.tensor_copy(sqrow[0:1, :], pb[5][0:1, :]), m_v, 1,
               expect=V_P0(ct))
        e.wait_ge(m_t, T_CSK)
        e.tensor_copy(csrow[0:1, 0:512], pb[4][0:1, :])
        e.tensor_copy(csrow[0:1, 512:1024], pb[5][0:1, :])
        e.tensor_copy(cskrow[0:1, 0:512], pb[0][0:1, :])
        e.tensor_copy(cskrow[0:1, 512:1024], pb[1][0:1, :])
        e.wait_ge(m_hw, HW_CONST)
        mk(e.reduce_sum(sumw_row[:, :], wrb[0:1, :], axis=AXX), m_v, 1,
           expect=V_CSQ)
        if not ST1:
            return

        def p1norm(j, vm=None):
            e.wait_ge(m_s, S_P1(j, 3) if vm is None else vm[0])
            if j >= 3:
                e.wait_ge(m_t, (T_P1G if vm is None else vm[1])(j - 3))
            with nc.allow_low_precision(reason="1/sum feeds a bf16 matmul"):
                e.reciprocal(recqt[j % 3][:, :], sume[j % 2][:, :])
            mk(e.tensor_mul(qa[j % 3][:, :], qfix[j % 2][:, :],
                            wexp[j % 3][:, :]), m_v, 1,
               expect=V_P1(j, 4) if vm is None else vm[2](j))

        for i in range(NBLK):
            e.wait_ge(m_st2[i % 2], STW(i))
            e.tensor_scalar_mul(mean_t[i % 2][:, :], sx_bl[i % 2][:, :], INV_D)
            e.tensor_scalar_mul(e2_t[i % 2][:, :], sq_bl[i % 2][:, :], INV_D)
            e.tensor_mul(var_t[i % 2][:, :], mean_t[i % 2][:, :],
                         mean_t[i % 2][:, :])
            mk(e.tensor_sub(var_t[i % 2][:, :], e2_t[i % 2][:, :],
                            var_t[i % 2][:, :]), m_v, 1, expect=V_P1(i, 1))
            e.wait_ge(m_s, S_P1(i, 1))
            mk(e.tensor_scalar(c_all[:, i:i + 1], mean_t[i % 2][:, :],
                               s_all[:, i:i + 1], -1.0, op0=ALU.mult,
                               op1=ALU.mult), m_v, 1, expect=V_P1(i, 2))
            e.wait_ge(m_s, S_P1(i, 2))
            e.wait_ge(m_sd, SD_CSB1)
            if i >= 2:
                e.wait_ge(m_s, S_P1(i - 2, 3))
                e.wait_ge(m_gst2[i % 2], GSTW(i - 2))
            mk(e.scalar_tensor_tensor(qfix[i % 2][:, :], csb[:, :],
                                      c_all[:, i:i + 1], qs_sb[i % 3][:, :],
                                      op0=ALU.mult, op1=ALU.add), m_v, 1,
               expect=V_P1(i, 3))
            if i >= 1:
                p1norm(i - 1)
        p1norm(NBLK - 1)
        e.wait_ge(m_t, T_P1G(NBLK - 1))
        e.tensor_copy(gqrow[0:1, 0:512], pb[4][0:1, :])
        mk(e.tensor_copy(gqrow[0:1, 512:1024], pb[5][0:1, :]), m_v, 1,
           expect=V_GQ)
        if not ST2:
            return
        e.wait_ge(m_gm, GM_GQB)
        e.wait_ge(m_wk, W_FULL)
        for c in range(DC):
            ins = e.tensor_mul(wchunk(wB, c), wchunk(wB, c), gqb[:, :])
            if c == DC - 1:
                mk(ins, m_v, 1, expect=V_WKP)
        e.wait_ge(m_gm, GM_GQR)
        mk(e.tensor_mul(cskrow[:, :], cskrow[:, :], gqrow[:, :]), m_v, 1,
           expect=V_CSK)
        for i in range(NBLK):
            e.wait_ge(m_s, S_P2(i, 1))
            e.wait_ge(m_sd, SD_CSB2)
            if i >= 2:
                e.wait_ge(m_s, S_P2(i - 2, 2))
            else:
                e.wait_ge(m_gst2[i % 2], GSTW(NBLK - 2 + i))
            mk(e.scalar_tensor_tensor(qfix[i % 2][:, :], csb[:, :],
                                      c_all[:, i:i + 1], qs_sb[i % 3][:, :],
                                      op0=ALU.mult, op1=ALU.add), m_v, 1,
               expect=V_P2(i, 1))
            if i >= 1:
                p1norm(i - 1, vm=(S_P2(i - 1, 2), T_P2G,
                                  lambda j: V_P2(j, 2)))
        p1norm(NBLK - 1, vm=(S_P2(NBLK - 1, 2), T_P2G,
                             lambda j: V_P2(j, 2)))
        e.wait_ge(m_t, T_P2G(NBLK - 1))
        e.tensor_copy(grow[0:1, 0:512], pb[4][0:1, :])
        mk(e.tensor_copy(grow[0:1, 512:1024], pb[5][0:1, :]), m_v, 1,
           expect=V_GK)
        e.wait_ge(m_stc, STC_GKC)
        e.wait_ge(m_wr, W_FULL)
        for c in range(DC):
            ins = e.tensor_scalar(wchunk(wR, c), wchunk(wR, c),
                                  gksc[:, c:c + 1], None, op0=ALU.mult)
            if c == DC - 1:
                mk(ins, m_v, 1, expect=V_WRP)
        for u in range(16):
            m, jh = divmod(u, 2)
            e.wait_ge(m_t, T_WVR(u))
            mk(e.tensor_copy(wA[:, m * D + jh * 512:m * D + (jh + 1) * 512],
                             pb[u % 4][:, :]), m_v, 1, expect=V_WVR(u))
        e.wait_ge(m_t, T_CSV)
        e.tensor_copy(csrow[0:1, 0:512], pb[4][0:1, :])
        mk(e.tensor_copy(csrow[0:1, 512:1024], pb[5][0:1, :]), m_v, 1,
           expect=V_CSV)
        for i in range(NBLK):
            e.wait_ge(m_s, S_P3(i, 1))
            e.wait_ge(m_sd, SD_CSB3)
            if i >= 2:
                e.wait_ge(m_g, G_SC(i - 2))
            mk(e.scalar_tensor_tensor(u2_sb[i % 2][:, :], csb[:, :],
                                      c_all[:, i:i + 1], u1_sb[i % 2][:, :],
                                      op0=ALU.mult, op1=ALU.add), m_v, 1,
               expect=V_P3(i, 1))
            e.wait_ge(m_g, G_P3(i))
            mk(e.scalar_tensor_tensor(x1_t[i % 3][:, :], xtm_t[i % 3][:, :],
                                      1.0, u2_sb[i % 2][:, :], op0=ALU.mult,
                                      op1=ALU.add,
                                      accum_out=sx1_t[i % 3][:, :]), m_v, 1,
               expect=V_P3(i, 2))
            e.wait_ge(m_s, S_P3(i, 2))
            e.tensor_scalar_mul(mean_t[i % 3][:, :], sx1_t[i % 3][:, :], INV_D)
            e.tensor_scalar_mul(e2_t[i % 3][:, :], sx1q_t[i % 3][:, :], INV_D)
            e.tensor_mul(var_t[i % 3][:, :], mean_t[i % 3][:, :],
                         mean_t[i % 3][:, :])
            mk(e.tensor_sub(var_t[i % 3][:, :], e2_t[i % 3][:, :],
                            var_t[i % 3][:, :]), m_v, 1, expect=V_P3(i, 3))
            e.wait_ge(m_s, S_P3(i, 3))
            e.tensor_scalar(c1_t[i % 3][:, :], mean_t[i % 3][:, :],
                            s1_t[i % 3][:, :], -1.0, op0=ALU.mult, op1=ALU.mult)
            e.tensor_scalar(sp_t[i % 3][:, :], s1_t[i % 3][:, :], 1.0, None,
                            op0=ALU.add)
            mk(e.tensor_mul(c1s_t[i % 3][:, :], c1_t[i % 3][:, :],
                            sumwb[:, :]), m_v, 1, expect=V_P3(i, 4))
            e.wait_ge(m_g, G_SC(i))
            e.tensor_scalar(scw[i % 2][:, :], scw[i % 2][:, :], 1.0, None,
                            op0=ALU.mult, op1=ALU.add,
                            accum_out=sxw_t[i % 3][:, :])
            mk(e.tensor_scalar(sc_all[:, i:i + 1], sxw_t[i % 3][:, :],
                               s1_t[i % 3][:, :], c1s_t[i % 3][:, :],
                               op0=ALU.mult, op1=ALU.add), m_v, 1,
               expect=V_P3(i, 5))
        e.wait_ge(m_t, T_SCT)
        mk(e.tensor_copy(scT[:, :], pb6[0:32, 0:128]), m_v, 1, expect=V_SCT)
        if not ST3:
            return
        e.wait_ge(m_stc, STC_SCF)
        mk(e.max(out=candv[:, 0:8], in_=scfull[:, :]), m_v, 1, expect=V_TK(1))
        e.wait_ge(m_v, V_TK(1))
        mk(e.max_index(out=candi[:, 0:8], in_max=candv[:, 0:8],
                       in_values=scfull[:, :]), m_v, 1, expect=V_TK(2))
        e.wait_ge(m_v, V_TK(2))
        mk(e.match_replace(out=ws_sb[:, :], in_to_replace=candv[:, 0:8],
                           in_values=scfull[:, :], imm_value=NEG), m_v, 1,
           expect=V_TK(3))
        e.wait_ge(m_v, V_TK(3))
        mk(e.max(out=candv[:, 8:16], in_=ws_sb[:, :]), m_v, 1, expect=V_TK(4))
        e.wait_ge(m_v, V_TK(4))
        mk(e.max_index(out=candi[:, 8:16], in_max=candv[:, 8:16],
                       in_values=ws_sb[:, :]), m_v, 1, expect=V_TK(5))
        e.wait_ge(m_v, V_TK(5))
        e.tensor_copy(candif[:, :], candi[:, :])
        e.tensor_scalar(candif[:, :], candif[:, :], iota64_sb[:, :], None,
                        op0=ALU.add)
        mk(e.tensor_copy(candtu[:, :], candif[:, :]), m_v, 1, expect=V_TK(6))
        e.wait_ge(m_stc, STC_TOK)
        for j in range(MYCH):
            if j >= 2:
                e.wait_ge(m_gi, GI_RS(j - 2))
            e.tensor_scalar(cmp_scr[:, :], candrep[:, :], tchall[:, j:j + 1],
                            None, op0=ALU.is_gt, op1=ALU.add,
                            accum_out=rank_f[j % 2][:, :])
            mk(e.tensor_copy(rank_u[j % 2][:, :], rank_f[j % 2][:, :]), m_v,
               1, expect=V_RK(j))
        if not ST4:
            return
        for b in range(MYPB):
            e.wait_ge(m_s, S_RSQ(b))
            e.wait_ge(m_w12, W12_FULL)
            mk(e.scalar_tensor_tensor(h_sb[b][:, :], xsel[b][:, :],
                                      ssq_t[b][:, :], g1b[:, :],
                                      op0=ALU.mult, op1=ALU.mult), m_v, 1,
               expect=V_XS(b))
        for b in range(MYPB):
            e.wait_ge(m_t, T_FH(b))
            mk(e.tensor_copy(hfm[:, b * D:(b + 1) * D], pT[:, :]), m_v, 1,
               expect=V_FH(b))
        for b in range(MYPB):
            for j in range(6):
                e.wait_ge(m_t, T_QKV(b, j))
                mk(e.tensor_copy(qkv_sb[b][:, j * 512:(j + 1) * 512],
                                 pb[j % 2][:, :]), m_v, 1, expect=V_QKV(b, j))
        kfm4 = kfm.rearrange("p (hp kb c) -> p hp kb c", hp=8, kb=NPB)
        pT3 = pT.rearrange("p (hp c) -> p hp c", hp=8)
        for kb in range(NPB):
            e.wait_ge(m_t, T_KT(kb))
            mk(e.tensor_copy(kfm4[:, :, kb, :], pT3[:, :, :]), m_v, 1,
               expect=V_KT(kb))
        if not ST5:
            return
        qfm4 = qfm.rearrange("p (hp bx c) -> p hp bx c", hp=8, bx=MYPB)
        for b in range(MYPB):
            e.wait_ge(m_t, T_QT(b))
            mk(e.tensor_copy(qfm4[:, :, b, :], pT3[:, :, :]), m_v, 1,
               expect=V_QT(b))
        def att_evac(w):
            h2, b2 = divmod(w, 2)
            e.wait_ge(m_t, T_SUM(h2, b2))
            mk(e.tensor_copy(sums_sb[0:1, w * 128:(w + 1) * 128],
                             pb[w % 2][0:1, 0:128]),
               m_v, 1, expect=V_SUMC(h2, b2))
            e.wait_ge(m_t, T_O(h2, b2))
            mk(e.tensor_copy(o_sb[b2][:, h2 * 64:(h2 + 1) * 64],
                             pb[2 + w % 2][:, 0:64]),
               m_v, 1, expect=V_OE(h2, b2))

        for h in range(NH):
            for b in range(MYPB):
                w = 2 * h + b
                e.wait_ge(m_t, T_ATT(h, b))
                if w >= 3:
                    e.wait_ge(m_s, S_EXP((w - 3) // 2, (w - 3) % 2))
                wd = 384 if b == 0 else 512
                mk(e.tensor_add(attm[w % 3][:, 0:wd],
                                [pb[4], pb[5], pb6][w % 3][:, 0:wd],
                                maskf_sb[:, b * 512:b * 512 + wd]), m_v, 1,
                   expect=V_MSK(h, b))
                if w >= 2:
                    att_evac(w - 2)
        for w in range(2 * NH - 2, 2 * NH):
            att_evac(w)
        e.wait_ge(m_stc, STC_RQA)
        e.reciprocal(recqall[:, :], recqall[:, :])
        for b in range(MYPB):
            for h in range(NH):
                w = 2 * h + b
                ins = e.tensor_scalar(o_sb[b][:, h * 64:(h + 1) * 64],
                                      o_sb[b][:, h * 64:(h + 1) * 64],
                                      recqall[:, w:w + 1], None, op0=ALU.mult)
                if h == NH - 1:
                    mk(ins, m_v, 1, expect=V_ON(b))
        for b in range(MYPB):
            e.wait_ge(m_t, T_OT(b))
            mk(e.tensor_copy(ofm[:, b * D:(b + 1) * D], pT[:, :]), m_v, 1,
               expect=V_OT(b))
        for b in range(MYPB):
            e.wait_ge(m_t, T_XA(b))
            e.tensor_copy(xattn[b][:, 0:512], pb[0][:, :])
            mk(e.tensor_copy(xattn[b][:, 512:1024], pb[1][:, :]), m_v, 1,
               expect=V_XAE(b))
        for b in range(MYPB):
            e.wait_ge(m_s, S_XS(b))
            mk(e.tensor_add(s2_t[b][:, :], xattn[b][:, :], xself[b][:, :]),
               m_v, 1, expect=V_S2(b))
        if not ST6:
            return
        for b in range(MYPB):
            e.wait_ge(m_s, S_H2(b, 2))
            e.wait_ge(m_w12, W12_FULL)
            mk(e.scalar_tensor_tensor(h2_sb[b][:, :], s2_t[b][:, :],
                                      s2q_t[b][:, :], g2b[:, :],
                                      op0=ALU.mult, op1=ALU.mult), m_v, 1,
               expect=V_H2(b))
        for b in range(MYPB):
            e.wait_ge(m_t, T_H2T(b))
            mk(e.tensor_copy(h2fm[:, b * D:(b + 1) * D], pT[:, :]), m_v, 1,
               expect=V_H2T(b))
        for ft in range(NFT):
            for b in range(MYPB):
                fs = FT_SIZES[ft]
                e.wait_ge(m_s, S_SIL(ft, b))
                mk(e.tensor_mul(g_sb[:, 0:fs], asil[:, 0:fs],
                                pb[1][:, 0:fs]), m_v, 1, expect=V_SW(ft, b, 1))
                e.wait_ge(m_t, T_SW(ft, b, 3))
                ngc = (fs + 127) // 128
                mk(e.tensor_copy(gfm[:, 0:ngc * 128], pT[:, 0:ngc * 128]),
                   m_v, 1, expect=V_SW(ft, b, 2))
        for b in range(MYPB):
            e.wait_ge(m_t, T_SW(NFT - 1, b, 4))
            e.tensor_add(delta_my[b][:, 0:512], xattn[b][:, 0:512],
                         pb[2 + 2 * b][:, :])
            mk(e.tensor_add(delta_my[b][:, 512:1024], xattn[b][:, 512:1024],
                            pb[3 + 2 * b][:, :]), m_v, 1, expect=V_XFF(b))
        for b in range(NPB):
            e.wait_ge(m_stc, STC_IDXA(b))
            e.tensor_copy(idxf[b][:, :], idxp[b][:, :])
            e.tensor_scalar(idxf[b][:, :], idxf[b][:, :], cinfob[:, 0:1],
                            None, op0=ALU.subtract)
            e.tensor_scalar(idxm[b][:, :], idxf[b][:, :], 0.0, None,
                            op0=ALU.is_ge)
            e.tensor_scalar(idxm2[b][:, :], idxf[b][:, :], float(S), None,
                            op0=ALU.is_lt)
            e.tensor_mul(idxm[b][:, :], idxm[b][:, :], idxm2[b][:, :])
            e.tensor_mul(idxf[b][:, :], idxf[b][:, :], idxm[b][:, :])
            e.tensor_scalar(idxm[b][:, :], idxm[b][:, :], -100000.0, 100000.0,
                            op0=ALU.mult, op1=ALU.add)
            e.tensor_add(idxf[b][:, :], idxf[b][:, :], idxm[b][:, :])
            mk(e.tensor_copy(idxu[b][:, :], idxf[b][:, :]), m_v, 1,
               expect=V_FIL(b))

    @blk.tensor
    def _(e):
        e.wait_ge(m_hw, 4 * W16)
        tl = [None]

        def M(*a, **kw):
            tl[0] = e.matmul(*a, **kw)
            return tl[0]

        def T(*a, **kw):
            tl[0] = e.transpose(*a, **kw)
            return tl[0]

        for ct in range(NCT):
            e.wait_ge(m_gx, 16 * (ct + 1))
            if ct >= 1:
                e.wait_ge(m_v, V_P0(ct - 1))
            for c in range(DC):
                M(pb[4][0:1, :], ones_sb[:, :],
                         xfm[:, c * S + 512 * ct:c * S + 512 * (ct + 1)],
                         start=(c == 0), stop=(c == DC - 1))
            mk(tl[0], m_t, 1)
            e.wait_ge(m_s, S_P0(ct))
            for c in range(DC):
                M(pb[5][0:1, :], ones_sb[:, :],
                         sqs[:, c * 512:(c + 1) * 512],
                         start=(c == 0), stop=(c == DC - 1))
            mk(tl[0], m_t, 1, expect=T_P0(ct))
        e.wait_ge(m_wq, W_FULL)
        e.wait_ge(m_v, V_P0(NCT - 1))
        for half in range(2):
            for c in range(DC):
                M([pb[4], pb[5]][half][0:1, :], ones_sb[:, :],
                         wA[:, c * D + half * 512:c * D + (half + 1) * 512],
                         start=(c == 0), stop=(c == DC - 1))
        mk(tl[0], m_t, 1, expect=T_CSQ)
        e.wait_ge(m_wk, W_FULL)
        for half in range(2):
            for c in range(DC):
                M([pb[0], pb[1]][half][0:1, :], ones_sb[:, :],
                         wB[:, c * D + half * 512:c * D + (half + 1) * 512],
                         start=(c == 0), stop=(c == DC - 1))
        mk(tl[0], m_t, 1, expect=T_CSK)
        if not ST1:
            return
        e.wait_ge(m_v, V_CSQ)

        def gq_mm(i, pss, end):
            e.wait_ge(m_v, V_P1(i, 4) if not end else V_P2(i, 2))
            M(pss[0][0:1, :], recqt[i % 3][:, :], qa[i % 3][:, 0:512],
                     start=(i == 0), stop=(i == NBLK - 1))
            M(pss[1][0:1, :], recqt[i % 3][:, :], qa[i % 3][:, 512:1024],
                     start=(i == 0), stop=(i == NBLK - 1))

        for i in range(NBLK):
            if i >= 2:
                e.wait_ge(m_s, S_P1(i - 2, 2))
            for half in range(2):
                for c in range(DC):
                    M(pb[2 * (i % 2) + half][:, :],
                             xfm[:, c * S + 128 * i:c * S + 128 * (i + 1)],
                             wchunk(wA, c)[:, 512 * half:512 * (half + 1)],
                             start=(c == 0), stop=(c == DC - 1))
            mk(tl[0], m_t, 1, expect=T_P1Q(i))
            if i >= 3:
                gq_mm(i - 3, (pb[4], pb[5]), False)
                mk(tl[0], m_t, 1, expect=T_P1G(i - 3))
        for j in range(NBLK - 3, NBLK):
            gq_mm(j, (pb[4], pb[5]), False)
            mk(tl[0], m_t, 1, expect=T_P1G(j))
        if not ST2:
            return
        e.wait_ge(m_v, V_WKP)
        e.wait_ge(m_v, V_GQ)
        for i in range(NBLK):
            if i >= 2:
                e.wait_ge(m_s, S_P2(i - 2, 1))
            else:
                e.wait_ge(m_s, S_P1(NBLK - 2 + i, 2))
            for half in range(2):
                for c in range(DC):
                    M(pb[2 * (i % 2) + half][:, :],
                             xfm[:, c * S + 128 * i:c * S + 128 * (i + 1)],
                             wchunk(wB, c)[:, 512 * half:512 * (half + 1)],
                             start=(c == 0), stop=(c == DC - 1))
            mk(tl[0], m_t, 1, expect=T_P2Q(i))
            if i >= 3:
                gq_mm(i - 3, (pb[4], pb[5]), True)
                mk(tl[0], m_t, 1, expect=T_P2G(i - 3))
        for j in range(NBLK - 3, NBLK):
            gq_mm(j, (pb[4], pb[5]), True)
            mk(tl[0], m_t, 1, expect=T_P2G(j))
        e.wait_ge(m_wv, W_FULL)
        e.wait_ge(m_v, V_WRP)
        for u in range(16):
            m, jh = divmod(u, 2)
            if u >= 4:
                e.wait_ge(m_v, V_WVR(u - 4))
            else:
                e.wait_ge(m_s, S_P2(NBLK - 4 + u, 1))
            for c in range(DC):
                M(pb[u % 4][:, :],
                         wchunk(wVT, c)[:, m * 128:(m + 1) * 128],
                         wchunk(wR, c)[:, jh * 512:(jh + 1) * 512],
                         start=(c == 0), stop=(c == DC - 1))
            mk(tl[0], m_t, 1, expect=T_WVR(u))
        e.wait_ge(m_v, V_GK)
        e.wait_ge(m_v, V_WVR(15))
        for half in range(2):
            for c in range(DC):
                M([pb[4], pb[5]][half][0:1, :], ones_sb[:, :],
                         wA[:, c * D + half * 512:c * D + (half + 1) * 512],
                         start=(c == 0), stop=(c == DC - 1))
        mk(tl[0], m_t, 1, expect=T_CSV)
        e.wait_ge(m_v, V_CSV)
        for i in range(NBLK):
            if i >= 3:
                e.wait_ge(m_s, S_P3(i - 3, 1))
            elif i == 0:
                e.wait_ge(m_v, V_WVR(13))
            elif i == 1:
                e.wait_ge(m_v, V_WVR(15))
            for half in range(2):
                for c in range(DC):
                    M(pb[2 * (i % 3) + half][:, :],
                             xfm[:, c * S + 128 * i:c * S + 128 * (i + 1)],
                             wchunk(wA, c)[:, 512 * half:512 * (half + 1)],
                             start=(c == 0), stop=(c == DC - 1))
            mk(tl[0], m_t, 1, expect=T_P3Q(i))
        e.wait_ge(m_v, V_P3(NBLK - 1, 5))
        mk(T(pb6[0:32, 0:128], sc_all[:, :], id32_sb[:, :]), m_t, 1,
           expect=T_SCT)
        if not ST4:
            return
        for b in range(MYPB):
            e.wait_ge(m_v, V_XS(b))
            if b >= 1:
                e.wait_ge(m_v, V_FH(b - 1))
            for c in range(DC):
                T(pT[:, c * 128:(c + 1) * 128],
                            h_sb[b][:, c * 128:(c + 1) * 128], id_sb[:, :])
            mk(tl[0], m_t, 1, expect=T_FH(b))
        for b in range(MYPB):
            for j in range(6):
                e.wait_ge(m_wqkv, 128 * (j // 2 + 1))
                wsrc = [wqkvq, wqkvk, wqkvv][j // 2]
                jj = j % 2
                if 6 * b + j >= 2:
                    pi = 6 * b + j - 2
                    e.wait_ge(m_v, V_QKV(pi // 6, pi % 6))
                for c in range(DC):
                    M(pb[j % 2][:, :],
                             hfm[:, b * D + c * 128:b * D + (c + 1) * 128],
                             wchunk(wsrc, c)[:, jj * 512:(jj + 1) * 512],
                             start=(c == 0), stop=(c == DC - 1))
                mk(tl[0], m_t, 1, expect=T_QKV(b, j))
        for kb in range(NPB):
            e.wait_ge(m_stc, STC_KV(kb))
            if kb >= 1:
                e.wait_ge(m_v, V_KT(kb - 1))
            else:
                e.wait_ge(m_v, V_FH(MYPB - 1))
            for hp in range(8):
                T(pT[:, hp * 128:(hp + 1) * 128],
                            kvall[kb][:, hp * 128:(hp + 1) * 128], id_sb[:, :])
            mk(tl[0], m_t, 1, expect=T_KT(kb))
        if not ST5:
            return
        for b in range(MYPB):
            e.wait_ge(m_v, V_QKV(b, 1))
            if b >= 1:
                e.wait_ge(m_v, V_QT(b - 1))
            else:
                e.wait_ge(m_v, V_KT(NPB - 1))
            for hp in range(8):
                T(pT[:, hp * 128:(hp + 1) * 128],
                            qkv_sb[b][:, hp * 128:(hp + 1) * 128], id_sb[:, :])
            mk(tl[0], m_t, 1, expect=T_QT(b))
        abank = [pb[4], pb[5], pb6]

        def sum_o(w):
            h2, b2 = divmod(w, 2)
            nkb = NPB - 1 if b2 == 0 else NPB
            e.wait_ge(m_s, S_EXP(h2, b2))
            if w >= 2:
                e.wait_ge(m_v, V_SUMC((w - 2) // 2, (w - 2) % 2))
            for kb in range(nkb):
                M(pb[w % 2][0:1, 0:128],
                         ones_sb[:, :],
                         probsT[w % 3][:, kb * 128:(kb + 1) * 128],
                         start=(kb == 0), stop=(kb == nkb - 1))
            mk(tl[0], m_t, 1, expect=T_SUM(h2, b2))
            if w >= 2:
                e.wait_ge(m_v, V_OE((w - 2) // 2, (w - 2) % 2))
            for kb in range(nkb):
                M(pb[2 + w % 2][:, 0:64],
                         probsT[w % 3][:, kb * 128:(kb + 1) * 128],
                         kvall[kb][:, D + h2 * 64:D + (h2 + 1) * 64],
                         start=(kb == 0), stop=(kb == nkb - 1))
            mk(tl[0], m_t, 1, expect=T_O(h2, b2))

        for h in range(NH):
            hp, hh = divmod(h, 2)
            for b in range(MYPB):
                w = 2 * h + b
                if w >= 3:
                    e.wait_ge(m_v, V_MSK((w - 3) // 2, (w - 3) % 2))
                if w == 0:
                    e.wait_ge(m_v, V_QT(MYPB - 1))
                for kb in range(NPB - 1 if b == 0 else NPB):
                    M(abank[w % 3][:, kb * 128:(kb + 1) * 128],
                             kfm4_slice(kfm, hp, kb, hh),
                             qfm4_slice(qfm, hp, b, hh),
                             start=True, stop=True)
                mk(tl[0], m_t, 1, expect=T_ATT(h, b))
                if w >= 2:
                    sum_o(w - 2)
        for w in range(2 * NH - 2, 2 * NH):
            sum_o(w)
        for b in range(MYPB):
            e.wait_ge(m_v, V_ON(b))
            if b >= 1:
                e.wait_ge(m_v, V_OT(b - 1))
            else:
                e.wait_ge(m_v, V_QT(MYPB - 1))
            for hp in range(8):
                T(pT[:, hp * 128:(hp + 1) * 128],
                            o_sb[b][:, hp * 128:(hp + 1) * 128], id_sb[:, :])
            mk(tl[0], m_t, 1, expect=T_OT(b))
        e.wait_ge(m_wout, W_FULL)
        for b in range(MYPB):
            e.wait_ge(m_v, V_OT(b))
            if b >= 1:
                e.wait_ge(m_v, V_XAE(b - 1))
            for half in range(2):
                for c in range(DC):
                    M(pb[half][:, :],
                             ofm[:, b * D + c * 128:b * D + (c + 1) * 128],
                             wchunk(wout_sb, c)[:, half * 512:(half + 1) * 512],
                             start=(c == 0), stop=(c == DC - 1))
            mk(tl[0], m_t, 1, expect=T_XA(b))
        if not ST6:
            return
        for b in range(MYPB):
            e.wait_ge(m_v, V_H2(b))
            if b >= 1:
                e.wait_ge(m_v, V_H2T(b - 1))
            else:
                e.wait_ge(m_v, V_OT(MYPB - 1))
            for c in range(DC):
                T(pT[:, c * 128:(c + 1) * 128],
                            h2_sb[b][:, c * 128:(c + 1) * 128], id_sb[:, :])
            mk(tl[0], m_t, 1, expect=T_H2T(b))
        for ft in range(NFT):
            for b in range(MYPB):
                fs = FT_SIZES[ft]
                u = ft * MYPB + b
                jp = ft % 2
                e.wait_ge(m_wf[ft % 2], WF_AT(ft))
                if u >= 1:
                    pi = u - 1
                    e.wait_ge(m_s, S_SIL(pi // MYPB, pi % MYPB))
                else:
                    e.wait_ge(m_v, V_XAE(MYPB - 1))
                    e.wait_ge(m_v, V_MSK(NH - 1, MYPB - 1))
                for c in range(DC):
                    M(pb[0][:, 0:fs],
                             h2fm[:, b * D + c * 128:b * D + (c + 1) * 128],
                             wchunk(wf1[jp], c, 512)[:, 0:fs],
                             start=(c == 0), stop=(c == DC - 1))
                mk(tl[0], m_t, 1, expect=T_SW(ft, b, 1))
                if u >= 1:
                    pi = u - 1
                    e.wait_ge(m_v, V_SW(pi // MYPB, pi % MYPB, 1))
                for c in range(DC):
                    M(pb[1][:, 0:fs],
                             h2fm[:, b * D + c * 128:b * D + (c + 1) * 128],
                             wchunk(wf2[jp], c, 512)[:, 0:fs],
                             start=(c == 0), stop=(c == DC - 1))
                mk(tl[0], m_t, 1, expect=T_SW(ft, b, 2))
                ngc = (fs + 127) // 128
                e.wait_ge(m_v, V_SW(ft, b, 1))
                if u >= 1:
                    pi = u - 1
                    e.wait_ge(m_v, V_SW(pi // MYPB, pi % MYPB, 2))
                else:
                    e.wait_ge(m_v, V_H2T(MYPB - 1))
                for gc in range(ngc):
                    gw = min(128, fs - gc * 128)
                    T(pT[0:gw, gc * 128:gc * 128 + 128],
                                g_sb[:, gc * 128:gc * 128 + gw], id_sb[:, :])
                mk(tl[0], m_t, 1, expect=T_SW(ft, b, 3))
                e.wait_ge(m_v, V_SW(ft, b, 2))
                for gc in range(ngc):
                    gw = min(128, fs - gc * 128)
                    for half in range(2):
                        M(pb[2 + 2 * b + half][:, :],
                                 gfm[0:gw, gc * 128:gc * 128 + 128],
                                 wf3[jp][0:gw, gc * D + half * 512:gc * D + (half + 1) * 512],
                                 start=(ft == 0 and gc == 0),
                                 stop=(ft == NFT - 1 and gc == ngc - 1))
                mk(tl[0], m_t, 1, expect=T_SW(ft, b, 4))

    @blk.gpsimd
    def _(e):
        e = ChainProxy(e, m_gx2)
        xv = x_fm.rearrange("(c p) s -> p c s", p=128)
        for ct in range(NCT):
            mk(e.dma_start(out=xfm.rearrange("p (c s) -> p c s", c=DC)[:, :, 512 * ct:512 * (ct + 1)],
                           in_=xv[:, :, 512 * ct:512 * (ct + 1)]),
               m_gx, W16)
            e.wait_ge(m_gx, 16 * (ct + 1))

        def wload(dst, src, semh):
            sv = src.rearrange("(c p) n -> p c n", p=128)
            for c in range(DC):
                mk(e.dma_start(out=wchunk(dst, c, sv.shape[2]),
                               in_=sv[:, c, :]), semh, W16)

        wload(wA, Wq_d, m_wq)
        e.wait_ge(m_hw, HW_CONST)
        mk(e.collective_compute("AllReduce", ALU.add, replica_groups=rg,
                                ins=[cc_w_i[:, :]], outs=[cc_w_o[:, :]]),
           m_cc, 1, expect=CC_WARM)
        wload(wB, Wk_d, m_wk)
        wload(wVT, WvT_d, m_wv)
        wload(wR, Wr_d, m_wr)
        mk(e.dma_start(out=g1b[:, :], in_=g1_d[0:1, :].to_broadcast([128, D])),
           m_w12, W16)
        mk(e.dma_start(out=g2b[:, :], in_=g2_d[0:1, :].to_broadcast([128, D])),
           m_w12, W16, expect=W12_FULL)
        if not ST1:
            return
        for i in range(NBLK):
            e.wait_ge(m_v, V_P1(i, 3))
            mk(e.dma_start(out=q_dram[128 * i:128 * (i + 1), :],
                           in_=qfix[i % 2][:, :]), m_gst2[i % 2], W16,
               expect=GSTW(i))
            if i < 16:
                mk(e.dma_start(out=wqkvT_bf[64 * i:64 * (i + 1), :],
                               in_=WqkvT_d[64 * i:64 * (i + 1), :]),
                   m_pc, W16)
            elif i < 24:
                j = i - 16
                if j == 0:
                    e.wait_ge(m_pc, 16 * 16)
                mk(e.dma_start(out=woutT_bf[128 * j:128 * (j + 1), :],
                               in_=WoutT_d[128 * j:128 * (j + 1), :]),
                   m_pc, W16)
        if not ST2 and NO_AR1:
            return
        e.wait_ge(m_sd, SD_GQ)
        mk(e.collective_compute("AllReduce", ALU.add, replica_groups=rg,
                                ins=[cc_gq_i[:, :]], outs=[cc_gq_o[:, :]]),
           m_cc, 1, expect=CC_GQ)
        if not ST2:
            return
        e.wait_ge(m_cc, CC_GQ)
        mk(e.dma_start(out=gqb[:, :],
                       in_=cc_gq_o[0:1, :].to_broadcast([128, D])), m_gm, W16,
           expect=GM_GQB)
        e.wait_ge(m_gm, GM_GQB)
        mk(e.dma_start(out=gqrow[:, :], in_=cc_gq_o[:, :]), m_gm, W16,
           expect=GM_GQR)
        e.wait_ge(m_sd, SD_GK)
        mk(e.collective_compute("AllReduce", ALU.add, replica_groups=rg,
                                ins=[cc_gk_i[:, :]], outs=[cc_gk_o[:, :]]),
           m_cc, 1, expect=CC_GK)
        for i in range(NBLK):
            mk(e.dma_start(out=qf32[i % 2][:, :],
                           in_=q_dram[128 * i:128 * (i + 1), :]), m_gq3, W16,
               expect=GQ3(i))
            e.wait_ge(m_xt[i % 2], XTW(i))
            e.wait_ge(m_gq3, GQ3(i))
            mk(e.tensor_tensor(xtm_t[i % 3][:, :], xtm_t[i % 3][:, :],
                               qf32[i % 2][:, :], ALU.add), m_g, 1,
               expect=G_P3(i))
            e.wait_ge(m_v, V_P3(i, 2))
            mk(e.tensor_tensor(scw[i % 2][:, :], x1_t[i % 3][:, :],
                               wrb[:, :], ALU.mult), m_g, 1, expect=G_SC(i))
        if not ST3:
            return
        e.wait_ge(m_v, V_MZ)
        par_g = e.value_load(cinfu_sb[0:1, 3:4])
        ordz = ordidx_sh.rearrange("(a r) b -> a (r b)", a=2)[bass.ts(par_g, 1), :]
        mk(e.dma_start(out=ordz, in_=zidx[0:2, :]), m_gi, W16, expect=GI_OZ)
        e.wait_ge(m_gi, GI_OZ)
        e.wait_ge(m_sd, SD_SC)
        e.wait_ge(m_lncp, 64)
        e.wait_ge(m_hb2[0], HB_FULL)
        e.wait_ge(m_hb2[1], HB_FULL)
        mk(e.collective_compute("AllReduce", ALU.add, replica_groups=rg,
                                ins=[cc_sc_i[:, :]], outs=[cc_sc_o[:, :]]),
           m_cc, 1, expect=CC_SC)
        e.wait_ge(m_cc, CC_SC)
        for j in range(MYCH):
            e.wait_ge(m_v, V_RK(j))
            mk(e.indirect_dma_start(
                out=ordidx_sh[:, :],
                out_offset=IndirectOffsetOnAxis(rank_u[j % 2][:, :], 0),
                in_=tokall[:, j:j + 1], in_offset=None,
                bounds_check=CAP - 1, oob_is_err=False), m_gi, W16,
               expect=GI_RS(j))
            e.wait_ge(m_gi, GI_RS(j))
        mk(e.collective_compute("AllReduce", ALU.add, replica_groups=rg,
                                ins=[cc_b_i[:, :]], outs=[cc_b_o[:, :]]),
           m_cc, 1, expect=CC_ORD)
        if not ST4:
            return
        e.wait_ge(m_cc, CC_ORD)
        for b in range(MYPB):
            e.wait_ge(m_stc, STC_IDXP(b))
            mk(e.indirect_dma_start(
                out=xsel[b][:, :], out_offset=None,
                in_=ln1_sh[:, :],
                in_offset=IndirectOffsetOnAxis(idxp[b][:, :], 0),
                bounds_check=2 * S - 1, oob_is_err=False), m_gi, W16,
               expect=GI_GA(b))
            e.wait_ge(m_gi, GI_GA(b))
        e.wait_ge(m_sd, SD_KV_END)
        mk(e.collective_compute("AllReduce", ALU.add, replica_groups=rg,
                                ins=[cc_b_i[:, :]], outs=[cc_b_o[:, :]]),
           m_cc, 1, expect=CC_KV)
        if not ST6:
            return
        w1v = W1T_d.rearrange("(c p) n -> p c n", p=128)
        w2v = W2T_d.rearrange("(c p) n -> p c n", p=128)
        for ft in range(NFT):
            fs = FT_SIZES[ft]
            fo = FT_OFF[ft]
            jp = ft % 2
            if ft >= 2:
                e.wait_ge(m_t, T_SW(ft - 2, MYPB - 1, 4))
            else:
                e.wait_ge(m_t, T_XA(MYPB - 1))
            for c in range(DC):
                mk(e.dma_start(out=wchunk(wf1[jp], c, 512)[:, 0:fs],
                               in_=w1v[:, c, fo:fo + fs]), m_wf[jp], W16)
            for c in range(DC):
                mk(e.dma_start(out=wchunk(wf2[jp], c, 512)[:, 0:fs],
                               in_=w2v[:, c, fo:fo + fs]), m_wf[jp], W16)
            ngc = (fs + 127) // 128
            for gc in range(ngc):
                gw_ = min(128, fs - gc * 128)
                mk(e.dma_start(out=wf3[jp][0:gw_, gc * D:(gc + 1) * D],
                               in_=W3T_d[fo + gc * 128:fo + gc * 128 + gw_, :]),
                   m_wf[jp], W16)
            assert cnt[m_wf[jp].name] == WF_AT(ft), (cnt[m_wf[jp].name], WF_AT(ft))
        e.wait_ge(m_sd, SD_DEL_END)
        mk(e.collective_compute("AllReduce", ALU.add, replica_groups=rg,
                                ins=[cc_b_i[:, :]], outs=[cc_b_o[:, :]]),
           m_cc, 1, expect=CC_DEL)
        for b in range(NPB):
            e.wait_ge(m_v, V_FIL(b))
            e.wait_ge(m_stc, STC_DEL(b))
            e.wait_ge(m_hb2[0], HB_FULL)
            e.wait_ge(m_hb2[1], HB_FULL)
            mk(e.indirect_dma_start(
                out=out[:, :],
                out_offset=IndirectOffsetOnAxis(idxu[b][:, :], 0),
                in_=delta_ld[b][:, :], in_offset=None,
                bounds_check=S - 1, oob_is_err=False,
                compute_op=ALU.add), m_gi, W16, expect=GI_FS(b))

    esF.close()
    es0.close()
    return nc


def kfm4_slice(kfm, hp, kb, hh):
    c0 = (hp * NPB + kb) * 128
    return kfm[hh * 64:(hh + 1) * 64, c0:c0 + 128]


def qfm4_slice(qfm, hp, b, hh):
    c0 = (hp * MYPB + b) * 128
    return qfm[hh * 64:(hh + 1) * 64, c0:c0 + 128]


_NC_CACHE = {}
_LAST_RES = None


def kernel(L0_attn_x, L0_attn_Wq, L0_attn_Wk, L0_attn_Wv, L0_attn_Wr,
           L0_ffn_w_router, L0_ffn_W_qkv, L0_ffn_W_out, L0_ffn_g1, L0_ffn_g2,
           L0_ffn_W1, L0_ffn_W2, L0_ffn_W3, stage=99):
    x = np.asarray(L0_attn_x)
    B = x.shape[0]
    n = 8

    f32 = np.float32
    cA = np.ascontiguousarray
    Wq = cA(L0_attn_Wq, dtype=f32)
    Wk = cA(L0_attn_Wk, dtype=f32)
    WvT = cA(np.asarray(L0_attn_Wv).T, dtype=f32)
    Wr = cA(L0_attn_Wr, dtype=f32)
    WqkvT = cA(np.asarray(L0_ffn_W_qkv).T, dtype=f32)
    WoutT = cA(np.asarray(L0_ffn_W_out).T, dtype=f32)
    W1T = cA(np.asarray(L0_ffn_W1).T, dtype=f32)
    W2T = cA(np.asarray(L0_ffn_W2).T, dtype=f32)
    W3T = cA(np.asarray(L0_ffn_W3).T, dtype=f32)
    wr_row = cA(np.asarray(L0_ffn_w_router).reshape(1, D), dtype=f32)
    g1 = cA(np.asarray(L0_ffn_g1).reshape(1, D), dtype=f32)
    g2 = cA(np.asarray(L0_ffn_g2).reshape(1, D), dtype=f32)

    ident_bf = np.eye(128, dtype=ml_dtypes.bfloat16)
    ident_f32 = np.eye(128, dtype=f32)
    ones_bf = np.ones((128, 1), dtype=ml_dtypes.bfloat16)
    iota64 = (np.arange(128, dtype=f32) * 64.0).reshape(128, 1)

    key = (stage, DBG)
    if key not in _NC_CACHE:
        _NC_CACHE[key] = build_nc(stage)
    nc = _NC_CACHE[key]

    in_maps = []
    for core in range(n):
        batch = core // 2
        parity = core % 2
        lo = parity * S
        xs = cA(x[batch, lo:lo + S, :], dtype=f32)
        xfm_s = cA(xs.T, dtype=f32)
        mask = np.empty((128, MYPB * NPB * 128), dtype=ml_dtypes.bfloat16)
        for b in range(MYPB):
            qb_g = parity * MYPB + b
            for kb in range(NPB):
                kk = kb * 128 + np.arange(128)[:, None]
                qq = qb_g * 128 + np.arange(128)[None, :]
                blkm = np.where(kk <= qq, 0.0, NEG).astype(f32)
                mask[:, b * NPB * 128 + kb * 128:(b) * NPB * 128 + (kb + 1) * 128] = blkm
        cinfo_u = np.zeros((1, 8), np.uint32)
        cinfo_u[0, 0] = parity * NBLK
        cinfo_u[0, 1] = parity * MYPB
        cinfo_u[0, 2] = parity * MYCH
        cinfo_u[0, 3] = parity
        cinfo_f = np.zeros((1, 8), f32)
        cinfo_f[0, 0] = float(lo)
        in_maps.append({
            "x_tm": xs, "x_fm": xfm_s,
            "Wq": Wq, "Wk": Wk, "WvT": WvT, "Wr": Wr,
            "WqkvT": WqkvT, "WoutT": WoutT, "W1T": W1T, "W2T": W2T,
            "W3T": W3T, "w_router": wr_row, "g1": g1, "g2": g2,
            "ident_bf": ident_bf, "ident_f32": ident_f32, "ones_bf": ones_bf,
            "iota64": iota64, "maskffn": mask,
            "cinfo_u": cinfo_u, "cinfo_f": cinfo_f,
        })

    res = run_bass_kernel_spmd(nc, in_maps, core_ids=list(range(n)))
    global _LAST_RES
    _LAST_RES = res
    outp = np.empty((B, 2 * S, D), np.float32)
    for core in range(n):
        batch = core // 2
        parity = core % 2
        lo = parity * S
        outp[batch, lo:lo + S, :] = res.results[core]["out"]
    return outp


# version 18 (final)
# speedup vs baseline: 1.1183x; 1.0036x over previous
import numpy as np
import ml_dtypes
from contextlib import ExitStack

import concourse.bass as bass
import concourse.mybir as mybir
from concourse.bass_utils import run_bass_kernel_spmd
from concourse.bass_core_rust import IndirectOffsetOnAxis

F32 = mybir.dt.float32
BF16 = mybir.dt.bfloat16
U32 = mybir.dt.uint32
AF = mybir.ActivationFunctionType
ALU = mybir.AluOpType
AXX = mybir.AxisListType.X

D = 1024
DC = 8
S = 4096
NBLK = 32
NCT = 8
CAP = 512
DFF = 2730
NH = 16
HD = 64
SCALE = 1024 ** -0.5
EPS_LN = 1e-5
EPS_RMS = 1e-6
NEG = -1.0e30
CANDK = 16
NCAND = 128 * CANDK
NCH = NCAND // 128
MYCH = NCH // 2
NPB = CAP // 128
MYPB = NPB // 2
FT_SIZES = [512, 512, 512, 512, 512, 170]
FT_OFF = [0, 512, 1024, 1536, 2048, 2560]
NFT = len(FT_SIZES)
INV_D = 1.0 / D
W16 = 16
DBG = False
NO_AR1 = False


def build_nc(stage=99):
    nc = bass.Bass("TRN2", target_bir_lowering=False, debug=False)
    ST1 = stage >= 1
    ST2 = stage >= 2
    ST3 = stage >= 3
    ST4 = stage >= 4
    ST5 = stage >= 5
    ST6 = stage >= 6

    x_tm = nc.dram_tensor("x_tm", [S, D], F32, kind="ExternalInput").ap()
    x_fm = nc.dram_tensor("x_fm", [D, S], F32, kind="ExternalInput").ap()
    Wq_d = nc.dram_tensor("Wq", [D, D], F32, kind="ExternalInput").ap()
    Wk_d = nc.dram_tensor("Wk", [D, D], F32, kind="ExternalInput").ap()
    WvT_d = nc.dram_tensor("WvT", [D, D], F32, kind="ExternalInput").ap()
    Wr_d = nc.dram_tensor("Wr", [D, D], F32, kind="ExternalInput").ap()
    WqkvT_d = nc.dram_tensor("WqkvT", [D, 3 * D], F32, kind="ExternalInput").ap()
    WoutT_d = nc.dram_tensor("WoutT", [D, D], F32, kind="ExternalInput").ap()
    W1T_d = nc.dram_tensor("W1T", [D, DFF], F32, kind="ExternalInput").ap()
    W2T_d = nc.dram_tensor("W2T", [D, DFF], F32, kind="ExternalInput").ap()
    W3T_d = nc.dram_tensor("W3T", [DFF, D], F32, kind="ExternalInput").ap()
    wrout_d = nc.dram_tensor("w_router", [1, D], F32, kind="ExternalInput").ap()
    g1_d = nc.dram_tensor("g1", [1, D], F32, kind="ExternalInput").ap()
    g2_d = nc.dram_tensor("g2", [1, D], F32, kind="ExternalInput").ap()
    ident_d = nc.dram_tensor("ident_bf", [128, 128], BF16, kind="ExternalInput").ap()
    idf32_d = nc.dram_tensor("ident_f32", [128, 128], F32, kind="ExternalInput").ap()
    ones_d = nc.dram_tensor("ones_bf", [128, 1], BF16, kind="ExternalInput").ap()
    iota64_d = nc.dram_tensor("iota64", [128, 1], F32, kind="ExternalInput").ap()
    maskf_d = nc.dram_tensor("maskffn", [128, MYPB * NPB * 128], BF16, kind="ExternalInput").ap()
    cinfu_d = nc.dram_tensor("cinfo_u", [1, 8], U32, kind="ExternalInput").ap()
    cinff_d = nc.dram_tensor("cinfo_f", [1, 8], F32, kind="ExternalInput").ap()

    out = nc.dram_tensor("out", [S, D], F32, kind="ExternalOutput").ap()
    if DBG:
        d_gq = nc.dram_tensor("d_gq", [1, D], F32, kind="ExternalOutput").ap()
        d_gk = nc.dram_tensor("d_gk", [1, D], F32, kind="ExternalOutput").ap()
        d_sc = nc.dram_tensor("d_sc", [1, 2 * S], F32, kind="ExternalOutput").ap()
        d_oi = nc.dram_tensor("d_oi", [CAP, 1], U32, kind="ExternalOutput").ap()
        d_q = nc.dram_tensor("d_q", [S, D], BF16, kind="ExternalOutput").ap()

    q_dram = nc.dram_tensor("q_dram", [S, D], BF16, kind="Internal").ap()
    sx_dram = nc.dram_tensor("sx_dram", [NBLK, 128], F32, kind="Internal").ap()
    sq_dram = nc.dram_tensor("sq_dram", [NBLK, 128], F32, kind="Internal").ap()
    csq_dram = nc.dram_tensor("csq_dram", [1, D], F32, kind="Internal").ap()
    csk_dram = nc.dram_tensor("csk_dram", [1, D], F32, kind="Internal").ap()
    csv_dram = nc.dram_tensor("csv_dram", [1, D], F32, kind="Internal").ap()
    sumw_dram = nc.dram_tensor("sumw_dram", [1, 1], F32, kind="Internal").ap()
    candv_dram = nc.dram_tensor("candv_dram", [1, NCAND], F32, kind="Internal").ap()
    candt_dram = nc.dram_tensor("candt_dram", [1, NCAND], U32, kind="Internal").ap()
    sums_dram = nc.dram_tensor("sums_dram", [2, NH * MYPB * 128], F32, kind="Internal").ap()
    ln1_loc = nc.dram_tensor("ln1_loc", [S, D], BF16, kind="Internal").ap()
    wqkvT_bf = nc.dram_tensor("wqkvT_bf", [D, 3 * D], BF16, kind="Internal").ap()
    woutT_bf = nc.dram_tensor("woutT_bf", [D, D], BF16, kind="Internal").ap()

    cc_w_i = nc.dram_tensor("cc_w_i", [1, 8], F32, kind="Internal").ap()
    cc_w_o = nc.dram_tensor("cc_w_o", [1, 8], F32, kind="Internal").ap()
    cc_gq_i = nc.dram_tensor("cc_gq_i", [1, D], F32, kind="Internal").ap()
    cc_gq_o = nc.dram_tensor("cc_gq_o", [1, D], F32, kind="Internal").ap()
    cc_gk_i = nc.dram_tensor("cc_gk_i", [1, D], F32, kind="Internal").ap()
    cc_gk_o = nc.dram_tensor("cc_gk_o", [1, D], F32, kind="Internal").ap()
    cc_sc_i = nc.dram_tensor("cc_sc_i", [1, 2 * S], F32, kind="Internal").ap()
    cc_sc_o = nc.dram_tensor("cc_sc_o", [1, 2 * S], F32, kind="Internal").ap()
    cc_b_i = nc.dram_tensor("cc_b_i", [1, 8], F32, kind="Internal").ap()
    cc_b_o = nc.dram_tensor("cc_b_o", [1, 8], F32, kind="Internal").ap()

    ln1_sh = nc.dram_tensor("ln1_sh", [2 * S, D], BF16, kind="Internal",
                            addr_space="Shared").ap()
    ordidx_sh = nc.dram_tensor("ordidx_sh", [CAP, 1], U32, kind="Internal",
                               addr_space="Shared").ap()
    kv_sh = nc.dram_tensor("kv_sh", [CAP, 2 * D], BF16, kind="Internal",
                           addr_space="Shared").ap()
    delta_sh = nc.dram_tensor("delta_sh", [CAP, D], F32, kind="Internal",
                              addr_space="Shared").ap()

    rg = [[0, 1], [2, 3], [4, 5], [6, 7]]

    es0 = ExitStack()
    esA = ExitStack()
    esA2 = ExitStack()
    esP3 = ExitStack()
    esT = ExitStack()
    esF = ExitStack()
    esF2 = ExitStack()
    esW = ExitStack()

    def sbt(stk, name, shape, dt):
        return stk.enter_context(nc.sbuf_tensor(name, shape, dt))

    sem = lambda name: es0.enter_context(nc.semaphore(name))

    id_sb = sbt(es0, "id_sb", [128, 128], BF16)
    id32_sb = sbt(es0, "id32_sb", [128, 128], F32)
    ones_sb = sbt(es0, "ones_sb", [128, 1], BF16)
    iota64_sb = sbt(es0, "iota64_sb", [128, 1], F32)
    maskf_sb = sbt(es0, "maskf_sb", [128, MYPB * NPB * 128], BF16)
    cinfu_sb = sbt(es0, "cinfu_sb", [1, 8], U32)
    cinfob = sbt(es0, "cinfob", [128, 8], F32)
    wrb = sbt(es0, "wrb", [128, D], F32)
    sumwb = sbt(es0, "sumwb", [128, 1], F32)
    g1b = sbt(es0, "g1b", [128, D], BF16)
    g2b = sbt(es0, "g2b", [128, D], BF16)
    sumw_row = sbt(es0, "sumw_row", [1, 1], F32)
    zrow = sbt(es0, "zrow", [128, 64], F32)
    zidx = sbt(es0, "zidx", [4, 128], U32)
    epsln_c = sbt(es0, "epsln_c", [128, 1], F32)
    epsrms_c = sbt(es0, "epsrms_c", [128, 1], F32)
    csrow = sbt(es0, "csrow", [1, D], F32)
    cskrow = sbt(es0, "cskrow", [1, D], F32)
    gqrow = sbt(es0, "gqrow", [1, D], F32)
    grow = gqrow
    gqb = sbt(es0, "gqb", [128, D], BF16)
    gksc = sbt(es0, "gksc", [128, DC], F32)
    s_all = sbt(es0, "s_all", [128, NBLK], F32)
    c_all = sbt(es0, "c_all", [128, NBLK], F32)
    sc_all = sbt(es0, "sc_all", [128, NBLK], F32)
    scT = sbt(es0, "scT", [32, 128], F32)
    sxrow = sbt(es0, "sxrow", [1, 512], F32)
    sqrow = sbt(es0, "sqrow", [1, 512], F32)
    mean_t = [sbt(es0, f"mean_t{j}", [128, 1], F32) for j in range(3)]
    e2_t = [sbt(es0, f"e2_t{j}", [128, 1], F32) for j in range(3)]
    var_t = [sbt(es0, f"var_t{j}", [128, 1], F32) for j in range(3)]
    std_t = [sbt(es0, f"std_t{j}", [128, 1], F32) for j in range(3)]
    sume = [sbt(es0, f"sume{j}", [128, 1], F32) for j in range(2)]
    recip_t = [sbt(es0, f"recip{j}", [128, 1], F32) for j in range(2)]
    recqt = [sbt(es0, f"recqt{j}", [128, 1], BF16) for j in range(3)]
    sx_bl = [sbt(es0, f"sx_bl{j}", [128, 1], F32) for j in range(2)]
    sq_bl = [sbt(es0, f"sq_bl{j}", [128, 1], F32) for j in range(2)]
    sxw_t = [sbt(es0, f"sxw_t{j}", [128, 1], F32) for j in range(3)]
    sx1_t = [sbt(es0, f"sx1_t{j}", [128, 1], F32) for j in range(3)]
    sx1q_t = [sbt(es0, f"sx1q_t{j}", [128, 1], F32) for j in range(3)]
    s1_t = [sbt(es0, f"s1_t{j}", [128, 1], F32) for j in range(3)]
    c1_t = [sbt(es0, f"c1_t{j}", [128, 1], F32) for j in range(3)]
    sp_t = [sbt(es0, f"sp_t{j}", [128, 1], F32) for j in range(3)]
    c1s_t = [sbt(es0, f"c1s_t{j}", [128, 1], F32) for j in range(3)]
    sbf_scr = sbt(es0, "sbf_scr", [128, D], BF16)
    ln1_t = [sbt(es0, f"ln1_t{j}", [128, D], BF16) for j in range(2)]

    xfm = sbt(esA, "xfm", [128, DC * S], BF16)
    wA = sbt(esA, "wA", [128, DC * D], BF16)
    wB = sbt(esA, "wB", [128, DC * D], BF16)
    wVT = sbt(esA, "wVT", [128, DC * D], BF16)
    wR = sbt(esA, "wR", [128, DC * D], BF16)
    csb = sbt(esA, "csb", [128, D], F32)
    qs_sb = [sbt(esA, f"qs_sb{j}", [128, D], F32) for j in range(3)]
    qfix = [sbt(esA, f"qfix{j}", [128, D], F32) for j in range(2)]
    sqs = sbt(esA2, "sqs", [128, DC * 512], BF16)
    qa = [sbt(esA2, f"qa{j}", [128, D], BF16) for j in range(3)]
    esA2.close()
    qf32 = [sbt(esP3, f"qf32{j}", [128, D], BF16) for j in range(2)]
    xtm_t = [sbt(esP3, f"xtm_t{j}", [128, D], F32) for j in range(3)]
    x1_t = xtm_t
    u1_sb = qs_sb
    u2_sb = qfix
    scw = qfix
    wexp = qs_sb
    esP3.close()
    esA.close()

    scfull = sbt(esT, "scfull", [128, 64], F32)
    ws_sb = sbt(esT, "ws_sb", [128, 64], F32)
    candv = sbt(esT, "candv", [128, CANDK], F32)
    candi = sbt(esT, "candi", [128, CANDK], U32)
    candif = sbt(esT, "candif", [128, CANDK], F32)
    candtu = sbt(esT, "candtu", [128, CANDK], U32)
    candrep = sbt(esT, "candrep", [128, NCAND], F32)
    cmp_scr = sbt(esT, "cmp_scr", [128, NCAND], BF16)
    tchall = sbt(esT, "tchall", [128, MYCH], F32)
    tokall = sbt(esT, "tokall", [128, MYCH], U32)
    rank_f = [sbt(esT, f"rank_f{j}", [128, 1], F32) for j in range(2)]
    rank_u = [sbt(esT, f"rank_u{j}", [128, 1], U32) for j in range(2)]
    esT.close()

    wout_sb = sbt(esF, "wout_sb", [128, DC * D], BF16)
    qkv_sb = [sbt(esF, f"qkv_sb{b}", [128, 3 * D], BF16) for b in range(MYPB)]
    xsel = [sbt(esF, f"xsel{b}", [128, D], BF16) for b in range(MYPB)]
    xself = [sbt(esF, f"xself{b}", [128, D], F32) for b in range(MYPB)]
    h_sb = [sbt(esF, f"h_sb{b}", [128, D], BF16) for b in range(MYPB)]
    sums_sb = sbt(esF, "sums_sb", [1, NH * MYPB * 128], F32)
    recqall = sbt(esF, "recqall", [128, 2 * NH], F32)
    o_sb = [sbt(esF, f"o_sb{b}", [128, D], BF16) for b in range(MYPB)]
    idxp = [sbt(esF, f"idxp{b}", [128, 1], U32) for b in range(NPB)]
    idxf = [sbt(esF, f"idxf{b}", [128, 1], F32) for b in range(NPB)]
    idxm = [sbt(esF, f"idxm{b}", [128, 1], F32) for b in range(NPB)]
    idxm2 = [sbt(esF, f"idxm2{b}", [128, 1], F32) for b in range(NPB)]
    idxu = [sbt(esF, f"idxu{b}", [128, 1], U32) for b in range(NPB)]
    ssq_t = [sbt(esF, f"ssq_t{b}", [128, 1], F32) for b in range(MYPB)]
    rms_t = [sbt(esF, f"rms_t{b}", [128, 1], F32) for b in range(MYPB)]
    xattn = [sbt(esF, f"xattn{b}", [128, D], F32) for b in range(MYPB)]
    wqkvq = sbt(esF2, "wqkvq", [128, DC * D], BF16)
    wqkvk = sbt(esF2, "wqkvk", [128, DC * D], BF16)
    wqkvv = sbt(esF2, "wqkvv", [128, DC * D], BF16)
    kvall = [sbt(esF2, f"kvall{b}", [128, 2 * D], BF16) for b in range(NPB)]
    kfm = sbt(esF2, "kfm", [128, 8 * NPB * 128], BF16)
    qfm = sbt(esF2, "qfm", [128, 8 * MYPB * 128], BF16)
    hfm = sbt(esF2, "hfm", [128, MYPB * D], BF16)
    attm = [sbt(esF2, f"attm{j}", [128, NPB * 128], F32) for j in range(3)]
    probsT = [sbt(esF2, f"probsT{j}", [128, NPB * 128], BF16) for j in range(3)]
    ofm = sbt(esF2, "ofm", [128, MYPB * D], BF16)
    esF2.close()

    wf1 = [sbt(esW, f"wf1_{j}", [128, DC * 512], BF16) for j in range(2)]
    wf2 = [sbt(esW, f"wf2_{j}", [128, DC * 512], BF16) for j in range(2)]
    wf3 = [sbt(esW, f"wf3_{j}", [128, 4 * D], BF16) for j in range(2)]
    s2_t = [sbt(esW, f"s2_t{b}", [128, D], F32) for b in range(MYPB)]
    s2q_t = [sbt(esW, f"s2q_t{b}", [128, 1], F32) for b in range(MYPB)]
    rms2_t = [sbt(esW, f"rms2_t{b}", [128, 1], F32) for b in range(MYPB)]
    h2_sb = [sbt(esW, f"h2_sb{b}", [128, D], BF16) for b in range(MYPB)]
    h2fm = sbt(esW, "h2fm", [128, MYPB * D], BF16)
    asil = sbt(esW, "asil", [128, 512], F32)
    g_sb = sbt(esW, "g_sb", [128, 512], BF16)
    gfm = sbt(esW, "gfm", [128, 512], BF16)
    delta_my = [sbt(esW, f"delta_my{b}", [128, D], F32) for b in range(MYPB)]
    delta_ld = [sbt(esW, f"delta_ld{b}", [128, D], F32) for b in range(NPB)]
    esW.close()

    pb = [es0.enter_context(nc.psum_tensor(f"pb{j}", [128, 512], F32)) for j in range(6)]
    pb6 = es0.enter_context(nc.psum_tensor("pb6", [128, 512], F32))
    pT = es0.enter_context(nc.psum_tensor("pT", [128, 1024], BF16))

    m_hw = sem("m_hw"); m_gx = sem("m_gx")
    m_wq = sem("m_wq"); m_wk = sem("m_wk"); m_wv = sem("m_wv")
    m_wr = sem("m_wr"); m_w12 = sem("m_w12"); m_wqkv = sem("m_wqkv")
    m_wout = sem("m_wout")
    m_wf = [sem("m_wf0"), sem("m_wf1")]
    m_st2 = [sem("m_st0"), sem("m_st1")]
    m_xt = [sem("m_xt0"), sem("m_xt1")]
    m_stc = sem("m_stc")
    m_sd = sem("m_sd")
    m_sp = [sem("m_sp0"), sem("m_sp1")]
    m_ln = [sem("m_ln0"), sem("m_ln1")]
    m_hb2 = [sem("m_hb0"), sem("m_hb1")]
    m_gst2 = [sem("m_gst0"), sem("m_gst1")]
    m_gq3 = sem("m_gq3"); m_gi = sem("m_gi"); m_gm = sem("m_gm")
    m_lncp = sem("m_lncp"); m_pc = sem("m_pc")
    m_t = sem("m_t"); m_v = sem("m_v"); m_s = sem("m_s"); m_cc = sem("m_cc")
    m_g = sem("m_g")
    m_vx = sem("m_vx"); m_sx = sem("m_sx"); m_gx2 = sem("m_gx2")

    class ChainProxy:
        _PASS = {"wait_ge", "value_load", "dma_start", "indirect_dma_start",
                 "collective_compute", "then_inc", "nop"}

        def __init__(self, eng, semh):
            self._e = eng
            self._sem = semh
            self._n = 0

        def __getattr__(self, name):
            fn = getattr(self._e, name)
            if name in ChainProxy._PASS or name.startswith("_"):
                return fn

            def call(*a, **kw):
                ins = fn(*a, **kw)
                try:
                    ins.then_inc(self._sem, 1)
                except Exception:
                    return ins
                self._n += 1
                self._e.wait_ge(self._sem, self._n)
                return _MarkShim(self._e)
            return call

    class _MarkShim:
        def __init__(self, eng):
            self._e = eng

        def then_inc(self, semh, n):
            self._e.sem_inc(semh, n)
            return self

    T_P0 = lambda ct: 2 * (ct + 1)
    T_P0_END = 2 * NCT
    T_CSQ = T_P0_END + 1
    T_CSK = T_P0_END + 2
    T_P1Q = lambda i: T_CSK + (i + 1 if i < 3 else 2 * i - 2)
    T_P1G = lambda i: T_CSK + (2 * i + 5 if i < NBLK - 3 else NBLK + 1 + i)
    T_P1_END = T_CSK + 2 * NBLK
    T_P2Q = lambda i: T_P1_END + (i + 1 if i < 3 else 2 * i - 2)
    T_P2G = lambda i: T_P1_END + (2 * i + 5 if i < NBLK - 3 else NBLK + 1 + i)
    T_P2_END = T_P1_END + 2 * NBLK
    T_WVR = lambda u: T_P2_END + u + 1
    T_CSV = T_P2_END + 17
    T_P3Q = lambda i: T_CSV + i + 1
    T_SCT = T_CSV + NBLK + 1
    T_P3_END = T_SCT
    T_FH = lambda b: T_P3_END + b + 1
    T_QKV = lambda b, j: T_P3_END + 2 + 6 * b + j + 1
    T_QKV_END = T_P3_END + 14
    T_KT = lambda kb: T_QKV_END + kb + 1
    T_QT = lambda b: T_QKV_END + 4 + b + 1
    T_KQ_END = T_QKV_END + 6
    T_ATT = lambda h, b: T_KQ_END + (
        (2 * h + b + 1) if 2 * h + b < 2 else 3 * (2 * h + b) - 3)
    T_SUM = lambda h, b: T_KQ_END + (
        (3 * (2 * h + b) + 4) if 2 * h + b < 30 else 93 + 2 * (2 * h + b - 30))
    T_O = lambda h, b: T_KQ_END + (
        (3 * (2 * h + b) + 5) if 2 * h + b < 30 else 94 + 2 * (2 * h + b - 30))
    T_AT_END = T_KQ_END + 6 * NH
    T_OT = lambda b: T_AT_END + b + 1
    T_XA = lambda b: T_AT_END + 2 + b + 1
    T_H2T = lambda b: T_AT_END + 4 + b + 1
    T_XA_END = T_AT_END + 6
    T_SW = lambda ft, b, k: T_XA_END + 4 * (ft * MYPB + b) + k
    T_SW_END = T_XA_END + 4 * MYPB * NFT
    T_END = T_SW_END

    S_P0 = lambda ct: 8 * (ct + 1)
    S_P0_END = 8 * NCT
    S_P1 = lambda i, k: S_P0_END + (
        (1 if i == 0 else 3 * i) if k == 1 else
        (2 if i == 0 else 3 * i + 1) if k == 2 else
        (3 * i + 5 if i < NBLK - 1 else 3 * NBLK))
    S_P1_END = S_P0_END + 3 * NBLK
    S_P2 = lambda i, k: S_P1_END + (
        (1 if i == 0 else 2 * i) if k == 1 else
        (2 * i + 3 if i < NBLK - 1 else 2 * NBLK))
    S_P2_END = S_P1_END + 2 * NBLK
    S_P3 = lambda i, k: S_P2_END + (
        (i + 1 if i < 2 else 5 * i - 6) if k == 1 else
        (5 * i + 3 if i < NBLK - 1 else 5 * NBLK - 3) if k == 2
        else (5 * i + 2 + k if i <= NBLK - 3 else
              (5 * NBLK - 9 + k if i == NBLK - 2 else 5 * NBLK - 5 + k)))
    S_P3_END = S_P2_END + 5 * NBLK
    S_XS = lambda b: S_P3_END + 2 * b + 1
    S_RSQ = lambda b: S_P3_END + 2 * b + 2
    S_XS_END = S_P3_END + 2 * MYPB
    S_EXP = lambda h, b: S_XS_END + 2 * h + b + 1
    S_EXP_END = S_XS_END + 2 * NH
    S_H2 = lambda b, k: S_EXP_END + 2 * b + k
    S_H2_END = S_EXP_END + 2 * MYPB
    S_SIL = lambda ft, b: S_H2_END + ft * MYPB + b + 1
    S_SIL_END = S_H2_END + MYPB * NFT
    S_END = S_SIL_END

    V_MZ = 1
    V_P0 = lambda ct: V_MZ + 2 * (ct + 1)
    V_P0_END = V_MZ + 2 * NCT
    V_CSQ = V_P0_END + 1
    V_P1 = lambda i, k: V_CSQ + (
        (k if i == 0 else 4 * i + k - 1) if k < 4 else
        (4 * i + 7 if i < NBLK - 1 else 4 * NBLK))
    V_P1_END = V_CSQ + 4 * NBLK
    V_GQ = V_P1_END + 1
    V_WKP = V_P1_END + 2
    V_CSK = V_P1_END + 3
    V_P2 = lambda i, k: V_CSK + (
        (1 if i == 0 else 2 * i) if k == 1 else
        (2 * i + 3 if i < NBLK - 1 else 2 * NBLK))
    V_P2_END = V_CSK + 2 * NBLK
    V_GK = V_P2_END + 1
    V_WRP = V_P2_END + 2
    V_WVR = lambda u: V_WRP + u + 1
    V_CSV = V_WRP + 17
    V_P3 = lambda i, k: V_CSV + 5 * i + k
    V_P3_END = V_CSV + 5 * NBLK
    V_SCT = V_P3_END + 1
    V_TK = lambda k: V_SCT + k
    V_TK_END = V_SCT + 6
    V_RK = lambda j: V_TK_END + j + 1
    V_RK_END = V_TK_END + MYCH
    V_XS = lambda b: V_RK_END + b + 1
    V_XS_END = V_RK_END + MYPB
    V_FH = lambda b: V_XS_END + b + 1
    V_QKV = lambda b, j: V_XS_END + 2 + 6 * b + j + 1
    V_QKV_END = V_XS_END + 14
    V_KT = lambda kb: V_QKV_END + kb + 1
    V_QT = lambda b: V_QKV_END + 4 + b + 1
    V_KQ_END = V_QKV_END + 6
    V_MSK = lambda h, b: V_KQ_END + (
        (2 * h + b + 1) if 2 * h + b < 2 else 3 * (2 * h + b) - 3)
    V_SUMC = lambda h, b: V_KQ_END + (
        (3 * (2 * h + b) + 4) if 2 * h + b < 30 else 93 + 2 * (2 * h + b - 30))
    V_OE = lambda h, b: V_KQ_END + (
        (3 * (2 * h + b) + 5) if 2 * h + b < 30 else 94 + 2 * (2 * h + b - 30))
    V_AT_END = V_KQ_END + 6 * NH
    V_ON = lambda b: V_AT_END + b + 1
    V_OT = lambda b: V_AT_END + 2 + b + 1
    V_XAE = lambda b: V_AT_END + 4 + b + 1
    V_S2 = lambda b: V_AT_END + 6 + b + 1
    V_H2 = lambda b: V_AT_END + 8 + b + 1
    V_H2T = lambda b: V_AT_END + 10 + b + 1
    V_H2_END = V_AT_END + 12
    V_SW = lambda ft, b, k: V_H2_END + 2 * (ft * MYPB + b) + k
    V_SW_END = V_H2_END + 2 * MYPB * NFT
    V_XFF = lambda b: V_SW_END + b + 1
    V_XFF_END = V_SW_END + MYPB
    V_FIL = lambda b: V_XFF_END + b + 1
    V_FIL_END = V_XFF_END + NPB
    V_END = V_FIL_END

    G_P3 = lambda i: 2 * i + 1
    G_SC = lambda i: 2 * i + 2
    G_END = NBLK

    CC_WARM, CC_GQ, CC_GK, CC_SC, CC_ORD, CC_KV, CC_DEL = 1, 2, 3, 4, 5, 6, 7

    HW_CONST = 12 * W16

    STW = lambda i: 32 * (i // 2 + 1)
    XTW = lambda i: 16 * (i // 2 + 1)
    SPW = lambda ct: 32 * (ct // 2 + 1)
    LNW = lambda i: 16 * (i // 2 + 1)
    HBW = lambda i: 16 * (i // 2 + 1)
    GSTW = lambda i: 16 * (i // 2 + 1)
    LN_FULL = 16 * (NBLK // 2)
    HB_FULL = 16 * (NBLK // 2)

    STC_GKC = 16
    STC_SCF = 32
    STC_CREP = 48
    STC_TCH = 64
    STC_TOK = 80
    STC_IDXP = lambda b: 80 + 16 * (b + 1)
    STC_IDXP_END = 80 + 16 * MYPB
    STC_KV = lambda kb: STC_IDXP_END + 16 * (kb + 1)
    STC_KV_END = STC_IDXP_END + 16 * NPB
    STC_RQA = STC_KV_END + 16
    STC_DEL = lambda b: STC_RQA + 16 * (b + 1)
    STC_DEL_END = STC_RQA + 16 * NPB
    STC_IDXA = lambda b: STC_DEL_END + 16 * (b + 1)
    STC_END = STC_DEL_END + 16 * NPB

    SD_CSQ = 16
    SD_SUMW = 32
    SD_CSB1 = 48
    SD_SWB = 64
    SD_GQ = 80
    SD_CSK = 96
    SD_CSB2 = 112
    SD_GK = 128
    SD_CSV = 144
    SD_CSB3 = 160
    SD_SC = 176
    SD_CANDV = 192
    SD_CANDT = 208
    SD_KV = lambda b: 208 + 16 * (b + 1)
    SD_KV_END = 208 + 16 * MYPB
    SD_SUMSA = SD_KV_END + 16
    SD_DEL = lambda b: SD_SUMSA + 16 * (b + 1)
    SD_DEL_END = SD_SUMSA + 16 * MYPB
    SD_END = SD_DEL_END

    GX_END = 16 * NCT
    W_FULL = 16 * 8
    W12_FULL = 32
    WQKV_FULL = 16 * 24
    _ft_dmas = [8 + 8 + ((fs + 127) // 128) for fs in FT_SIZES]
    _wf_cum = [[], []]
    for _j, _d in enumerate(_ft_dmas):
        _wf_cum[_j % 2].append((sum(_wf_cum[_j % 2][-1:]) if False else 0))
    _wf_tot = [0, 0]
    _wf_at = []
    for _j, _d in enumerate(_ft_dmas):
        _wf_tot[_j % 2] += 16 * _d
        _wf_at.append(_wf_tot[_j % 2])
    WF_AT = lambda j: _wf_at[j]
    GQ3 = lambda i: 16 * (i + 1)
    GQ3_END = 16 * NBLK
    GM_GQB = 16
    GM_GQR = 32
    GI_OZ = 16
    GI_RS = lambda j: GI_OZ + 16 * (j + 1)
    GI_RS_END = GI_OZ + 16 * MYCH
    GI_GA = lambda b: GI_RS_END + 16 * (b + 1)
    GI_GA_END = GI_RS_END + 16 * MYPB
    GI_FS = lambda b: GI_GA_END + 16 * (b + 1)
    GI_FS_END = GI_GA_END + 16 * NPB

    cnt = {}
    def mk(inst, semh, n=1, expect=None):
        key = semh.name
        cnt[key] = cnt.get(key, 0) + n
        if expect is not None:
            assert cnt[key] == expect, (key, cnt[key], expect)
        return inst.then_inc(semh, n)

    blk = es0.enter_context(nc.Block())

    def wchunk(t, c, n=D):
        return t[:, c * n:(c + 1) * n]

    @blk.sync
    def _(e):
        e.dma_start(out=id_sb[:, :], in_=ident_d[:, :]).then_inc(m_hw, W16)
        e.dma_start(out=id32_sb[:, :], in_=idf32_d[:, :]).then_inc(m_hw, W16)
        e.dma_start(out=ones_sb[:, :], in_=ones_d[:, :]).then_inc(m_hw, W16)
        e.dma_start(out=iota64_sb[:, :], in_=iota64_d[:, :]).then_inc(m_hw, W16)
        e.wait_ge(m_hw, 4 * W16)
        e.dma_start(out=maskf_sb[:, :], in_=maskf_d[:, :]).then_inc(m_hw, W16)
        e.dma_start(out=cinfu_sb[:, :], in_=cinfu_d[:, :]).then_inc(m_hw, W16)
        e.dma_start(out=cinfob[:, :], in_=cinff_d[0:1, :].to_broadcast([128, 8])).then_inc(m_hw, W16)
        e.dma_start(out=wrb[:, :], in_=wrout_d[0:1, :].to_broadcast([128, D])).then_inc(m_hw, W16)
        e.dma_start(out=scT[0:1, 0:1], in_=cinff_d[0:1, 1:2]).then_inc(m_hw, W16)
        e.wait_ge(m_v, V_MZ)
        e.dma_start(out=cc_sc_i.rearrange("a (p c) -> (a p) c", p=128),
                    in_=zrow[:, :]).then_inc(m_hw, W16)
        e.dma_start(out=cc_w_i[:, :], in_=zrow[0:1, 0:8]).then_inc(m_hw, W16)
        e.dma_start(out=cc_b_i[:, :], in_=zrow[0:1, 0:8]).then_inc(m_hw, W16)
        cnt["m_hw"] = 12 * W16

        def stc(inst, expect):
            mk(inst, m_stc, W16, expect=expect)
            e.wait_ge(m_stc, expect)

        if not ST1:
            e.wait_ge(m_sp[0], SPW(NCT - 2))
            e.wait_ge(m_sp[1], SPW(NCT - 1))
            if DBG:
                e.wait_ge(m_v, V_CSQ)
                e.dma_start(out=d_gq[:, :], in_=csq_dram[:, :]).then_inc(m_hw, W16)
                e.wait_ge(m_hw, HW_CONST + 16)
            return
        for i in range(NBLK):
            ct = i // 4
            e.wait_ge(m_sp[ct % 2], SPW(ct))
            if i >= 2:
                e.wait_ge(m_v, V_P1(i - 2, 2))
            mk(e.dma_start(out=sx_bl[i % 2][:, :],
                           in_=sx_dram[i:i + 1, :].rearrange("a (p c) -> (a p) c", c=1)),
               m_st2[i % 2], W16)
            mk(e.dma_start(out=sq_bl[i % 2][:, :],
                           in_=sq_dram[i:i + 1, :].rearrange("a (p c) -> (a p) c", c=1)),
               m_st2[i % 2], W16, expect=STW(i))
        if not ST2:
            if NO_AR1:
                e.wait_ge(m_gst2[0], GSTW(NBLK - 2))
                e.wait_ge(m_gst2[1], GSTW(NBLK - 1))
                e.wait_ge(m_t, T_P1_END)
            else:
                e.wait_ge(m_cc, CC_GQ)
            if DBG:
                e.dma_start(out=d_gq[:, :], in_=cc_gq_o[:, :]).then_inc(m_hw, W16)
                e.wait_ge(m_gst2[0], GSTW(NBLK - 2))
                e.wait_ge(m_gst2[1], GSTW(NBLK - 1))
                e.dma_start(out=d_q[:, :], in_=q_dram[:, :]).then_inc(m_hw, W16)
                e.wait_ge(m_hw, HW_CONST + 32)
            return
        e.wait_ge(m_cc, CC_GK)
        stc(e.dma_start(out=gksc[:, :],
                        in_=cc_gk_o.rearrange("a (c p) -> (a p) c", p=128),
                        allow_slow_non_contiguous=True), STC_GKC)
        par_c = e.value_load(cinfu_sb[0:1, 3:4])
        mylncp = ln1_sh.rearrange("(o r) d -> o (r d)", o=2)[bass.ts(par_c, 1), :]
        ln1f = ln1_loc.rearrange("r d -> (r d)")
        BLKE = 128 * D
        LNCH = [(0, 14), (14, 8), (22, 8), (30, 2)]

        def lncp(k):
            b0, nb = LNCH[k]
            e.wait_ge(m_ln[0], 16 * ((b0 + nb + 1) // 2))
            e.wait_ge(m_ln[1], 16 * ((b0 + nb + 1) // 2))
            mk(e.dma_start(out=mylncp[:, b0 * BLKE:(b0 + nb) * BLKE],
                           in_=ln1f[b0 * BLKE:(b0 + nb) * BLKE]), m_lncp, W16)

        for i in range(NBLK):
            if i >= 3:
                e.wait_ge(m_g, G_SC(i - 3))
                e.wait_ge(m_hb2[(i - 1) % 2], HBW(i - 3))
            mk(e.dma_start(out=xtm_t[i % 3][:, :],
                           in_=x_tm[128 * i:128 * (i + 1), :]),
               m_xt[i % 2], W16, expect=XTW(i))
            if i == 17:
                lncp(0)
            elif i == 25:
                lncp(1)
            if i >= 1:
                e.wait_ge(m_s, S_P3(i - 1, 5))
                mk(e.dma_start(out=out[128 * (i - 1):128 * i, :],
                               in_=xtm_t[(i - 1) % 3][:, :]),
                   m_hb2[(i - 1) % 2], W16, expect=HBW(i - 1))
        lncp(2)
        lncp(3)
        e.wait_ge(m_s, S_P3(NBLK - 1, 5))
        mk(e.dma_start(out=out[128 * (NBLK - 1):128 * NBLK, :],
                       in_=xtm_t[(NBLK - 1) % 3][:, :]),
           m_hb2[(NBLK - 1) % 2], W16, expect=HBW(NBLK - 1))
        if not ST3:
            e.wait_ge(m_sd, SD_SC)
            e.wait_ge(m_hb2[0], HB_FULL)
            e.wait_ge(m_hb2[1], HB_FULL)
            if DBG:
                e.dma_start(out=d_gq[:, :], in_=cc_gq_o[:, :]).then_inc(m_hw, W16)
                e.dma_start(out=d_gk[:, :], in_=cc_gk_o[:, :]).then_inc(m_hw, W16)
                e.dma_start(out=d_sc[:, :], in_=cc_sc_i[:, :]).then_inc(m_hw, W16)
                e.wait_ge(m_hw, HW_CONST + 48)
            return
        e.wait_ge(m_cc, CC_SC)
        stc(e.dma_start(out=scfull[:, :],
                        in_=cc_sc_o.rearrange("a (p c) -> (a p) c", p=128)),
            STC_SCF)
        e.wait_ge(m_sd, SD_CANDT)
        stc(e.dma_start(out=candrep[:, :],
                        in_=candv_dram[0:1, :].to_broadcast([128, NCAND])),
            STC_CREP)
        par_r = e.value_load(cinfu_sb[0:1, 3:4])
        cvv = candv_dram.rearrange("a (o c p) -> (a o) p c", o=2, p=128)[bass.ts(par_r, 1), :, :]
        ctv = candt_dram.rearrange("a (o c p) -> (a o) p c", o=2, p=128)[bass.ts(par_r, 1), :, :]
        stc(e.dma_start(out=tchall[:, :],
                        in_=cvv, allow_slow_non_contiguous=True), STC_TCH)
        stc(e.dma_start(out=tokall[:, :],
                        in_=ctv, allow_slow_non_contiguous=True), STC_TOK)
        if not ST4:
            e.wait_ge(m_cc, CC_ORD)
            if DBG:
                e.dma_start(out=d_oi[:, :], in_=ordidx_sh[:, :]).then_inc(m_hw, W16)
                e.dma_start(out=d_sc[:, :], in_=cc_sc_o[:, :]).then_inc(m_hw, W16)
                e.wait_ge(m_hw, HW_CONST + 32)
            return
        ordv = ordidx_sh.rearrange("(o r) b -> o r b", o=2)[bass.ts(par_r, 1), :, :]
        e.wait_ge(m_cc, CC_ORD)
        for b in range(MYPB):
            stc(e.dma_start(out=idxp[b][:, :],
                            in_=ordv[:, 128 * b:128 * (b + 1), :]), STC_IDXP(b))
        e.wait_ge(m_cc, CC_KV)
        for kb in range(NPB):
            stc(e.dma_start(out=kvall[kb][:, :],
                            in_=kv_sh[128 * kb:128 * (kb + 1), :]), STC_KV(kb))
        if not ST5:
            e.wait_ge(m_v, V_KT(NPB - 1))
            e.wait_ge(m_sd, SD_KV_END)
            return
        smv = sums_dram.rearrange("o (w p) -> o p w", p=128)[bass.ts(par_r, 1), :, :]
        e.wait_ge(m_sd, SD_SUMSA)
        stc(e.dma_start(out=recqall[:, :], in_=smv,
                        allow_slow_non_contiguous=True), STC_RQA)
        if not ST6:
            e.wait_ge(m_v, V_S2(MYPB - 1))
            e.wait_ge(m_s, S_H2(MYPB - 1, 2))
            return
        e.wait_ge(m_cc, CC_DEL)
        for b in range(NPB):
            stc(e.dma_start(out=delta_ld[b][:, :],
                            in_=delta_sh[128 * b:128 * (b + 1), :]), STC_DEL(b))
        for b in range(NPB):
            stc(e.dma_start(out=idxp[b][:, :],
                            in_=ordidx_sh[128 * b:128 * (b + 1), :]), STC_IDXA(b))
        e.wait_ge(m_gi, GI_FS_END)

    @blk.scalar
    def _(e):
        e = ChainProxy(e, m_sx)
        e.wait_ge(m_hw, HW_CONST)
        par_s = e.value_load(cinfu_sb[0:1, 3:4])
        myln1 = ln1_sh.rearrange("(o r) d -> o (r d)", o=2)[bass.ts(par_s, 1), :]
        mykv = kv_sh.rearrange("(o r) d -> o r d", o=2)[bass.ts(par_s, 1), :, :]
        mydel = delta_sh.rearrange("(o r) d -> o r d", o=2)[bass.ts(par_s, 1), :, :]
        mysums = sums_dram[bass.ts(par_s, 1), :]

        def sdw(inst, expect):
            mk(inst, m_sd, W16, expect=expect)
            e.wait_ge(m_sd, expect)


        def statw(ct):
            e.wait_ge(m_v, V_P0(ct))
            if ct >= 2:
                e.wait_ge(m_sp[ct % 2], SPW(ct - 2))
            mk(e.dma_start(out=sx_dram[4 * ct:4 * (ct + 1), :],
                           in_=sxrow[0:1, :]), m_sp[ct % 2], W16)
            mk(e.dma_start(out=sq_dram[4 * ct:4 * (ct + 1), :],
                           in_=sqrow[0:1, :]), m_sp[ct % 2], W16,
               expect=SPW(ct))

        for ct in range(NCT):
            e.wait_ge(m_gx, 16 * (ct + 1))
            if ct >= 1:
                e.wait_ge(m_t, T_P0(ct - 1))
            for c in range(DC):
                mk(e.activation(sqs[:, c * 512:(c + 1) * 512],
                                xfm[:, c * S + 512 * ct:c * S + 512 * (ct + 1)],
                                AF.Square), m_s, 1)
            assert cnt["m_s"] == S_P0(ct)
            if ct >= 1:
                statw(ct - 1)
        statw(NCT - 1)
        e.wait_ge(m_v, V_CSQ)
        sdw(e.dma_start(out=csq_dram[:, :], in_=csrow[:, :]), SD_CSQ)
        sdw(e.dma_start(out=sumw_dram[:, :], in_=sumw_row[:, :]), SD_SUMW)
        sdw(e.dma_start(out=csb[:, :],
                        in_=csq_dram[0:1, :].to_broadcast([128, D])), SD_CSB1)
        sdw(e.dma_start(out=sumwb[:, :],
                        in_=sumw_dram[0:1, :].to_broadcast([128, 1])), SD_SWB)
        if not ST1:
            return
        for i in range(NBLK):
            e.wait_ge(m_v, V_P1(i, 1))
            e.activation(std_t[i % 2][:, :], var_t[i % 2][:, :], AF.Ln,
                         bias=epsln_c[:, :])
            mk(e.activation(s_all[:, i:i + 1], std_t[i % 2][:, :], AF.Exp,
                            scale=-0.5), m_s, 1, expect=S_P1(i, 1))
            e.wait_ge(m_t, T_P1Q(i))
            if i >= 3:
                e.wait_ge(m_v, V_P1(i - 3, 4))
            for half in range(2):
                ins = e.activation(qs_sb[i % 3][:, 512 * half:512 * (half + 1)],
                                   pb[2 * (i % 2) + half][:, :], AF.Identity,
                                   scale=s_all[:, i:i + 1])
                if half:
                    mk(ins, m_s, 1, expect=S_P1(i, 2))
            if i >= 1:
                e.wait_ge(m_v, V_P1(i - 1, 3))
                mk(e.activation(wexp[(i - 1) % 3][:, :], qfix[(i - 1) % 2][:, :],
                                AF.Exp, scale=SCALE,
                                accum_out=sume[(i - 1) % 2][:, :]), m_s, 1,
                   expect=S_P1(i - 1, 3))
        e.wait_ge(m_v, V_P1(NBLK - 1, 3))
        mk(e.activation(wexp[(NBLK - 1) % 3][:, :], qfix[(NBLK - 1) % 2][:, :],
                        AF.Exp, scale=SCALE,
                        accum_out=sume[(NBLK - 1) % 2][:, :]), m_s, 1,
           expect=S_P1(NBLK - 1, 3))
        e.wait_ge(m_v, V_GQ)
        sdw(e.dma_start(out=cc_gq_i[:, :], in_=gqrow[:, :]), SD_GQ)
        if not ST2:
            return
        e.wait_ge(m_v, V_CSK)
        sdw(e.dma_start(out=csk_dram[:, :], in_=cskrow[:, :]), SD_CSK)
        sdw(e.dma_start(out=csb[:, :],
                        in_=csk_dram[0:1, :].to_broadcast([128, D])), SD_CSB2)
        for i in range(NBLK):
            e.wait_ge(m_t, T_P2Q(i))
            if i >= 3:
                e.wait_ge(m_v, V_P2(i - 3, 2))
            for half in range(2):
                ins = e.activation(qs_sb[i % 3][:, 512 * half:512 * (half + 1)],
                                   pb[2 * (i % 2) + half][:, :], AF.Identity,
                                   scale=s_all[:, i:i + 1])
                if half:
                    mk(ins, m_s, 1, expect=S_P2(i, 1))
            if i >= 1:
                e.wait_ge(m_v, V_P2(i - 1, 1))
                mk(e.activation(wexp[(i - 1) % 3][:, :], qfix[(i - 1) % 2][:, :],
                                AF.Exp, scale=SCALE,
                                accum_out=sume[(i - 1) % 2][:, :]), m_s, 1,
                   expect=S_P2(i - 1, 2))
        e.wait_ge(m_v, V_P2(NBLK - 1, 1))
        mk(e.activation(wexp[(NBLK - 1) % 3][:, :], qfix[(NBLK - 1) % 2][:, :],
                        AF.Exp, scale=SCALE,
                        accum_out=sume[(NBLK - 1) % 2][:, :]), m_s, 1,
           expect=S_P2(NBLK - 1, 2))
        e.wait_ge(m_v, V_GK)
        sdw(e.dma_start(out=cc_gk_i[:, :], in_=grow[:, :]), SD_GK)
        e.wait_ge(m_v, V_CSV)
        sdw(e.dma_start(out=csv_dram[:, :], in_=csrow[:, :]), SD_CSV)
        sdw(e.dma_start(out=csb[:, :],
                        in_=csv_dram[0:1, :].to_broadcast([128, D])), SD_CSB3)
        def p3sq(j):
            e.wait_ge(m_v, V_P3(j, 2))
            mk(e.activation(sbf_scr[:, :], x1_t[j % 3][:, :], AF.Square,
                            accum_out=sx1q_t[j % 3][:, :]), m_s, 1,
               expect=S_P3(j, 2))

        def p3tail(j):
            e.wait_ge(m_v, V_P3(j, 3))
            e.activation(std_t[j % 3][:, :], var_t[j % 3][:, :], AF.Ln,
                         bias=epsln_c[:, :])
            mk(e.activation(s1_t[j % 3][:, :], std_t[j % 3][:, :], AF.Exp,
                            scale=-0.5), m_s, 1, expect=S_P3(j, 3))
            e.wait_ge(m_v, V_P3(j, 4))
            if j >= 2:
                e.wait_ge(m_ln[j % 2], LNW(j - 2))
            mk(e.activation(ln1_t[j % 2][:, :], x1_t[j % 3][:, :], AF.Identity,
                            scale=s1_t[j % 3][:, :], bias=c1_t[j % 3][:, :]),
               m_s, 1, expect=S_P3(j, 4))
            mk(e.dma_start(out=ln1_loc[128 * j:128 * (j + 1), :],
                           in_=ln1_t[j % 2][:, :]), m_ln[j % 2], W16,
               expect=LNW(j))
            e.wait_ge(m_g, G_SC(j))
            mk(e.activation(xtm_t[j % 3][:, :], x1_t[j % 3][:, :], AF.Identity,
                            scale=sp_t[j % 3][:, :], bias=c1_t[j % 3][:, :]),
               m_s, 1, expect=S_P3(j, 5))

        for i in range(NBLK):
            e.wait_ge(m_t, T_P3Q(i))
            if i >= 2:
                e.wait_ge(m_v, V_P3(i - 2, 1))
            for half in range(2):
                ins = e.activation(u1_sb[i % 2][:, 512 * half:512 * (half + 1)],
                                   pb[2 * (i % 3) + half][:, :], AF.Identity,
                                   scale=s_all[:, i:i + 1])
                if half:
                    mk(ins, m_s, 1, expect=S_P3(i, 1))
            if i >= 2:
                p3tail(i - 2)
            if i >= 1:
                p3sq(i - 1)
        p3tail(NBLK - 2)
        p3sq(NBLK - 1)
        p3tail(NBLK - 1)
        e.wait_ge(m_v, V_SCT)
        scv = cc_sc_i.rearrange("a (r p) -> (a r) p", p=128)
        sdw(e.dma_start(out=scv[bass.ts(par_s, 32), :], in_=scT[:, :]), SD_SC)
        if not ST3:
            return
        e.wait_ge(m_v, V_TK_END)
        mk(e.dma_start(out=candv_dram.rearrange("a (p c) -> (a p) c", p=128),
                       in_=candv[:, :]), m_sd, W16, expect=SD_CANDV)
        sdw(e.dma_start(out=candt_dram.rearrange("a (p c) -> (a p) c", p=128),
                        in_=candtu[:, :]), SD_CANDT)
        e.wait_ge(m_pc, 16 * 16)
        qv = wqkvT_bf.rearrange("(c p) n -> p c n", p=128)
        for part in range(3):
            for c in range(DC):
                mk(e.dma_start(
                    out=wchunk([wqkvq, wqkvk, wqkvv][part], c),
                    in_=qv[:, c, part * D:(part + 1) * D]), m_wqkv, W16)
            e.wait_ge(m_wqkv, 128 * (part + 1))
        assert cnt["m_wqkv"] == WQKV_FULL
        e.wait_ge(m_pc, 24 * 16)
        e.wait_ge(m_v, V_RK(MYCH - 1))
        e.wait_ge(m_gi, GI_RS(MYCH - 1))
        ov = woutT_bf.rearrange("(c p) n -> p c n", p=128)
        for c in range(DC):
            mk(e.dma_start(out=wchunk(wout_sb, c), in_=ov[:, c, :]),
               m_wout, W16)
        assert cnt["m_wout"] == W_FULL
        if not ST4:
            return
        for b in range(MYPB):
            e.wait_ge(m_gi, GI_GA(b))
            e.activation(sbf_scr[:, :], xsel[b][:, :], AF.Square,
                         accum_out=ssq_t[b][:, :])
            mk(e.copy(out=xself[b][:, :], in_=xsel[b][:, :]), m_s, 1,
               expect=S_XS(b))
            e.activation(rms_t[b][:, :], ssq_t[b][:, :], AF.Ln,
                         scale=INV_D, bias=epsrms_c[:, :])
            mk(e.activation(ssq_t[b][:, :], rms_t[b][:, :], AF.Exp,
                            scale=-0.5), m_s, 1, expect=S_RSQ(b))
        for b in range(MYPB):
            e.wait_ge(m_v, V_QKV(b, 5))
            sdw(e.dma_start(out=mykv[:, 128 * b:128 * (b + 1), :],
                            in_=qkv_sb[b][:, D:3 * D]), SD_KV(b))
        if not ST5:
            return
        for h in range(NH):
            for b in range(MYPB):
                w = 2 * h + b
                e.wait_ge(m_v, V_MSK(h, b))
                if w >= 3:
                    e.wait_ge(m_t, T_O((w - 3) // 2, (w - 3) % 2))
                wd = 384 if b == 0 else 512
                mk(e.activation(probsT[w % 3][:, 0:wd], attm[w % 3][:, 0:wd],
                                AF.Exp, scale=0.125), m_s, 1,
                   expect=S_EXP(h, b))
        e.wait_ge(m_v, V_SUMC(NH - 1, MYPB - 1))
        sdw(e.dma_start(out=mysums, in_=sums_sb[0:1, :]), SD_SUMSA)
        for b in range(MYPB):
            e.wait_ge(m_v, V_S2(b))
            mk(e.activation(sbf_scr[:, :], s2_t[b][:, :], AF.Square,
                            accum_out=s2q_t[b][:, :]), m_s, 1, expect=S_H2(b, 1))
            e.activation(rms2_t[b][:, :], s2q_t[b][:, :], AF.Ln,
                         scale=INV_D, bias=epsrms_c[:, :])
            mk(e.activation(s2q_t[b][:, :], rms2_t[b][:, :], AF.Exp,
                            scale=-0.5), m_s, 1, expect=S_H2(b, 2))
        if not ST6:
            return
        for ft in range(NFT):
            for b in range(MYPB):
                fs = FT_SIZES[ft]
                e.wait_ge(m_t, T_SW(ft, b, 2))
                if (ft * MYPB + b) >= 1:
                    pi = ft * MYPB + b - 1
                    e.wait_ge(m_v, V_SW(pi // MYPB, pi % MYPB, 1))
                mk(e.activation(asil[:, 0:fs],
                                [pb[0], pb6][(ft * MYPB + b) % 2][:, 0:fs],
                                AF.Silu), m_s, 1, expect=S_SIL(ft, b))
        for b in range(MYPB):
            e.wait_ge(m_v, V_XFF(b))
            sdw(e.dma_start(out=mydel[:, 128 * b:128 * (b + 1), :],
                            in_=delta_my[b][:, :]), SD_DEL(b))

    @blk.vector
    def _(e):
        e = ChainProxy(e, m_vx)
        e.memset(zrow[:, :], 0.0)
        e.memset(epsln_c[:, :], EPS_LN)
        e.memset(epsrms_c[:, :], EPS_RMS)
        mk(e.memset(zidx[:, :], 0), m_v, 1, expect=V_MZ)
        for ct in range(NCT):
            e.wait_ge(m_t, T_P0(ct))
            if ct >= 1:
                e.wait_ge(m_sp[(ct - 1) % 2], SPW(ct - 1))
            mk(e.tensor_copy(sxrow[0:1, :], pb[4][0:1, :]), m_v, 1)
            mk(e.tensor_copy(sqrow[0:1, :], pb[5][0:1, :]), m_v, 1,
               expect=V_P0(ct))
        e.wait_ge(m_t, T_CSK)
        e.tensor_copy(csrow[0:1, 0:512], pb[4][0:1, :])
        e.tensor_copy(csrow[0:1, 512:1024], pb[5][0:1, :])
        e.tensor_copy(cskrow[0:1, 0:512], pb[0][0:1, :])
        e.tensor_copy(cskrow[0:1, 512:1024], pb[1][0:1, :])
        e.wait_ge(m_hw, HW_CONST)
        mk(e.reduce_sum(sumw_row[:, :], wrb[0:1, :], axis=AXX), m_v, 1,
           expect=V_CSQ)
        if not ST1:
            return

        def p1norm(j, vm=None):
            e.wait_ge(m_s, S_P1(j, 3) if vm is None else vm[0])
            if j >= 3:
                e.wait_ge(m_t, (T_P1G if vm is None else vm[1])(j - 3))
            with nc.allow_low_precision(reason="1/sum feeds a bf16 matmul"):
                e.reciprocal(recqt[j % 3][:, :], sume[j % 2][:, :])
            mk(e.tensor_mul(qa[j % 3][:, :], qfix[j % 2][:, :],
                            wexp[j % 3][:, :]), m_v, 1,
               expect=V_P1(j, 4) if vm is None else vm[2](j))

        for i in range(NBLK):
            e.wait_ge(m_st2[i % 2], STW(i))
            e.tensor_scalar_mul(mean_t[i % 2][:, :], sx_bl[i % 2][:, :], INV_D)
            e.tensor_scalar_mul(e2_t[i % 2][:, :], sq_bl[i % 2][:, :], INV_D)
            e.tensor_mul(var_t[i % 2][:, :], mean_t[i % 2][:, :],
                         mean_t[i % 2][:, :])
            mk(e.tensor_sub(var_t[i % 2][:, :], e2_t[i % 2][:, :],
                            var_t[i % 2][:, :]), m_v, 1, expect=V_P1(i, 1))
            e.wait_ge(m_s, S_P1(i, 1))
            mk(e.tensor_scalar(c_all[:, i:i + 1], mean_t[i % 2][:, :],
                               s_all[:, i:i + 1], -1.0, op0=ALU.mult,
                               op1=ALU.mult), m_v, 1, expect=V_P1(i, 2))
            e.wait_ge(m_s, S_P1(i, 2))
            e.wait_ge(m_sd, SD_CSB1)
            if i >= 2:
                e.wait_ge(m_s, S_P1(i - 2, 3))
                e.wait_ge(m_gst2[i % 2], GSTW(i - 2))
            mk(e.scalar_tensor_tensor(qfix[i % 2][:, :], csb[:, :],
                                      c_all[:, i:i + 1], qs_sb[i % 3][:, :],
                                      op0=ALU.mult, op1=ALU.add), m_v, 1,
               expect=V_P1(i, 3))
            if i >= 1:
                p1norm(i - 1)
        p1norm(NBLK - 1)
        e.wait_ge(m_t, T_P1G(NBLK - 1))
        e.tensor_copy(gqrow[0:1, 0:512], pb[4][0:1, :])
        mk(e.tensor_copy(gqrow[0:1, 512:1024], pb[5][0:1, :]), m_v, 1,
           expect=V_GQ)
        if not ST2:
            return
        e.wait_ge(m_gm, GM_GQB)
        e.wait_ge(m_wk, W_FULL)
        for c in range(DC):
            ins = e.tensor_mul(wchunk(wB, c), wchunk(wB, c), gqb[:, :])
            if c == DC - 1:
                mk(ins, m_v, 1, expect=V_WKP)
        e.wait_ge(m_gm, GM_GQR)
        mk(e.tensor_mul(cskrow[:, :], cskrow[:, :], gqrow[:, :]), m_v, 1,
           expect=V_CSK)
        for i in range(NBLK):
            e.wait_ge(m_s, S_P2(i, 1))
            e.wait_ge(m_sd, SD_CSB2)
            if i >= 2:
                e.wait_ge(m_s, S_P2(i - 2, 2))
            else:
                e.wait_ge(m_gst2[i % 2], GSTW(NBLK - 2 + i))
            mk(e.scalar_tensor_tensor(qfix[i % 2][:, :], csb[:, :],
                                      c_all[:, i:i + 1], qs_sb[i % 3][:, :],
                                      op0=ALU.mult, op1=ALU.add), m_v, 1,
               expect=V_P2(i, 1))
            if i >= 1:
                p1norm(i - 1, vm=(S_P2(i - 1, 2), T_P2G,
                                  lambda j: V_P2(j, 2)))
        p1norm(NBLK - 1, vm=(S_P2(NBLK - 1, 2), T_P2G,
                             lambda j: V_P2(j, 2)))
        e.wait_ge(m_t, T_P2G(NBLK - 1))
        e.tensor_copy(grow[0:1, 0:512], pb[4][0:1, :])
        mk(e.tensor_copy(grow[0:1, 512:1024], pb[5][0:1, :]), m_v, 1,
           expect=V_GK)
        e.wait_ge(m_stc, STC_GKC)
        e.wait_ge(m_wr, W_FULL)
        for c in range(DC):
            ins = e.tensor_scalar(wchunk(wR, c), wchunk(wR, c),
                                  gksc[:, c:c + 1], None, op0=ALU.mult)
            if c == DC - 1:
                mk(ins, m_v, 1, expect=V_WRP)
        for u in range(16):
            m, jh = divmod(u, 2)
            e.wait_ge(m_t, T_WVR(u))
            mk(e.tensor_copy(wA[:, m * D + jh * 512:m * D + (jh + 1) * 512],
                             pb[u % 4][:, :]), m_v, 1, expect=V_WVR(u))
        e.wait_ge(m_t, T_CSV)
        e.tensor_copy(csrow[0:1, 0:512], pb[4][0:1, :])
        mk(e.tensor_copy(csrow[0:1, 512:1024], pb[5][0:1, :]), m_v, 1,
           expect=V_CSV)
        for i in range(NBLK):
            e.wait_ge(m_s, S_P3(i, 1))
            e.wait_ge(m_sd, SD_CSB3)
            if i >= 2:
                e.wait_ge(m_g, G_SC(i - 2))
            mk(e.scalar_tensor_tensor(u2_sb[i % 2][:, :], csb[:, :],
                                      c_all[:, i:i + 1], u1_sb[i % 2][:, :],
                                      op0=ALU.mult, op1=ALU.add), m_v, 1,
               expect=V_P3(i, 1))
            e.wait_ge(m_g, G_P3(i))
            mk(e.scalar_tensor_tensor(x1_t[i % 3][:, :], xtm_t[i % 3][:, :],
                                      1.0, u2_sb[i % 2][:, :], op0=ALU.mult,
                                      op1=ALU.add,
                                      accum_out=sx1_t[i % 3][:, :]), m_v, 1,
               expect=V_P3(i, 2))
            e.wait_ge(m_s, S_P3(i, 2))
            e.tensor_scalar_mul(mean_t[i % 3][:, :], sx1_t[i % 3][:, :], INV_D)
            e.tensor_scalar_mul(e2_t[i % 3][:, :], sx1q_t[i % 3][:, :], INV_D)
            e.tensor_mul(var_t[i % 3][:, :], mean_t[i % 3][:, :],
                         mean_t[i % 3][:, :])
            mk(e.tensor_sub(var_t[i % 3][:, :], e2_t[i % 3][:, :],
                            var_t[i % 3][:, :]), m_v, 1, expect=V_P3(i, 3))
            e.wait_ge(m_s, S_P3(i, 3))
            e.tensor_scalar(c1_t[i % 3][:, :], mean_t[i % 3][:, :],
                            s1_t[i % 3][:, :], -1.0, op0=ALU.mult, op1=ALU.mult)
            e.tensor_scalar(sp_t[i % 3][:, :], s1_t[i % 3][:, :], 1.0, None,
                            op0=ALU.add)
            mk(e.tensor_mul(c1s_t[i % 3][:, :], c1_t[i % 3][:, :],
                            sumwb[:, :]), m_v, 1, expect=V_P3(i, 4))
            e.wait_ge(m_g, G_SC(i))
            e.tensor_scalar(scw[i % 2][:, :], scw[i % 2][:, :], 1.0, None,
                            op0=ALU.mult, op1=ALU.add,
                            accum_out=sxw_t[i % 3][:, :])
            mk(e.tensor_scalar(sc_all[:, i:i + 1], sxw_t[i % 3][:, :],
                               s1_t[i % 3][:, :], c1s_t[i % 3][:, :],
                               op0=ALU.mult, op1=ALU.add), m_v, 1,
               expect=V_P3(i, 5))
        e.wait_ge(m_t, T_SCT)
        mk(e.tensor_copy(scT[:, :], pb6[0:32, 0:128]), m_v, 1, expect=V_SCT)
        if not ST3:
            return
        e.wait_ge(m_stc, STC_SCF)
        mk(e.max(out=candv[:, 0:8], in_=scfull[:, :]), m_v, 1, expect=V_TK(1))
        e.wait_ge(m_v, V_TK(1))
        mk(e.max_index(out=candi[:, 0:8], in_max=candv[:, 0:8],
                       in_values=scfull[:, :]), m_v, 1, expect=V_TK(2))
        e.wait_ge(m_v, V_TK(2))
        mk(e.match_replace(out=ws_sb[:, :], in_to_replace=candv[:, 0:8],
                           in_values=scfull[:, :], imm_value=NEG), m_v, 1,
           expect=V_TK(3))
        e.wait_ge(m_v, V_TK(3))
        mk(e.max(out=candv[:, 8:16], in_=ws_sb[:, :]), m_v, 1, expect=V_TK(4))
        e.wait_ge(m_v, V_TK(4))
        mk(e.max_index(out=candi[:, 8:16], in_max=candv[:, 8:16],
                       in_values=ws_sb[:, :]), m_v, 1, expect=V_TK(5))
        e.wait_ge(m_v, V_TK(5))
        e.tensor_copy(candif[:, :], candi[:, :])
        e.tensor_scalar(candif[:, :], candif[:, :], iota64_sb[:, :], None,
                        op0=ALU.add)
        mk(e.tensor_copy(candtu[:, :], candif[:, :]), m_v, 1, expect=V_TK(6))
        e.wait_ge(m_stc, STC_TOK)
        for j in range(MYCH):
            if j >= 2:
                e.wait_ge(m_gi, GI_RS(j - 2))
            e.tensor_scalar(cmp_scr[:, :], candrep[:, :], tchall[:, j:j + 1],
                            None, op0=ALU.is_gt, op1=ALU.add,
                            accum_out=rank_f[j % 2][:, :])
            mk(e.tensor_copy(rank_u[j % 2][:, :], rank_f[j % 2][:, :]), m_v,
               1, expect=V_RK(j))
        if not ST4:
            return
        for b in range(MYPB):
            e.wait_ge(m_s, S_RSQ(b))
            e.wait_ge(m_w12, W12_FULL)
            mk(e.scalar_tensor_tensor(h_sb[b][:, :], xsel[b][:, :],
                                      ssq_t[b][:, :], g1b[:, :],
                                      op0=ALU.mult, op1=ALU.mult), m_v, 1,
               expect=V_XS(b))
        for b in range(MYPB):
            e.wait_ge(m_t, T_FH(b))
            mk(e.tensor_copy(hfm[:, b * D:(b + 1) * D], pT[:, :]), m_v, 1,
               expect=V_FH(b))
        for b in range(MYPB):
            for j in range(6):
                e.wait_ge(m_t, T_QKV(b, j))
                mk(e.tensor_copy(qkv_sb[b][:, j * 512:(j + 1) * 512],
                                 pb[j % 2][:, :]), m_v, 1, expect=V_QKV(b, j))
        kfm4 = kfm.rearrange("p (hp kb c) -> p hp kb c", hp=8, kb=NPB)
        pT3 = pT.rearrange("p (hp c) -> p hp c", hp=8)
        for kb in range(NPB):
            e.wait_ge(m_t, T_KT(kb))
            mk(e.tensor_copy(kfm4[:, :, kb, :], pT3[:, :, :]), m_v, 1,
               expect=V_KT(kb))
        if not ST5:
            return
        qfm4 = qfm.rearrange("p (hp bx c) -> p hp bx c", hp=8, bx=MYPB)
        for b in range(MYPB):
            e.wait_ge(m_t, T_QT(b))
            mk(e.tensor_copy(qfm4[:, :, b, :], pT3[:, :, :]), m_v, 1,
               expect=V_QT(b))
        def att_evac(w):
            h2, b2 = divmod(w, 2)
            e.wait_ge(m_t, T_SUM(h2, b2))
            mk(e.tensor_copy(sums_sb[0:1, w * 128:(w + 1) * 128],
                             pb[w % 2][0:1, 0:128]),
               m_v, 1, expect=V_SUMC(h2, b2))
            e.wait_ge(m_t, T_O(h2, b2))
            mk(e.tensor_copy(o_sb[b2][:, h2 * 64:(h2 + 1) * 64],
                             pb[2 + w % 2][:, 0:64]),
               m_v, 1, expect=V_OE(h2, b2))

        for h in range(NH):
            for b in range(MYPB):
                w = 2 * h + b
                e.wait_ge(m_t, T_ATT(h, b))
                if w >= 3:
                    e.wait_ge(m_s, S_EXP((w - 3) // 2, (w - 3) % 2))
                wd = 384 if b == 0 else 512
                mk(e.tensor_add(attm[w % 3][:, 0:wd],
                                [pb[4], pb[5], pb6][w % 3][:, 0:wd],
                                maskf_sb[:, b * 512:b * 512 + wd]), m_v, 1,
                   expect=V_MSK(h, b))
                if w >= 2:
                    att_evac(w - 2)
        for w in range(2 * NH - 2, 2 * NH):
            att_evac(w)
        e.wait_ge(m_stc, STC_RQA)
        e.reciprocal(recqall[:, :], recqall[:, :])
        for b in range(MYPB):
            for h in range(NH):
                w = 2 * h + b
                ins = e.tensor_scalar(o_sb[b][:, h * 64:(h + 1) * 64],
                                      o_sb[b][:, h * 64:(h + 1) * 64],
                                      recqall[:, w:w + 1], None, op0=ALU.mult)
                if h == NH - 1:
                    mk(ins, m_v, 1, expect=V_ON(b))
        for b in range(MYPB):
            e.wait_ge(m_t, T_OT(b))
            mk(e.tensor_copy(ofm[:, b * D:(b + 1) * D], pT[:, :]), m_v, 1,
               expect=V_OT(b))
        for b in range(MYPB):
            e.wait_ge(m_t, T_XA(b))
            e.tensor_copy(xattn[b][:, 0:512], pb[0][:, :])
            mk(e.tensor_copy(xattn[b][:, 512:1024], pb[1][:, :]), m_v, 1,
               expect=V_XAE(b))
        for b in range(MYPB):
            e.wait_ge(m_s, S_XS(b))
            mk(e.tensor_add(s2_t[b][:, :], xattn[b][:, :], xself[b][:, :]),
               m_v, 1, expect=V_S2(b))
        if not ST6:
            return
        for b in range(MYPB):
            e.wait_ge(m_s, S_H2(b, 2))
            e.wait_ge(m_w12, W12_FULL)
            mk(e.scalar_tensor_tensor(h2_sb[b][:, :], s2_t[b][:, :],
                                      s2q_t[b][:, :], g2b[:, :],
                                      op0=ALU.mult, op1=ALU.mult), m_v, 1,
               expect=V_H2(b))
        for b in range(MYPB):
            e.wait_ge(m_t, T_H2T(b))
            mk(e.tensor_copy(h2fm[:, b * D:(b + 1) * D], pT[:, :]), m_v, 1,
               expect=V_H2T(b))
        for ft in range(NFT):
            for b in range(MYPB):
                fs = FT_SIZES[ft]
                e.wait_ge(m_s, S_SIL(ft, b))
                mk(e.tensor_mul(g_sb[:, 0:fs], asil[:, 0:fs],
                                pb[1][:, 0:fs]), m_v, 1, expect=V_SW(ft, b, 1))
                e.wait_ge(m_t, T_SW(ft, b, 3))
                ngc = (fs + 127) // 128
                mk(e.tensor_copy(gfm[:, 0:ngc * 128], pT[:, 0:ngc * 128]),
                   m_v, 1, expect=V_SW(ft, b, 2))
        for b in range(MYPB):
            e.wait_ge(m_t, T_SW(NFT - 1, b, 4))
            e.tensor_add(delta_my[b][:, 0:512], xattn[b][:, 0:512],
                         pb[2 + 2 * b][:, :])
            mk(e.tensor_add(delta_my[b][:, 512:1024], xattn[b][:, 512:1024],
                            pb[3 + 2 * b][:, :]), m_v, 1, expect=V_XFF(b))
        for b in range(NPB):
            e.wait_ge(m_stc, STC_IDXA(b))
            e.tensor_copy(idxf[b][:, :], idxp[b][:, :])
            e.tensor_scalar(idxf[b][:, :], idxf[b][:, :], cinfob[:, 0:1],
                            None, op0=ALU.subtract)
            e.tensor_scalar(idxm[b][:, :], idxf[b][:, :], 0.0, None,
                            op0=ALU.is_ge)
            e.tensor_scalar(idxm2[b][:, :], idxf[b][:, :], float(S), None,
                            op0=ALU.is_lt)
            e.tensor_mul(idxm[b][:, :], idxm[b][:, :], idxm2[b][:, :])
            e.tensor_mul(idxf[b][:, :], idxf[b][:, :], idxm[b][:, :])
            e.tensor_scalar(idxm[b][:, :], idxm[b][:, :], -100000.0, 100000.0,
                            op0=ALU.mult, op1=ALU.add)
            e.tensor_add(idxf[b][:, :], idxf[b][:, :], idxm[b][:, :])
            mk(e.tensor_copy(idxu[b][:, :], idxf[b][:, :]), m_v, 1,
               expect=V_FIL(b))

    @blk.tensor
    def _(e):
        e.wait_ge(m_hw, 4 * W16)
        tl = [None]

        def M(*a, **kw):
            tl[0] = e.matmul(*a, **kw)
            return tl[0]

        def T(*a, **kw):
            tl[0] = e.transpose(*a, **kw)
            return tl[0]

        for ct in range(NCT):
            e.wait_ge(m_gx, 16 * (ct + 1))
            if ct >= 1:
                e.wait_ge(m_v, V_P0(ct - 1))
            for c in range(DC):
                M(pb[4][0:1, :], ones_sb[:, :],
                         xfm[:, c * S + 512 * ct:c * S + 512 * (ct + 1)],
                         start=(c == 0), stop=(c == DC - 1))
            mk(tl[0], m_t, 1)
            e.wait_ge(m_s, S_P0(ct))
            for c in range(DC):
                M(pb[5][0:1, :], ones_sb[:, :],
                         sqs[:, c * 512:(c + 1) * 512],
                         start=(c == 0), stop=(c == DC - 1))
            mk(tl[0], m_t, 1, expect=T_P0(ct))
        e.wait_ge(m_wq, W_FULL)
        e.wait_ge(m_v, V_P0(NCT - 1))
        for half in range(2):
            for c in range(DC):
                M([pb[4], pb[5]][half][0:1, :], ones_sb[:, :],
                         wA[:, c * D + half * 512:c * D + (half + 1) * 512],
                         start=(c == 0), stop=(c == DC - 1))
        mk(tl[0], m_t, 1, expect=T_CSQ)
        e.wait_ge(m_wk, W_FULL)
        for half in range(2):
            for c in range(DC):
                M([pb[0], pb[1]][half][0:1, :], ones_sb[:, :],
                         wB[:, c * D + half * 512:c * D + (half + 1) * 512],
                         start=(c == 0), stop=(c == DC - 1))
        mk(tl[0], m_t, 1, expect=T_CSK)
        if not ST1:
            return
        e.wait_ge(m_v, V_CSQ)

        def gq_mm(i, pss, end):
            e.wait_ge(m_v, V_P1(i, 4) if not end else V_P2(i, 2))
            M(pss[0][0:1, :], recqt[i % 3][:, :], qa[i % 3][:, 0:512],
                     start=(i == 0), stop=(i == NBLK - 1))
            M(pss[1][0:1, :], recqt[i % 3][:, :], qa[i % 3][:, 512:1024],
                     start=(i == 0), stop=(i == NBLK - 1))

        for i in range(NBLK):
            if i >= 2:
                e.wait_ge(m_s, S_P1(i - 2, 2))
            for half in range(2):
                for c in range(DC):
                    M(pb[2 * (i % 2) + half][:, :],
                             xfm[:, c * S + 128 * i:c * S + 128 * (i + 1)],
                             wchunk(wA, c)[:, 512 * half:512 * (half + 1)],
                             start=(c == 0), stop=(c == DC - 1))
            mk(tl[0], m_t, 1, expect=T_P1Q(i))
            if i >= 3:
                gq_mm(i - 3, (pb[4], pb[5]), False)
                mk(tl[0], m_t, 1, expect=T_P1G(i - 3))
        for j in range(NBLK - 3, NBLK):
            gq_mm(j, (pb[4], pb[5]), False)
            mk(tl[0], m_t, 1, expect=T_P1G(j))
        if not ST2:
            return
        e.wait_ge(m_v, V_WKP)
        e.wait_ge(m_v, V_GQ)
        for i in range(NBLK):
            if i >= 2:
                e.wait_ge(m_s, S_P2(i - 2, 1))
            else:
                e.wait_ge(m_s, S_P1(NBLK - 2 + i, 2))
            for half in range(2):
                for c in range(DC):
                    M(pb[2 * (i % 2) + half][:, :],
                             xfm[:, c * S + 128 * i:c * S + 128 * (i + 1)],
                             wchunk(wB, c)[:, 512 * half:512 * (half + 1)],
                             start=(c == 0), stop=(c == DC - 1))
            mk(tl[0], m_t, 1, expect=T_P2Q(i))
            if i >= 3:
                gq_mm(i - 3, (pb[4], pb[5]), True)
                mk(tl[0], m_t, 1, expect=T_P2G(i - 3))
        for j in range(NBLK - 3, NBLK):
            gq_mm(j, (pb[4], pb[5]), True)
            mk(tl[0], m_t, 1, expect=T_P2G(j))
        e.wait_ge(m_wv, W_FULL)
        e.wait_ge(m_v, V_WRP)
        for u in range(16):
            m, jh = divmod(u, 2)
            if u >= 4:
                e.wait_ge(m_v, V_WVR(u - 4))
            else:
                e.wait_ge(m_s, S_P2(NBLK - 4 + u, 1))
            for c in range(DC):
                M(pb[u % 4][:, :],
                         wchunk(wVT, c)[:, m * 128:(m + 1) * 128],
                         wchunk(wR, c)[:, jh * 512:(jh + 1) * 512],
                         start=(c == 0), stop=(c == DC - 1))
            mk(tl[0], m_t, 1, expect=T_WVR(u))
        e.wait_ge(m_v, V_GK)
        e.wait_ge(m_v, V_WVR(15))
        for half in range(2):
            for c in range(DC):
                M([pb[4], pb[5]][half][0:1, :], ones_sb[:, :],
                         wA[:, c * D + half * 512:c * D + (half + 1) * 512],
                         start=(c == 0), stop=(c == DC - 1))
        mk(tl[0], m_t, 1, expect=T_CSV)
        e.wait_ge(m_v, V_CSV)
        for i in range(NBLK):
            if i >= 3:
                e.wait_ge(m_s, S_P3(i - 3, 1))
            elif i == 0:
                e.wait_ge(m_v, V_WVR(13))
            elif i == 1:
                e.wait_ge(m_v, V_WVR(15))
            for half in range(2):
                for c in range(DC):
                    M(pb[2 * (i % 3) + half][:, :],
                             xfm[:, c * S + 128 * i:c * S + 128 * (i + 1)],
                             wchunk(wA, c)[:, 512 * half:512 * (half + 1)],
                             start=(c == 0), stop=(c == DC - 1))
            mk(tl[0], m_t, 1, expect=T_P3Q(i))
        e.wait_ge(m_v, V_P3(NBLK - 1, 5))
        mk(T(pb6[0:32, 0:128], sc_all[:, :], id32_sb[:, :]), m_t, 1,
           expect=T_SCT)
        if not ST4:
            return
        for b in range(MYPB):
            e.wait_ge(m_v, V_XS(b))
            if b >= 1:
                e.wait_ge(m_v, V_FH(b - 1))
            for c in range(DC):
                T(pT[:, c * 128:(c + 1) * 128],
                            h_sb[b][:, c * 128:(c + 1) * 128], id_sb[:, :])
            mk(tl[0], m_t, 1, expect=T_FH(b))
        for b in range(MYPB):
            for j in range(6):
                e.wait_ge(m_wqkv, 128 * (j // 2 + 1))
                wsrc = [wqkvq, wqkvk, wqkvv][j // 2]
                jj = j % 2
                if 6 * b + j >= 2:
                    pi = 6 * b + j - 2
                    e.wait_ge(m_v, V_QKV(pi // 6, pi % 6))
                for c in range(DC):
                    M(pb[j % 2][:, :],
                             hfm[:, b * D + c * 128:b * D + (c + 1) * 128],
                             wchunk(wsrc, c)[:, jj * 512:(jj + 1) * 512],
                             start=(c == 0), stop=(c == DC - 1))
                mk(tl[0], m_t, 1, expect=T_QKV(b, j))
        for kb in range(NPB):
            e.wait_ge(m_stc, STC_KV(kb))
            if kb >= 1:
                e.wait_ge(m_v, V_KT(kb - 1))
            else:
                e.wait_ge(m_v, V_FH(MYPB - 1))
            for hp in range(8):
                T(pT[:, hp * 128:(hp + 1) * 128],
                            kvall[kb][:, hp * 128:(hp + 1) * 128], id_sb[:, :])
            mk(tl[0], m_t, 1, expect=T_KT(kb))
        if not ST5:
            return
        for b in range(MYPB):
            e.wait_ge(m_v, V_QKV(b, 1))
            if b >= 1:
                e.wait_ge(m_v, V_QT(b - 1))
            else:
                e.wait_ge(m_v, V_KT(NPB - 1))
            for hp in range(8):
                T(pT[:, hp * 128:(hp + 1) * 128],
                            qkv_sb[b][:, hp * 128:(hp + 1) * 128], id_sb[:, :])
            mk(tl[0], m_t, 1, expect=T_QT(b))
        abank = [pb[4], pb[5], pb6]

        def sum_o(w):
            h2, b2 = divmod(w, 2)
            nkb = NPB - 1 if b2 == 0 else NPB
            e.wait_ge(m_s, S_EXP(h2, b2))
            if w >= 2:
                e.wait_ge(m_v, V_SUMC((w - 2) // 2, (w - 2) % 2))
            for kb in range(nkb):
                M(pb[w % 2][0:1, 0:128],
                         ones_sb[:, :],
                         probsT[w % 3][:, kb * 128:(kb + 1) * 128],
                         start=(kb == 0), stop=(kb == nkb - 1))
            mk(tl[0], m_t, 1, expect=T_SUM(h2, b2))
            if w >= 2:
                e.wait_ge(m_v, V_OE((w - 2) // 2, (w - 2) % 2))
            for kb in range(nkb):
                M(pb[2 + w % 2][:, 0:64],
                         probsT[w % 3][:, kb * 128:(kb + 1) * 128],
                         kvall[kb][:, D + h2 * 64:D + (h2 + 1) * 64],
                         start=(kb == 0), stop=(kb == nkb - 1))
            mk(tl[0], m_t, 1, expect=T_O(h2, b2))

        for h in range(NH):
            hp, hh = divmod(h, 2)
            for b in range(MYPB):
                w = 2 * h + b
                if w >= 3:
                    e.wait_ge(m_v, V_MSK((w - 3) // 2, (w - 3) % 2))
                if w == 0:
                    e.wait_ge(m_v, V_QT(MYPB - 1))
                for kb in range(NPB - 1 if b == 0 else NPB):
                    M(abank[w % 3][:, kb * 128:(kb + 1) * 128],
                             kfm4_slice(kfm, hp, kb, hh),
                             qfm4_slice(qfm, hp, b, hh),
                             start=True, stop=True)
                mk(tl[0], m_t, 1, expect=T_ATT(h, b))
                if w >= 2:
                    sum_o(w - 2)
        for w in range(2 * NH - 2, 2 * NH):
            sum_o(w)
        for b in range(MYPB):
            e.wait_ge(m_v, V_ON(b))
            if b >= 1:
                e.wait_ge(m_v, V_OT(b - 1))
            else:
                e.wait_ge(m_v, V_QT(MYPB - 1))
            for hp in range(8):
                T(pT[:, hp * 128:(hp + 1) * 128],
                            o_sb[b][:, hp * 128:(hp + 1) * 128], id_sb[:, :])
            mk(tl[0], m_t, 1, expect=T_OT(b))
        e.wait_ge(m_wout, W_FULL)
        for b in range(MYPB):
            e.wait_ge(m_v, V_OT(b))
            if b >= 1:
                e.wait_ge(m_v, V_XAE(b - 1))
            for half in range(2):
                for c in range(DC):
                    M(pb[half][:, :],
                             ofm[:, b * D + c * 128:b * D + (c + 1) * 128],
                             wchunk(wout_sb, c)[:, half * 512:(half + 1) * 512],
                             start=(c == 0), stop=(c == DC - 1))
            mk(tl[0], m_t, 1, expect=T_XA(b))
        if not ST6:
            return
        for b in range(MYPB):
            e.wait_ge(m_v, V_H2(b))
            if b >= 1:
                e.wait_ge(m_v, V_H2T(b - 1))
            else:
                e.wait_ge(m_v, V_OT(MYPB - 1))
            for c in range(DC):
                T(pT[:, c * 128:(c + 1) * 128],
                            h2_sb[b][:, c * 128:(c + 1) * 128], id_sb[:, :])
            mk(tl[0], m_t, 1, expect=T_H2T(b))
        for ft in range(NFT):
            for b in range(MYPB):
                fs = FT_SIZES[ft]
                u = ft * MYPB + b
                jp = ft % 2
                e.wait_ge(m_wf[ft % 2], WF_AT(ft))
                if u >= 2:
                    pi = u - 2
                    e.wait_ge(m_s, S_SIL(pi // MYPB, pi % MYPB))
                elif u == 0:
                    e.wait_ge(m_v, V_XAE(MYPB - 1))
                    e.wait_ge(m_v, V_MSK(NH - 1, MYPB - 1))
                for c in range(DC):
                    M([pb[0], pb6][u % 2][:, 0:fs],
                             h2fm[:, b * D + c * 128:b * D + (c + 1) * 128],
                             wchunk(wf1[jp], c, 512)[:, 0:fs],
                             start=(c == 0), stop=(c == DC - 1))
                mk(tl[0], m_t, 1, expect=T_SW(ft, b, 1))
                if u >= 1:
                    pi = u - 1
                    e.wait_ge(m_v, V_SW(pi // MYPB, pi % MYPB, 1))
                for c in range(DC):
                    M(pb[1][:, 0:fs],
                             h2fm[:, b * D + c * 128:b * D + (c + 1) * 128],
                             wchunk(wf2[jp], c, 512)[:, 0:fs],
                             start=(c == 0), stop=(c == DC - 1))
                mk(tl[0], m_t, 1, expect=T_SW(ft, b, 2))
                ngc = (fs + 127) // 128
                e.wait_ge(m_v, V_SW(ft, b, 1))
                if u >= 1:
                    pi = u - 1
                    e.wait_ge(m_v, V_SW(pi // MYPB, pi % MYPB, 2))
                else:
                    e.wait_ge(m_v, V_H2T(MYPB - 1))
                for gc in range(ngc):
                    gw = min(128, fs - gc * 128)
                    T(pT[0:gw, gc * 128:gc * 128 + 128],
                                g_sb[:, gc * 128:gc * 128 + gw], id_sb[:, :])
                mk(tl[0], m_t, 1, expect=T_SW(ft, b, 3))
                e.wait_ge(m_v, V_SW(ft, b, 2))
                for gc in range(ngc):
                    gw = min(128, fs - gc * 128)
                    for half in range(2):
                        M(pb[2 + 2 * b + half][:, :],
                                 gfm[0:gw, gc * 128:gc * 128 + 128],
                                 wf3[jp][0:gw, gc * D + half * 512:gc * D + (half + 1) * 512],
                                 start=(ft == 0 and gc == 0),
                                 stop=(ft == NFT - 1 and gc == ngc - 1))
                mk(tl[0], m_t, 1, expect=T_SW(ft, b, 4))

    @blk.gpsimd
    def _(e):
        e = ChainProxy(e, m_gx2)
        xv = x_fm.rearrange("(c p) s -> p c s", p=128)
        for ct in range(NCT):
            mk(e.dma_start(out=xfm.rearrange("p (c s) -> p c s", c=DC)[:, :, 512 * ct:512 * (ct + 1)],
                           in_=xv[:, :, 512 * ct:512 * (ct + 1)]),
               m_gx, W16)
            e.wait_ge(m_gx, 16 * (ct + 1))

        def wload(dst, src, semh):
            sv = src.rearrange("(c p) n -> p c n", p=128)
            for c in range(DC):
                mk(e.dma_start(out=wchunk(dst, c, sv.shape[2]),
                               in_=sv[:, c, :]), semh, W16)

        wload(wA, Wq_d, m_wq)
        e.wait_ge(m_hw, HW_CONST)
        mk(e.collective_compute("AllReduce", ALU.add, replica_groups=rg,
                                ins=[cc_w_i[:, :]], outs=[cc_w_o[:, :]]),
           m_cc, 1, expect=CC_WARM)
        wload(wB, Wk_d, m_wk)
        wload(wVT, WvT_d, m_wv)
        wload(wR, Wr_d, m_wr)
        mk(e.dma_start(out=g1b[:, :], in_=g1_d[0:1, :].to_broadcast([128, D])),
           m_w12, W16)
        mk(e.dma_start(out=g2b[:, :], in_=g2_d[0:1, :].to_broadcast([128, D])),
           m_w12, W16, expect=W12_FULL)
        if not ST1:
            return
        for i in range(NBLK):
            e.wait_ge(m_v, V_P1(i, 3))
            mk(e.dma_start(out=q_dram[128 * i:128 * (i + 1), :],
                           in_=qfix[i % 2][:, :]), m_gst2[i % 2], W16,
               expect=GSTW(i))
            if i < 16:
                mk(e.dma_start(out=wqkvT_bf[64 * i:64 * (i + 1), :],
                               in_=WqkvT_d[64 * i:64 * (i + 1), :]),
                   m_pc, W16)
            elif i < 24:
                j = i - 16
                if j == 0:
                    e.wait_ge(m_pc, 16 * 16)
                mk(e.dma_start(out=woutT_bf[128 * j:128 * (j + 1), :],
                               in_=WoutT_d[128 * j:128 * (j + 1), :]),
                   m_pc, W16)
        if not ST2 and NO_AR1:
            return
        e.wait_ge(m_sd, SD_GQ)
        mk(e.collective_compute("AllReduce", ALU.add, replica_groups=rg,
                                ins=[cc_gq_i[:, :]], outs=[cc_gq_o[:, :]]),
           m_cc, 1, expect=CC_GQ)
        if not ST2:
            return
        e.wait_ge(m_cc, CC_GQ)
        mk(e.dma_start(out=gqb[:, :],
                       in_=cc_gq_o[0:1, :].to_broadcast([128, D])), m_gm, W16,
           expect=GM_GQB)
        e.wait_ge(m_gm, GM_GQB)
        mk(e.dma_start(out=gqrow[:, :], in_=cc_gq_o[:, :]), m_gm, W16,
           expect=GM_GQR)
        e.wait_ge(m_sd, SD_GK)
        mk(e.collective_compute("AllReduce", ALU.add, replica_groups=rg,
                                ins=[cc_gk_i[:, :]], outs=[cc_gk_o[:, :]]),
           m_cc, 1, expect=CC_GK)
        for i in range(NBLK):
            mk(e.dma_start(out=qf32[i % 2][:, :],
                           in_=q_dram[128 * i:128 * (i + 1), :]), m_gq3, W16,
               expect=GQ3(i))
            e.wait_ge(m_xt[i % 2], XTW(i))
            e.wait_ge(m_gq3, GQ3(i))
            mk(e.tensor_tensor(xtm_t[i % 3][:, :], xtm_t[i % 3][:, :],
                               qf32[i % 2][:, :], ALU.add), m_g, 1,
               expect=G_P3(i))
            e.wait_ge(m_v, V_P3(i, 2))
            mk(e.tensor_tensor(scw[i % 2][:, :], x1_t[i % 3][:, :],
                               wrb[:, :], ALU.mult), m_g, 1, expect=G_SC(i))
        if not ST3:
            return
        e.wait_ge(m_v, V_MZ)
        par_g = e.value_load(cinfu_sb[0:1, 3:4])
        ordz = ordidx_sh.rearrange("(a r) b -> a (r b)", a=2)[bass.ts(par_g, 1), :]
        mk(e.dma_start(out=ordz, in_=zidx[0:2, :]), m_gi, W16, expect=GI_OZ)
        e.wait_ge(m_gi, GI_OZ)
        e.wait_ge(m_sd, SD_SC)
        e.wait_ge(m_lncp, 64)
        e.wait_ge(m_hb2[0], HB_FULL)
        e.wait_ge(m_hb2[1], HB_FULL)
        mk(e.collective_compute("AllReduce", ALU.add, replica_groups=rg,
                                ins=[cc_sc_i[:, :]], outs=[cc_sc_o[:, :]]),
           m_cc, 1, expect=CC_SC)
        e.wait_ge(m_cc, CC_SC)
        for j in range(MYCH):
            e.wait_ge(m_v, V_RK(j))
            mk(e.indirect_dma_start(
                out=ordidx_sh[:, :],
                out_offset=IndirectOffsetOnAxis(rank_u[j % 2][:, :], 0),
                in_=tokall[:, j:j + 1], in_offset=None,
                bounds_check=CAP - 1, oob_is_err=False), m_gi, W16,
               expect=GI_RS(j))
            e.wait_ge(m_gi, GI_RS(j))
        mk(e.collective_compute("AllReduce", ALU.add, replica_groups=rg,
                                ins=[cc_b_i[:, :]], outs=[cc_b_o[:, :]]),
           m_cc, 1, expect=CC_ORD)
        if not ST4:
            return
        e.wait_ge(m_cc, CC_ORD)
        for b in range(MYPB):
            e.wait_ge(m_stc, STC_IDXP(b))
            mk(e.indirect_dma_start(
                out=xsel[b][:, :], out_offset=None,
                in_=ln1_sh[:, :],
                in_offset=IndirectOffsetOnAxis(idxp[b][:, :], 0),
                bounds_check=2 * S - 1, oob_is_err=False), m_gi, W16,
               expect=GI_GA(b))
            e.wait_ge(m_gi, GI_GA(b))
        e.wait_ge(m_sd, SD_KV_END)
        mk(e.collective_compute("AllReduce", ALU.add, replica_groups=rg,
                                ins=[cc_b_i[:, :]], outs=[cc_b_o[:, :]]),
           m_cc, 1, expect=CC_KV)
        if not ST6:
            return
        w1v = W1T_d.rearrange("(c p) n -> p c n", p=128)
        w2v = W2T_d.rearrange("(c p) n -> p c n", p=128)
        for ft in range(NFT):
            fs = FT_SIZES[ft]
            fo = FT_OFF[ft]
            jp = ft % 2
            if ft >= 2:
                e.wait_ge(m_t, T_SW(ft - 2, MYPB - 1, 4))
            else:
                e.wait_ge(m_t, T_XA(MYPB - 1))
            for c in range(DC):
                mk(e.dma_start(out=wchunk(wf1[jp], c, 512)[:, 0:fs],
                               in_=w1v[:, c, fo:fo + fs]), m_wf[jp], W16)
            for c in range(DC):
                mk(e.dma_start(out=wchunk(wf2[jp], c, 512)[:, 0:fs],
                               in_=w2v[:, c, fo:fo + fs]), m_wf[jp], W16)
            ngc = (fs + 127) // 128
            for gc in range(ngc):
                gw_ = min(128, fs - gc * 128)
                mk(e.dma_start(out=wf3[jp][0:gw_, gc * D:(gc + 1) * D],
                               in_=W3T_d[fo + gc * 128:fo + gc * 128 + gw_, :]),
                   m_wf[jp], W16)
            assert cnt[m_wf[jp].name] == WF_AT(ft), (cnt[m_wf[jp].name], WF_AT(ft))
        e.wait_ge(m_sd, SD_DEL_END)
        mk(e.collective_compute("AllReduce", ALU.add, replica_groups=rg,
                                ins=[cc_b_i[:, :]], outs=[cc_b_o[:, :]]),
           m_cc, 1, expect=CC_DEL)
        for b in range(NPB):
            e.wait_ge(m_v, V_FIL(b))
            e.wait_ge(m_stc, STC_DEL(b))
            e.wait_ge(m_hb2[0], HB_FULL)
            e.wait_ge(m_hb2[1], HB_FULL)
            mk(e.indirect_dma_start(
                out=out[:, :],
                out_offset=IndirectOffsetOnAxis(idxu[b][:, :], 0),
                in_=delta_ld[b][:, :], in_offset=None,
                bounds_check=S - 1, oob_is_err=False,
                compute_op=ALU.add), m_gi, W16, expect=GI_FS(b))

    esF.close()
    es0.close()
    return nc


def kfm4_slice(kfm, hp, kb, hh):
    c0 = (hp * NPB + kb) * 128
    return kfm[hh * 64:(hh + 1) * 64, c0:c0 + 128]


def qfm4_slice(qfm, hp, b, hh):
    c0 = (hp * MYPB + b) * 128
    return qfm[hh * 64:(hh + 1) * 64, c0:c0 + 128]


_NC_CACHE = {}
_LAST_RES = None


def kernel(L0_attn_x, L0_attn_Wq, L0_attn_Wk, L0_attn_Wv, L0_attn_Wr,
           L0_ffn_w_router, L0_ffn_W_qkv, L0_ffn_W_out, L0_ffn_g1, L0_ffn_g2,
           L0_ffn_W1, L0_ffn_W2, L0_ffn_W3, stage=99):
    x = np.asarray(L0_attn_x)
    B = x.shape[0]
    n = 8

    f32 = np.float32
    cA = np.ascontiguousarray
    Wq = cA(L0_attn_Wq, dtype=f32)
    Wk = cA(L0_attn_Wk, dtype=f32)
    WvT = cA(np.asarray(L0_attn_Wv).T, dtype=f32)
    Wr = cA(L0_attn_Wr, dtype=f32)
    WqkvT = cA(np.asarray(L0_ffn_W_qkv).T, dtype=f32)
    WoutT = cA(np.asarray(L0_ffn_W_out).T, dtype=f32)
    W1T = cA(np.asarray(L0_ffn_W1).T, dtype=f32)
    W2T = cA(np.asarray(L0_ffn_W2).T, dtype=f32)
    W3T = cA(np.asarray(L0_ffn_W3).T, dtype=f32)
    wr_row = cA(np.asarray(L0_ffn_w_router).reshape(1, D), dtype=f32)
    g1 = cA(np.asarray(L0_ffn_g1).reshape(1, D), dtype=f32)
    g2 = cA(np.asarray(L0_ffn_g2).reshape(1, D), dtype=f32)

    ident_bf = np.eye(128, dtype=ml_dtypes.bfloat16)
    ident_f32 = np.eye(128, dtype=f32)
    ones_bf = np.ones((128, 1), dtype=ml_dtypes.bfloat16)
    iota64 = (np.arange(128, dtype=f32) * 64.0).reshape(128, 1)

    key = (stage, DBG)
    if key not in _NC_CACHE:
        _NC_CACHE[key] = build_nc(stage)
    nc = _NC_CACHE[key]

    in_maps = []
    for core in range(n):
        batch = core // 2
        parity = core % 2
        lo = parity * S
        xs = cA(x[batch, lo:lo + S, :], dtype=f32)
        xfm_s = cA(xs.T, dtype=f32)
        mask = np.empty((128, MYPB * NPB * 128), dtype=ml_dtypes.bfloat16)
        for b in range(MYPB):
            qb_g = parity * MYPB + b
            for kb in range(NPB):
                kk = kb * 128 + np.arange(128)[:, None]
                qq = qb_g * 128 + np.arange(128)[None, :]
                blkm = np.where(kk <= qq, 0.0, NEG).astype(f32)
                mask[:, b * NPB * 128 + kb * 128:(b) * NPB * 128 + (kb + 1) * 128] = blkm
        cinfo_u = np.zeros((1, 8), np.uint32)
        cinfo_u[0, 0] = parity * NBLK
        cinfo_u[0, 1] = parity * MYPB
        cinfo_u[0, 2] = parity * MYCH
        cinfo_u[0, 3] = parity
        cinfo_f = np.zeros((1, 8), f32)
        cinfo_f[0, 0] = float(lo)
        in_maps.append({
            "x_tm": xs, "x_fm": xfm_s,
            "Wq": Wq, "Wk": Wk, "WvT": WvT, "Wr": Wr,
            "WqkvT": WqkvT, "WoutT": WoutT, "W1T": W1T, "W2T": W2T,
            "W3T": W3T, "w_router": wr_row, "g1": g1, "g2": g2,
            "ident_bf": ident_bf, "ident_f32": ident_f32, "ones_bf": ones_bf,
            "iota64": iota64, "maskffn": mask,
            "cinfo_u": cinfo_u, "cinfo_f": cinfo_f,
        })

    res = run_bass_kernel_spmd(nc, in_maps, core_ids=list(range(n)))
    global _LAST_RES
    _LAST_RES = res
    outp = np.empty((B, 2 * S, D), np.float32)
    for core in range(n):
        batch = core // 2
        parity = core % 2
        lo = parity * S
        outp[batch, lo:lo + S, :] = res.results[core]["out"]
    return outp
